# version 07
# baseline (speedup 1.0000x reference)
import math
import contextlib
import numpy as np
import ml_dtypes
import concourse.bass as bass
import concourse.mybir as mybir
from concourse.bass_utils import run_bass_kernel_spmd

F32 = mybir.dt.float32
BF16 = mybir.dt.bfloat16
I32 = mybir.dt.int32
U32 = mybir.dt.uint32
AF = mybir.ActivationFunctionType
ALU = mybir.AluOpType
BF = ml_dtypes.bfloat16

B, S, D = 2, 8192, 1024
H, DH = 16, 64
HID = 3072
NTOK = 4096
NOWN = 2048
NISL = 4096
HR = 72
DEQ = 1.0 / 127.0
FFN_SC = 0.5 * DEQ * DEQ
EPS = 1e-5
NEGBIG = -1.0e9
SLOPES = [2.0 ** (-8.0 * (i + 1) / H) for i in range(H)]


def slot_of_head(h):
    return 8 * (h % 2) + h // 2


def head_of_slot(sg):
    return 2 * (sg % 8) + sg // 8


T_BAND = 64.0
WINQ = []
for _sl in range(8):
    _w = T_BAND / SLOPES[2 * _sl + 1]
    WINQ.append(None if _w >= S else int(math.ceil(_w / 512.0)))


def chain_quads(j, sl):
    hi = 2 * j + 2
    lo = 0 if WINQ[sl] is None else max(0, 2 * j - WINQ[sl])
    return lo, hi


class Stream:
    def __init__(self, handle):
        self.h = handle
        self.n = 0

    def inc(self, ins, k):
        ins.then_inc(self.h, k)
        self.n += k
        return self.n


DBG = {"kv_vl": True, "kv_kl": True}


def build_program(dumps=(), stop_after=None):
    nc = bass.Bass("TRN2", target_bir_lowering=False)
    t = {}

    def din(name, shape, dt):
        t[name] = nc.dram_tensor(name, shape, dt, kind="ExternalInput").ap()

    def dout(name, shape, dt):
        t[name] = nc.dram_tensor(name, shape, dt, kind="ExternalOutput").ap()

    def dint(name, shape, dt, shared=False):
        t[name] = nc.dram_tensor(name, shape, dt, kind="Internal",
                                 addr_space="Shared" if shared else "Local").ap()

    din("x", [D, NTOK], F32)
    din("wgT", [D, HID], I32)
    din("wuT", [D, HID], I32)
    din("wdT", [HID, D], I32)
    din("wkT", [D, D], F32)
    din("wvT", [D, D], F32)
    din("wqTm", [D, 512], F32)
    din("woTs", [D, D], F32)
    din("outb", [128, 8], F32)
    din("csK", [512, NTOK], BF16)
    din("snK", [512, NTOK], BF16)
    din("csQ", [256, NISL], BF16)
    din("snQ", [256, NISL], BF16)
    din("a3K", [3 * 16, NTOK], BF16)
    din("qa3", [3 * 8, NISL], BF16)
    din("ones16", [16, S], BF16)
    din("maskP", [128, 8 * 512], BF16)
    din("hselK", [128, 8 * 16], BF16)
    din("hselQ", [128, 4 * 8], BF16)
    din("sel16", [16, 8 * 128], BF16)
    din("selhh", [16, 8], BF16)
    din("i128", [128, 128], BF16)
    din("onescol", [1, 128], F32)
    din("onescolf", [128, 1], F32)
    din("onescolb", [128, 1], BF16)
    din("nonce", [1, 16], U32)
    din("fconst", [1, 16], F32)
    dout("y", [D, NOWN], F32)

    dint("wg_bf", [128, 8 * HID], BF16)
    dint("wu_bf", [128, 8 * HID], BF16)
    dint("wd_bf", [8 * 128, 24 * 128], BF16)
    dint("QL", [8 * HR, NISL], BF16)
    dint("x3d", [D, NOWN], F32)
    dint("xmp", [D, NTOK], F32)
    dint("xip", [D, NISL], F32)

    dint("xms", [D, S], F32, shared=True)
    dint("KL", [H * HR + 16, S], BF16, shared=True)
    dint("VL", [64 * 16 * 128, 65], BF16, shared=True)
    dint("OL", [16 * 64 + 32, NISL], BF16, shared=True)
    dint("flags", [2, 16], U32, shared=True)
    dint("qdbg", [64, 8192], F32)
    dint("qdbh", [48, 8192], BF16)

    VLr = t["VL"].rearrange("(kc s k) c -> kc s k c", s=16, k=128)
    t["kvdbg"] = nc.dram_tensor("kvdbg", [128, 1040], BF16, kind="Internal").ap()
    KLs = t["KL"].rearrange("(s r) n -> s r n", r=HR) if False else None
    NORM0 = H * HR

    for nm in dumps:
        shp = {"xms": ([D, S], F32), "KL": ([H * HR + 16, S], BF16),
               "qdbg": ([64, 8192], F32), "qdbh": ([48, 8192], BF16),
               "VL": ([64 * 16 * 128, 65], BF16), "QL": ([8 * HR, NISL], BF16),
               "OL": ([16 * 64 + 32, NISL], BF16), "x3d": ([D, NOWN], F32)}[nm]
        dout("d_" + nm, *shp)

    est = contextlib.ExitStack()
    _streams = {}

    def st(name):
        if name not in _streams:
            _streams[name] = Stream(est.enter_context(nc.semaphore(name)))
        return _streams[name]

    sbp = lambda n, shp, dt: est.enter_context(nc.sbuf_tensor(n, shp, dt))

    def emit_dumps():
        if not dumps:
            return
        DU = st("DU")
        with nc.Block() as blk2:
            @blk2.gpsimd
            def _(g):
                for snm, s_ in list(_streams.items()):
                    if snm != "DU" and s_.n > 0:
                        g.wait_ge(s_.h, s_.n)
                for nm in dumps:
                    DU.inc(g.dma_start(out=t["d_" + nm][:], in_=t[nm][:]), 16)
                g.wait_ge(DU.h, DU.n)

    onescol_sb = sbp("onescol_sb", [1, 128], F32)
    onescolf_sb = sbp("onescolf_sb", [128, 1], F32)
    onescolb_sb = sbp("onescolb_sb", [128, 1], BF16)
    hselK_sb = sbp("hselK_sb", [128, 8 * 16], BF16)
    hselQ_sb = sbp("hselQ_sb", [128, 4 * 8], BF16)
    sel16_sb = sbp("sel16_sb", [16, 8 * 128], BF16)
    selhh_sb = sbp("selhh_sb", [16, 8], BF16)
    i128_sb = sbp("i128_sb", [128, 128], BF16)
    nonce_sb = sbp("nonce_sb", [1, 16], U32)
    fconst_sb = sbp("fconst_sb", [1, 16], F32)
    flag_sb = sbp("flag_sb", [1, 16], U32)
    onesrow_sb = sbp("onesrow_sb", [1, 128], BF16)
    roww_sb = sbp("roww_sb", [1, 2048], F32)
    rows16_sb = sbp("rows16_sb", [1, 8192], BF16)

    CST = st("CST")
    WCH = st("WCH")
    BDN = st("BDN")

    es_ = contextlib.ExitStack()
    wdtmp = es_.enter_context(nc.sbuf_tensor("wdtmp", [128, 24 * D], BF16))
    blk = es_.enter_context(nc.Block())
    @blk.sync
    def _(sy):
        for dst, src in [(onescol_sb, t["onescol"][:]),
                         (onescolf_sb, t["onescolf"][:]),
                         (onescolb_sb, t["onescolb"][:]),
                         (hselK_sb, t["hselK"][:]),
                         (hselQ_sb, t["hselQ"][:]),
                         (sel16_sb, t["sel16"][:]),
                         (i128_sb, t["i128"][:]),
                         (nonce_sb, t["nonce"][:]),
                         (fconst_sb, t["fconst"][:]),
                         (onesrow_sb, t["ones16"][0:1, 0:128]),
                         (selhh_sb, t["selhh"][:])]:
            CST.inc(sy.dma_start(out=dst[:], in_=src), 16)

    @blk.gpsimd
    def _(g):
        WCH.inc(g.dma_start(out=wdtmp[:].rearrange("p (c n) -> p c n", c=24),
                            in_=t["wdT"].rearrange("(c p) n -> p c n", p=128)), 16)
        g.wait_ge(WCH.h, WCH.n)
        wtr = wdtmp[:].rearrange("p (h n) -> p h n", h=24)
        for dc in range(8):
            WCH.inc(g.dma_start(
                out=t["wd_bf"][dc * 128:(dc + 1) * 128, :
                               ].rearrange("p (h n) -> p h n", h=24),
                in_=wtr[:, :, dc * 128:(dc + 1) * 128]), 16)
        g.wait_ge(WCH.h, WCH.n)
    es_.close()
    CST_ALL = CST.n
    WD_CACHED = WCH.n

    LN_LD = st("LN_LD"); LN_SQ = st("LN_SQ"); LN_MM = st("LN_MM")
    LN_EV = st("LN_EV"); LN_VAR = st("LN_VAR"); LN_R = st("LN_R")
    LN_MR = st("LN_MR"); LN_CB = st("LN_CB")

    def ln_block(pfx, n_tb, xsrc=None, gp_loader=None, pre_wait=()):
        b_ld, b_sq, b_mm, b_ev = LN_LD.n, LN_SQ.n, LN_MM.n, LN_EV.n
        b_var, b_r, b_mr = LN_VAR.n, LN_R.n, LN_MR.n
        b_cb = LN_CB.n
        es_ = contextlib.ExitStack()
        xa = es_.enter_context(nc.sbuf_tensor(pfx + "xa", [128, 8 * 512], F32))
        xb = es_.enter_context(nc.sbuf_tensor(pfx + "xb", [128, 8 * 512], F32))
        xb16 = es_.enter_context(nc.sbuf_tensor(pfx + "xb16", [128, 8 * 512], BF16))
        sq = es_.enter_context(nc.sbuf_tensor(pfx + "sq", [128, 8 * 512], BF16))
        p1 = es_.enter_context(nc.psum_tensor(pfx + "p1", [1, 512], F32))
        p2 = es_.enter_context(nc.psum_tensor(pfx + "p2", [1, 512], F32))
        blk = es_.enter_context(nc.Block())
        xt = [xa, xb]

        def loads(eng, cid=None):
            for ww in pre_wait:
                eng.wait_ge(ww[0].h, ww[1])
            for tb in range(n_tb):
                if tb >= 2:
                    eng.wait_ge(LN_MM.h, b_mm + tb - 1)
                if gp_loader is None:
                    LN_LD.inc(eng.dma_start(
                        out=xt[tb % 2][:].rearrange("p (c n) -> p c n", c=8),
                        in_=xsrc(tb)), 16)
                else:
                    gp_loader(eng, cid, tb, xt[tb % 2], LN_LD)

        if gp_loader is None:
            @blk.sync
            def _(sy):
                loads(sy)
        else:
            @blk.sync
            def _(sy):
                pid = sy.partition_id()
                n0 = LN_LD.n
                for cid in nc.Switch(engines=[sy], index=[pid], n=8):
                    LN_LD.n = n0
                    loads(sy, cid)

        @blk.scalar
        def _(s):
            for tb in range(n_tb):
                s.wait_ge(LN_CB.h, b_cb + tb + 1)
                if tb >= 1:
                    s.wait_ge(LN_MM.h, b_mm + tb)
                LN_SQ.inc(s.activation(out=sq[:], in_=xb16[:],
                                       func=AF.Square), 1)
                s.wait_ge(LN_VAR.h, b_var + tb + 1)
                s.activation(out=roww_sb[0:1, 1024:1536], in_=roww_sb[0:1, 512:1024],
                             func=AF.Ln, bias=fconst_sb[0:1, 0:1])
                LN_R.inc(s.activation(out=roww_sb[0:1, 1024:1536],
                                      in_=roww_sb[0:1, 1024:1536],
                                      func=AF.Exp, scale=-0.5), 1)

        @blk.tensor
        def _(te):
            te.wait_ge(CST.h, CST_ALL)
            for tb in range(n_tb):
                te.wait_ge(LN_SQ.h, b_sq + tb + 1)
                if tb >= 1:
                    te.wait_ge(LN_EV.h, b_ev + tb)
                for ic in range(8):
                    te.matmul(p1[0:1, :], onescolb_sb[:, 0:1],
                              xb16[:, ic * 512:(ic + 1) * 512],
                              start=(ic == 0), stop=(ic == 7))
                for ic in range(8):
                    mm = te.matmul(p2[0:1, :], onescolb_sb[:, 0:1],
                                   sq[:, ic * 512:(ic + 1) * 512],
                                   start=(ic == 0), stop=(ic == 7))
                LN_MM.inc(mm, 1)

        @blk.vector
        def _(ve):
            for tb in range(n_tb):
                ve.wait_ge(LN_LD.h, b_ld + (tb + 1) * 16)
                LN_CB.inc(ve.tensor_copy(xb16[:, :], xt[tb % 2][:, :]), 1)
                ve.wait_ge(LN_MM.h, b_mm + tb + 1)
                if tb >= 1:
                    ve.wait_ge(LN_R.h, b_r + tb)
                ve.tensor_copy(roww_sb[0:1, 0:512], p1[0:1, :])
                LN_EV.inc(ve.tensor_copy(roww_sb[0:1, 512:1024], p2[0:1, :]), 1)
                ve.tensor_scalar_mul(roww_sb[0:1, 0:512], roww_sb[0:1, 0:512],
                                     1.0 / 1024.0)
                ve.tensor_scalar_mul(roww_sb[0:1, 512:1024], roww_sb[0:1, 512:1024],
                                     1.0 / 1024.0)
                ve.tensor_mul(roww_sb[0:1, 1024:1536], roww_sb[0:1, 0:512],
                              roww_sb[0:1, 0:512])
                LN_VAR.inc(ve.tensor_sub(roww_sb[0:1, 512:1024],
                                         roww_sb[0:1, 512:1024],
                                         roww_sb[0:1, 1024:1536]), 1)
                ve.wait_ge(LN_R.h, b_r + tb + 1)
                ve.tensor_copy(rows16_sb[0:1, tb * 512:(tb + 1) * 512],
                               roww_sb[0:1, 1024:1536])
                LN_MR.inc(ve.tensor_mul(
                    rows16_sb[0:1, 4096 + tb * 512:4096 + (tb + 1) * 512],
                    roww_sb[0:1, 0:512], roww_sb[0:1, 1024:1536]), 1)

        es_.close()
        return LN_MR.n

    F_XL = st("F_XL"); F_WD = st("F_WD"); F_BC = st("F_BC"); F_AP = st("F_AP")
    F_GU = st("F_GU"); F_SI = st("F_SI"); F_MU = st("F_MU"); F_DM = st("F_DM")
    F_XE = st("F_XE"); F_XS = st("F_XS")

    def ffn_block(pfx, n_tb, xsrc, wg_sb, wu_sb, sink, sink_switch,
                  mr_count, extra_pe_wait=(), sync_pre_wait=(), xs_per_tb=16):
        b_xl, b_wd, b_bc, b_ap = F_XL.n, F_WD.n, F_BC.n, F_AP.n
        b_gu, b_si, b_mu, b_dm = F_GU.n, F_SI.n, F_MU.n, F_DM.n
        b_xe, b_xs = F_XE.n, F_XS.n
        es_ = contextlib.ExitStack()
        xa = es_.enter_context(nc.sbuf_tensor(pfx + "xa", [128, 8 * 512], F32))
        xb = es_.enter_context(nc.sbuf_tensor(pfx + "xb", [128, 8 * 512], F32))
        xn = es_.enter_context(nc.sbuf_tensor(pfx + "xn", [128, 8 * 512], BF16))
        xsc = es_.enter_context(nc.sbuf_tensor(pfx + "xsc", [128, 512], F32))
        sg0 = es_.enter_context(nc.sbuf_tensor(pfx + "sg0", [128, 512], BF16))
        sg1 = es_.enter_context(nc.sbuf_tensor(pfx + "sg1", [128, 512], BF16))
        ha = es_.enter_context(nc.sbuf_tensor(pfx + "ha", [128, 24 * 512], BF16))
        wdb = [es_.enter_context(nc.sbuf_tensor(pfx + "wd%d" % i,
                                                 [128, 24 * 128], BF16))
               for i in range(2)]
        pR = es_.enter_context(nc.psum_tensor(pfx + "pR", [128, 512], F32))
        pM = es_.enter_context(nc.psum_tensor(pfx + "pM", [128, 512], F32))
        pg0 = es_.enter_context(nc.psum_tensor(pfx + "pg0", [128, 512], F32))
        pg1 = es_.enter_context(nc.psum_tensor(pfx + "pg1", [128, 512], F32))
        pu0 = es_.enter_context(nc.psum_tensor(pfx + "pu0", [128, 512], F32))
        pu1 = es_.enter_context(nc.psum_tensor(pfx + "pu1", [128, 512], F32))
        pd0 = es_.enter_context(nc.psum_tensor(pfx + "pd0", [128, 512], F32))
        pd1 = es_.enter_context(nc.psum_tensor(pfx + "pd1", [128, 512], F32))
        blk = es_.enter_context(nc.Block())
        xt = [xa, xb]
        sg = [sg0, sg1]
        pg = [pg0, pg1]
        pu = [pu0, pu1]
        pd = [pd0, pd1]

        @blk.sync
        def _(sy):
            sy.wait_ge(WCH.h, WD_CACHED)
            for ww in sync_pre_wait:
                sy.wait_ge(ww[0].h, ww[1])
            for tb in range(n_tb):
                if tb >= 2:
                    sy.wait_ge(F_XS.h, b_xs + (tb - 1) * xs_per_tb)
                F_XL.inc(sy.dma_start(
                    out=xt[tb % 2][:].rearrange("p (c n) -> p c n", c=8),
                    in_=xsrc(tb)), 16)
                for dc in range(8):
                    k = tb * 8 + dc
                    if k >= 2:
                        sy.wait_ge(F_DM.h, b_dm + k - 1)
                    F_WD.inc(sy.dma_start(
                        out=wdb[k % 2][:, :],
                        in_=t["wd_bf"][dc * 128:(dc + 1) * 128, :]), 16)

        @blk.tensor
        def _(te):
            te.wait_ge(CST.h, CST_ALL)
            te.wait_ge(LN_MR.h, mr_count)
            for ww in extra_pe_wait:
                te.wait_ge(ww[0].h, ww[1])
            for tb in range(n_tb):
                nb = tb * 24
                te.wait_ge(F_XL.h, b_xl + (tb + 1) * 16)
                if tb >= 1:
                    te.wait_ge(F_AP.h, b_ap + tb * 8)
                te.matmul(pR[:, :], onesrow_sb[0:1, :],
                          rows16_sb[0:1, tb * 512:(tb + 1) * 512],
                          start=True, stop=True)
                F_BC.inc(te.matmul(pM[:, :], onesrow_sb[0:1, :],
                                   rows16_sb[0:1, 4096 + tb * 512:4096 + (tb + 1) * 512],
                                   start=True, stop=True), 1)
                for h in range(24):
                    hp = h % 2
                    if nb + h >= 2:
                        te.wait_ge(F_SI.h, b_si + nb + h - 1)
                        te.wait_ge(F_MU.h, b_mu + nb + h - 1)
                    for ic in range(8):
                        if h == 0:
                            te.wait_ge(F_AP.h, b_ap + tb * 8 + ic + 1)
                        te.matmul(pg[hp][:, :],
                                  wg_sb[:, ic * HID + h * 128:ic * HID + (h + 1) * 128],
                                  xn[:, ic * 512:(ic + 1) * 512],
                                  start=(ic == 0), stop=(ic == 7))
                    for ic in range(8):
                        mm = te.matmul(pu[hp][:, :],
                                       wu_sb[:, ic * HID + h * 128:ic * HID + (h + 1) * 128],
                                       xn[:, ic * 512:(ic + 1) * 512],
                                       start=(ic == 0), stop=(ic == 7))
                    F_GU.inc(mm, 1)
                te.wait_ge(F_MU.h, b_mu + nb + 24)
                for dc in range(8):
                    k = tb * 8 + dc
                    te.wait_ge(F_WD.h, b_wd + (k + 1) * 16)
                    if k >= 2:
                        te.wait_ge(F_XE.h, b_xe + k - 1)
                    for h in range(24):
                        mm = te.matmul(pd[k % 2][:, :],
                                       wdb[k % 2][:, h * 128:(h + 1) * 128],
                                       ha[:, h * 512:(h + 1) * 512],
                                       start=(h == 0), stop=(h == 23))
                    F_DM.inc(mm, 1)

        @blk.vector
        def _(ve):
            for tb in range(n_tb):
                nb = tb * 24
                ve.wait_ge(F_BC.h, b_bc + tb + 1)
                if tb >= 1:
                    ve.wait_ge(F_GU.h, b_gu + nb)
                for ic in range(8):
                    ve.tensor_mul(xsc[:, :],
                                  xt[tb % 2][:, ic * 512:(ic + 1) * 512], pR[:, :])
                    F_AP.inc(ve.tensor_sub(xn[:, ic * 512:(ic + 1) * 512],
                                           xsc[:, :], pM[:, :]), 1)
                for h in range(24):
                    ve.wait_ge(F_SI.h, b_si + nb + h + 1)
                    if tb >= 1 and h == 0:
                        ve.wait_ge(F_DM.h, b_dm + tb * 8)
                    F_MU.inc(ve.tensor_mul(ha[:, h * 512:(h + 1) * 512],
                                           sg[h % 2][:, :], pu[h % 2][:, :]), 1)
                for dc in range(8):
                    k = tb * 8 + dc
                    ve.wait_ge(F_DM.h, b_dm + k + 1)
                    xv = xt[tb % 2][:].rearrange("p (c n) -> p c n", c=8)[:, dc, :]
                    F_XE.inc(ve.scalar_tensor_tensor(
                        xv, pd[k % 2][:, :], FFN_SC, xv, ALU.mult, ALU.add), 1)

        @blk.scalar
        def _(s):
            for tb in range(n_tb):
                nb = tb * 24
                for h in range(24):
                    s.wait_ge(F_GU.h, b_gu + nb + h + 1)
                    if nb + h >= 2:
                        s.wait_ge(F_MU.h, b_mu + nb + h - 1)
                    F_SI.inc(s.activation(out=sg[h % 2][:, :], in_=pg[h % 2][:, :],
                                          func=AF.Silu, scale=DEQ), 1)

        @blk.gpsimd
        def _(g):
            if sink_switch:
                pid = g.partition_id()
                n0 = F_XS.n
                for cid in nc.Switch(engines=[g], index=[pid], n=8):
                    F_XS.n = n0
                    for tb in range(n_tb):
                        g.wait_ge(F_XE.h, b_xe + (tb + 1) * 8)
                        sink(g, cid, tb, xt[tb % 2])
            else:
                for tb in range(n_tb):
                    g.wait_ge(F_XE.h, b_xe + (tb + 1) * 8)
                    sink(g, None, tb, xt[tb % 2])
            g.wait_ge(F_XS.h, F_XS.n)

        es_.close()
    def x_cols(tb):
        return t["x"][:, tb * 512:(tb + 1) * 512].rearrange("(c p) n -> p c n", p=128)

    if stop_after == "b0":
        emit_dumps()
        est.close()
        return nc, t
    mr1 = ln_block("L1_", 8, xsrc=x_cols)
    if stop_after == "ln1":
        emit_dumps()
        est.close()
        return nc, t

    WGU = st("WGU")
    es_ = contextlib.ExitStack()
    wgT_sb = es_.enter_context(nc.sbuf_tensor("wgT_sb", [128, 8 * HID], BF16))
    wuT_sb = es_.enter_context(nc.sbuf_tensor("wuT_sb", [128, 8 * HID], BF16))
    with nc.Block() as blk:
        @blk.gpsimd
        def _(g):
            WGU.inc(g.dma_start(
                out=wgT_sb[:].rearrange("p (c n) -> p c n", c=8),
                in_=t["wgT"].rearrange("(c p) n -> p c n", p=128)), 16)
            WGU.inc(g.dma_start(
                out=wuT_sb[:].rearrange("p (c n) -> p c n", c=8),
                in_=t["wuT"].rearrange("(c p) n -> p c n", p=128)), 16)
            g.wait_ge(WGU.h, WGU.n)
            WGU.inc(g.dma_start(out=t["wg_bf"][:, :], in_=wgT_sb[:, :]), 16)
            WGU.inc(g.dma_start(out=t["wu_bf"][:, :], in_=wuT_sb[:, :]), 16)

    def ffn1_sink(g, cid, tb, buf):
        hh = cid % 2
        F_XS.inc(g.dma_start(
            out=t["xms"][:, hh * NTOK + tb * 512:hh * NTOK + (tb + 1) * 512
                         ].rearrange("(c p) n -> p c n", p=128),
            in_=buf[:].rearrange("p (c n) -> p c n", c=8)), 16)

    ffn_block("F1_", 8, x_cols, wgT_sb, wuT_sb, ffn1_sink, True, mr1,
              extra_pe_wait=[(WGU, 32)], xs_per_tb=16)
    es_.close()
    F1_XS_ALL = F_XS.n
    WGU_CACHED = WGU.n
    if stop_after == "ffn1":
        emit_dumps()
        est.close()
        return nc, t

    def ln2_loader(eng, cid, tb, dst, LD):
        hh = cid % 2
        col0 = hh * NTOK + tb * 512
        LD.inc(eng.dma_start(
            out=dst[:].rearrange("p (c n) -> p c n", c=8),
            in_=t["xms"][:, col0:col0 + 512
                         ].rearrange("(c p) n -> p c n", p=128)), 16)

    mr2 = ln_block("L2_", 8, gp_loader=ln2_loader, pre_wait=[(F_XS, F1_XS_ALL)])
    if stop_after == "ln2":
        emit_dumps()
        est.close()
        return nc, t

    emit_kv(nc, t, st, VLr, NORM0, mr2,
            onesrow_sb, onescolb_sb, hselK_sb, rows16_sb)

    if stop_after == "kv":
        emit_dumps()
        est.close()
        return nc, t
    kvdone = [(st("KV_WR"), st("KV_WR").n), (st("KV_KLW"), st("KV_KLW").n),
              (st("KV_KNW"), st("KV_KNW").n), (st("KV_VLW"), st("KV_VLW").n),
              (st("F_XS"), F_XS.n)]
    emit_barrier(nc, t, st, nonce_sb, flag_sb, 0, kvdone, BDN)

    if stop_after == "bar1":
        emit_dumps()
        est.close()
        return nc, t
    mr3 = emit_q(nc, t, st, ln_block, NORM0,
                 onesrow_sb, onescolb_sb, hselQ_sb, rows16_sb, selhh_sb, BDN)

    if stop_after == "q":
        emit_dumps()
        est.close()
        return nc, t
    emit_attn(nc, t, st, VLr, i128_sb)

    if stop_after == "attn":
        emit_dumps()
        est.close()
        return nc, t
    emit_barrier(nc, t, st, nonce_sb, flag_sb, 1,
                 [(st("AT_OW"), st("AT_OW").n)], BDN)

    if stop_after == "bar2":
        emit_dumps()
        est.close()
        return nc, t
    emit_tail(nc, t, st, ln_block, ffn_block,
              onescol_sb, onescolb_sb, sel16_sb, BDN, WGU_CACHED)

    emit_dumps()

    est.close()
    return nc, t


def emit_kv(nc, t, st, VLr, NORM0, mr2,
            onesrow_sb, onescolb_sb, hselK_sb, rows16_sb):
    LN_MR = st("LN_MR")
    KV_W = st("KV_W"); KV_XL = st("KV_XL"); KV_CT = st("KV_CT")
    KV_BC = st("KV_BC"); KV_AP = st("KV_AP"); KV_KP = st("KV_KP")
    KV_KE = st("KV_KE"); KV_RO = st("KV_RO"); KV_KQ = st("KV_KQ")
    KV_NM = st("KV_NM"); KV_NE = st("KV_NE"); KV_VM = st("KV_VM")
    KV_VE = st("KV_VE"); KV_VO = st("KV_VO")
    KV_KLW = st("KV_KLW"); KV_KNW = st("KV_KNW"); KV_VLW = st("KV_VLW")
    KV_WR = st("KV_WR")
    es = contextlib.ExitStack()
    wkT_sb = es.enter_context(nc.sbuf_tensor("wkT_sb", [128, 8 * 1024], BF16))
    wvT_sb = es.enter_context(nc.sbuf_tensor("wvT_sb", [128, 8 * 1024], BF16))
    KVxa = es.enter_context(nc.sbuf_tensor("KVxa", [128, 8 * 512], F32))
    KVxb = es.enter_context(nc.sbuf_tensor("KVxb", [128, 8 * 512], F32))
    KVxn = es.enter_context(nc.sbuf_tensor("KVxn", [128, 8 * 512], BF16))
    KVxsc = es.enter_context(nc.sbuf_tensor("KVxsc", [128, 512], F32))
    KVkraw = es.enter_context(nc.sbuf_tensor("KVkraw", [128, 2 * 512], BF16))
    KVkrot = es.enter_context(nc.sbuf_tensor("KVkrot", [128, 2 * 512], BF16))
    KVu = es.enter_context(nc.sbuf_tensor("KVu", [128, 512], BF16))
    KVksq = es.enter_context(nc.sbuf_tensor("KVksq", [128, 4 * 1024], BF16))
    KVcs = es.enter_context(nc.sbuf_tensor("KVcs", [128, 2 * 2048], BF16))
    KVsn = es.enter_context(nc.sbuf_tensor("KVsn", [128, 2 * 2048], BF16))
    KVvs0 = es.enter_context(nc.sbuf_tensor("KVvs0", [128, 1040], BF16))
    KVvs1 = es.enter_context(nc.sbuf_tensor("KVvs1", [128, 1040], BF16))
    KVkn = es.enter_context(nc.sbuf_tensor("KVkn", [16, 512], BF16))
    KVpR = es.enter_context(nc.psum_tensor("KVpR", [128, 512], F32))
    KVpM = es.enter_context(nc.psum_tensor("KVpM", [128, 512], F32))
    KVpkA = es.enter_context(nc.psum_tensor("KVpkA", [128, 512], F32))
    KVpkB = es.enter_context(nc.psum_tensor("KVpkB", [128, 512], F32))
    KVpvA = es.enter_context(nc.psum_tensor("KVpvA", [128, 512], F32))
    KVpvB = es.enter_context(nc.psum_tensor("KVpvB", [128, 512], F32))
    KVpn = es.enter_context(nc.psum_tensor("KVpn", [16, 512], F32))
    blk = es.enter_context(nc.Block())
    KVxt = [KVxa, KVxb]
    KVvs = [KVvs0, KVvs1]

    @blk.sync
    def _(sy):
        for tb in range(DBG.get("kv_tbs", 8)):
            if tb >= 2:
                sy.wait_ge(KV_RO.h, (tb - 1) * 4)
            if DBG.get("kv_ct", True):
                KV_CT.inc(sy.dma_start(
                    out=KVcs[:, (tb % 2) * 2048:(tb % 2 + 1) * 2048
                             ].rearrange("p (c n) -> p c n", c=4),
                    in_=t["csK"][:, tb * 512:(tb + 1) * 512
                                 ].rearrange("(c p) n -> p c n", p=128)), 16)
                KV_CT.inc(sy.dma_start(
                    out=KVsn[:, (tb % 2) * 2048:(tb % 2 + 1) * 2048
                             ].rearrange("p (c n) -> p c n", c=4),
                    in_=t["snK"][:, tb * 512:(tb + 1) * 512
                                 ].rearrange("(c p) n -> p c n", p=128)), 16)
            else:
                KV_CT.inc(sy.dma_start(
                    out=KVcs[:, (tb % 2) * 2048:(tb % 2) * 2048 + 512],
                    in_=t["csK"][0:128, 0:512]), 16)
                KV_CT.inc(sy.dma_start(
                    out=KVsn[:, (tb % 2) * 2048:(tb % 2) * 2048 + 512],
                    in_=t["snK"][0:128, 0:512]), 16)

    @blk.gpsimd
    def _(g):
        KV_W.inc(g.dma_start(
            out=wkT_sb[:].rearrange("p (c n) -> p c n", c=8),
            in_=t["wkT"].rearrange("(c p) n -> p c n", p=128)), 16)
        KV_W.inc(g.dma_start(
            out=wvT_sb[:].rearrange("p (c n) -> p c n", c=8),
            in_=t["wvT"].rearrange("(c p) n -> p c n", p=128)), 16)
        pid = g.partition_id()
        base = dict(XL=KV_XL.n, KLW=KV_KLW.n, KNW=KV_KNW.n,
                    VLW=KV_VLW.n, WR=KV_WR.n)
        for cid in nc.Switch(engines=[g], index=[pid], n=8):
            KV_XL.n, KV_KLW.n = base["XL"], base["KLW"]
            KV_KNW.n, KV_VLW.n, KV_WR.n = base["KNW"], base["VLW"], base["WR"]
            hh = cid % 2
            col0 = hh * NTOK
            KLh = t["KL"][0:H * HR, :].rearrange("(s r) n -> s r n", r=HR)
            for k in range(5):
                KV_WR.inc(g.dma_start(
                    out=KLh[0:16, 64 + k, col0:col0 + NTOK],
                    in_=t["ones16"][:, 0:NTOK]), 16)
            for k in range(3):
                KV_WR.inc(g.dma_start(
                    out=KLh[0:16, 69 + k, col0:col0 + NTOK],
                    in_=t["a3K"][k * 16:(k + 1) * 16, :]), 16)
            for tb in range(DBG.get("kv_tbs", 8)):
                if tb >= 2:
                    g.wait_ge(KV_AP.h, (tb - 1) * 8)
                if DBG.get("kv_xl", True):
                    KV_XL.inc(g.dma_start(
                        out=KVxt[tb % 2][:].rearrange("p (c n) -> p c n", c=8),
                        in_=t["xms"][:, col0 + tb * 512:col0 + (tb + 1) * 512
                                     ].rearrange("(c p) n -> p c n", p=128)), 16)
                else:
                    KV_XL.inc(g.dma_start(
                        out=KVxt[tb % 2][:, :],
                        in_=t["xms"][0:128, 0:4096]), 16)
                for ocp in range(4):
                    idx = tb * 4 + ocp
                    g.wait_ge(KV_RO.h, idx + 1)
                    for half in range(2):
                        for sub in range(2):
                            h = 2 * ocp + sub + 8 * half
                            sg_ = 8 * (h % 2) + h // 2
                            if DBG["kv_kl"]:
                                KV_KLW.inc(g.dma_start(
                                    out=t["KL"][sg_ * HR + 0:sg_ * HR + 64,
                                                col0 + tb * 512:col0 + (tb + 1) * 512],
                                    in_=KVkrot[sub * 64:(sub + 1) * 64,
                                               half * 512:(half + 1) * 512]), 16)
                            else:
                                KV_KLW.inc(g.dma_start(
                                    out=t["kvdbg"][sub * 64:(sub + 1) * 64, 0:512],
                                    in_=KVkrot[sub * 64:(sub + 1) * 64,
                                               half * 512:(half + 1) * 512]), 16)
                if tb >= 1:
                    g.wait_ge(KV_NE.h, tb)
                    KV_KNW.inc(g.dma_start(
                        out=t["KL"][NORM0:NORM0 + 16,
                                    col0 + (tb - 1) * 512:col0 + tb * 512],
                        in_=KVkn[:, :]), 16)
                for tc in range(4):
                    idx = tb * 4 + tc
                    g.wait_ge(KV_VE.h, idx + 1)
                    g.wait_ge(KV_VO.h, 2)
                    kc = hh * 32 + tb * 4 + tc
                    if DBG["kv_vl"]:
                        KV_VLW.inc(g.dma_start(
                            out=VLr[kc, :, :, :].rearrange("s k c -> k s c"),
                            in_=KVvs[tc % 2][:].rearrange("p (s c) -> p s c", s=16)), 16)
                    else:
                        KV_VLW.inc(g.dma_start(
                            out=t["kvdbg"][:, :], in_=KVvs[tc % 2][:, :]), 16)
            NTBg = DBG.get("kv_tbs", 8)
            if NTBg >= 1:
                g.wait_ge(KV_NE.h, NTBg)
                KV_KNW.inc(g.dma_start(
                    out=t["KL"][NORM0:NORM0 + 16,
                                col0 + (NTBg - 1) * 512:col0 + NTBg * 512],
                    in_=KVkn[:, :]), 16)

    @blk.tensor
    def _(te):
        def emit_norms(nb):
            te.wait_ge(KV_KQ.h, nb * 4 + 4)
            if nb >= 1:
                te.wait_ge(KV_NE.h, nb)
            mm = None
            for ocp in range(4):
                for half in range(2):
                    oc = ocp + 4 * half
                    mm = te.matmul(KVpn[:, :], hselK_sb[:, oc * 16:(oc + 1) * 16],
                                   KVksq[:, ocp * 1024 + half * 512:
                                         ocp * 1024 + (half + 1) * 512],
                                   start=(ocp == 0 and half == 0),
                                   stop=(ocp == 3 and half == 1))
            KV_NM.inc(mm, 1)

        te.wait_ge(KV_W.h, 32)
        te.wait_ge(LN_MR.h, mr2)
        NTB_KV = DBG.get("kv_tbs", 8)
        for tb in range(NTB_KV):
            te.wait_ge(KV_XL.h, (tb + 1) * 16)
            if tb >= 1:
                te.wait_ge(KV_AP.h, tb * 8)
            te.matmul(KVpR[:, :], onesrow_sb[0:1, :],
                      rows16_sb[0:1, tb * 512:(tb + 1) * 512], start=True, stop=True)
            KV_BC.inc(te.matmul(KVpM[:, :], onesrow_sb[0:1, :],
                                rows16_sb[0:1, 4096 + tb * 512:4096 + (tb + 1) * 512],
                                start=True, stop=True), 1)
            if tb >= 1:
                emit_norms(tb - 1)
            for ocp in range(4):
                idx = tb * 4 + ocp
                if idx >= 1:
                    te.wait_ge(KV_KE.h, idx)
                nic = 8 if DBG.get("kv_kproj", True) else 1
                for ic in range(nic):
                    if ocp == 0:
                        te.wait_ge(KV_AP.h, tb * 8 + ic + 1)
                    te.matmul(KVpkA[:, :],
                              wkT_sb[:, ic * 1024 + ocp * 128:ic * 1024 + (ocp + 1) * 128],
                              KVxn[:, ic * 512:(ic + 1) * 512],
                              start=(ic == 0), stop=(ic == nic - 1))
                for ic in range(nic):
                    mm = te.matmul(KVpkB[:, :],
                                   wkT_sb[:, ic * 1024 + (ocp + 4) * 128:ic * 1024 + (ocp + 5) * 128],
                                   KVxn[:, ic * 512:(ic + 1) * 512],
                                   start=(ic == 0), stop=(ic == nic - 1))
                KV_KP.inc(mm, 1)
            for tc in range(4):
                idx = tb * 4 + tc
                if idx >= 1:
                    te.wait_ge(KV_VE.h, idx)
                niv = 8 if DBG.get("kv_vproj", True) else 1
                for ic in range(niv):
                    te.matmul(KVpvA[:, :],
                              KVxn[:, ic * 512 + tc * 128:ic * 512 + (tc + 1) * 128],
                              wvT_sb[:, ic * 1024:ic * 1024 + 512],
                              start=(ic == 0), stop=(ic == niv - 1))
                for ic in range(niv):
                    mm = te.matmul(KVpvB[:, :],
                                   KVxn[:, ic * 512 + tc * 128:ic * 512 + (tc + 1) * 128],
                                   wvT_sb[:, ic * 1024 + 512:ic * 1024 + 1024],
                                   start=(ic == 0), stop=(ic == niv - 1))
                KV_VM.inc(mm, 1)
        if NTB_KV >= 1:
            emit_norms(NTB_KV - 1)

    @blk.vector
    def _(ve):
        KV_VO.inc(ve.memset(
            KVvs0[:].rearrange("p (s c) -> p s c", s=16)[:, :, 64:65], 1.0), 1)
        KV_VO.inc(ve.memset(
            KVvs1[:].rearrange("p (s c) -> p s c", s=16)[:, :, 64:65], 1.0), 1)
        for tb in range(DBG.get("kv_tbs", 8)):
            ve.wait_ge(KV_BC.h, tb + 1)
            if tb >= 1:
                ve.wait_ge(KV_KP.h, tb * 4)
                ve.wait_ge(KV_VM.h, tb * 4)
            for ic in range(8):
                ve.tensor_mul(KVxsc[:, :],
                              KVxt[tb % 2][:, ic * 512:(ic + 1) * 512], KVpR[:, :])
                KV_AP.inc(ve.tensor_sub(KVxn[:, ic * 512:(ic + 1) * 512],
                                        KVxsc[:, :], KVpM[:, :]), 1)
            ve.wait_ge(KV_CT.h, (tb + 1) * 32)
            for ocp in range(4):
                idx = tb * 4 + ocp
                ve.wait_ge(KV_KE.h, idx + 1)
                if idx >= 1:
                    ve.wait_ge(KV_KLW.h, idx * 64)
                    ve.wait_ge(KV_KQ.h, idx)
                cs = KVcs[:, (tb % 2) * 2048 + ocp * 512:(tb % 2) * 2048 + (ocp + 1) * 512]
                sn = KVsn[:, (tb % 2) * 2048 + ocp * 512:(tb % 2) * 2048 + (ocp + 1) * 512]
                if DBG.get("kv_rope", True):
                    ve.tensor_mul(KVu[:, :], KVkraw[:, 512:1024], sn)
                    ve.tensor_mul(KVkrot[:, 0:512], KVkraw[:, 0:512], cs)
                    ve.tensor_sub(KVkrot[:, 0:512], KVkrot[:, 0:512], KVu[:, :])
                    ve.tensor_mul(KVu[:, :], KVkraw[:, 0:512], sn)
                    ve.tensor_mul(KVkrot[:, 512:1024], KVkraw[:, 512:1024], cs)
                    KV_RO.inc(ve.tensor_add(KVkrot[:, 512:1024], KVkrot[:, 512:1024],
                                            KVu[:, :]), 1)
                else:
                    KV_RO.inc(ve.tensor_copy(KVkrot[:, :], KVkraw[:, :]), 1)

    @blk.scalar
    def _(s):
        for tb in range(DBG.get("kv_tbs", 8)):
            for ocp in range(4):
                idx = tb * 4 + ocp
                s.wait_ge(KV_KP.h, idx + 1)
                if idx >= 1:
                    s.wait_ge(KV_RO.h, idx)
                s.activation(out=KVkraw[:, 0:512], in_=KVpkA[:, :], func=AF.Copy)
                KV_KE.inc(s.activation(out=KVkraw[:, 512:1024], in_=KVpkB[:, :],
                                       func=AF.Copy), 1)
                s.wait_ge(KV_RO.h, idx + 1)
                if idx >= 4:
                    s.wait_ge(KV_NM.h, tb)
                if DBG.get("kv_sq", True):
                    KV_KQ.inc(s.activation(out=KVksq[:, ocp * 1024:(ocp + 1) * 1024],
                                           in_=KVkrot[:, :], func=AF.Square), 1)
                else:
                    KV_KQ.inc(s.activation(out=KVksq[:, ocp * 1024:(ocp + 1) * 1024],
                                           in_=KVkrot[:, :], func=AF.Copy), 1)
            if tb >= 1:
                s.wait_ge(KV_NM.h, tb)
                if tb >= 2:
                    s.wait_ge(KV_KNW.h, (tb - 1) * 16)
                KV_NE.inc(s.activation(out=KVkn[:, :], in_=KVpn[:, :],
                                       func=AF.Copy), 1)
            for tc in range(4):
                idx = tb * 4 + tc
                s.wait_ge(KV_VM.h, idx + 1)
                if idx >= 2:
                    s.wait_ge(KV_VLW.h, (idx - 1) * 16)
                if DBG.get("kv_vevac", True):
                    for gi, (pv, lohi, s0) in enumerate([
                            (KVpvA, 0, 0), (KVpvA, 1, 8),
                            (KVpvB, 0, 4), (KVpvB, 1, 12)]):
                        cp = s.activation(
                            out=KVvs[tc % 2][:, s0 * 65:(s0 + 4) * 65
                                             ].rearrange("p (s c) -> p s c", s=4)[:, :, 0:64],
                            in_=pv[:, :].rearrange("p (g e) -> p g e", g=4)
                            [:, :, lohi * 64:(lohi + 1) * 64],
                            func=AF.Copy)
                    KV_VE.inc(cp, 1)
                else:
                    s.activation(out=KVvs[tc % 2][:, 0:512], in_=KVpvA[:, :],
                                 func=AF.Copy)
                    KV_VE.inc(s.activation(out=KVvs[tc % 2][:, 512:1024],
                                           in_=KVpvB[:, :], func=AF.Copy), 1)
        NTB_KV2 = DBG.get("kv_tbs", 8)
        if NTB_KV2 >= 1:
            s.wait_ge(KV_NM.h, NTB_KV2)
            if NTB_KV2 >= 2:
                s.wait_ge(KV_KNW.h, (NTB_KV2 - 1) * 16)
            KV_NE.inc(s.activation(out=KVkn[:, :], in_=KVpn[:, :],
                                   func=AF.Copy), 1)


    es.close()
def emit_barrier(nc, t, st, nonce_sb, flag_sb, widx, wait_list, BDN):
    FW = st("BFW%d" % widx)
    PLm = st("BPL%d" % widx)
    b_pl = 0
    with nc.Block() as blk:
        def body(g):
            for s_, cnt in wait_list:
                g.wait_ge(s_.h, cnt)
            pid = g.partition_id()
            nfw = FW.n
            for cid in nc.Switch(engines=[g], index=[pid], n=8):
                FW.n = nfw
                hh = cid % 2
                FW.inc(g.dma_start(out=t["flags"][hh:hh + 1, widx:widx + 1],
                                   in_=nonce_sb[0:1, widx:widx + 1]), 16)
            g.wait_ge(FW.h, FW.n)
            with g.Fori(0, 1000000) as it:
                for cid in nc.Switch(engines=[g], index=[pid], n=8):
                    other = 1 - (cid % 2)
                    g.dma_start(out=flag_sb[0:1, 8:9],
                                in_=t["flags"][other:other + 1, widx:widx + 1]
                                ).then_inc(PLm.h, 16)
                g.wait_ge(PLm.h, (it + 1) * 16 + b_pl)
                val = nc.values_load(flag_sb[0:1, 8:9], engines=[g.engine],
                                     min_val=0, max_val=(1 << 31) - 1,
                                     skip_runtime_bounds_check=True)
                tgt = nc.values_load(nonce_sb[0:1, widx:widx + 1], engines=[g.engine],
                                     min_val=0, max_val=(1 << 31) - 1,
                                     skip_runtime_bounds_check=True)
                with g.If(val == tgt):
                    g.br("bar%d_done" % widx)
            nc.switch_bb("bar%d_done" % widx)
            PLm.n = b_pl
            BDN.inc(g.dma_start(out=flag_sb[0:1, 12:16], in_=nonce_sb[0:1, 0:4]), 16)
            g.wait_ge(BDN.h, BDN.n)

        if widx == 0:
            @blk.sync
            def _(sy):
                body(sy)
        else:
            @blk.gpsimd
            def _(g):
                body(g)


def emit_q(nc, t, st, ln_block, NORM0,
           onesrow_sb, onescolb_sb, hselQ_sb, rows16_sb, selhh_sb, BDN):
    LN_MR = st("LN_MR")

    XIW = st("XIW")
    xiw_base = XIW.n

    def isl_loader(g, cid, tb, dst, LD):
        s_isl = (cid % 4) // 2
        col0 = 512 * (2 * tb + s_isl)
        LD.inc(g.dma_start(
            out=dst[:].rearrange("p (c n) -> p c n", c=8),
            in_=t["xms"][:, col0:col0 + 512].rearrange("(c p) n -> p c n", p=128)), 16)

    mr3 = ln_block("Li_", 8, gp_loader=isl_loader, pre_wait=[(BDN, 16)])

    Q_W = st("Q_W"); Q_XL = st("Q_XL"); Q_CT = st("Q_CT")
    Q_BC = st("Q_BC"); Q_AP = st("Q_AP"); Q_QP = st("Q_QP")
    Q_QE = st("Q_QE"); Q_RO = st("Q_RO"); Q_QQ = st("Q_QQ")
    Q_NM = st("Q_NM"); Q_NE = st("Q_NE")
    Q_QW = st("Q_QW")
    Q_MKL = st("Q_MKL")
    Q_MS = st("Q_MS")
    Q_SEL = st("Q_SEL")
    Q_CQ = st("Q_CQ")
    Q_RM = st("Q_RM")
    Q_QL = st("Q_QL")
    es_ = contextlib.ExitStack()
    Qwq = es_.enter_context(nc.sbuf_tensor("Qwq", [128, 8 * 512], BF16))
    Qxa = es_.enter_context(nc.sbuf_tensor("Qxa", [128, 8 * 512], F32))
    Qxb = es_.enter_context(nc.sbuf_tensor("Qxb", [128, 8 * 512], F32))
    Qxn = es_.enter_context(nc.sbuf_tensor("Qxn", [128, 8 * 512], BF16))
    Qxsc = es_.enter_context(nc.sbuf_tensor("Qxsc", [128, 512], F32))
    Qraw = es_.enter_context(nc.sbuf_tensor("Qraw", [128, 2 * 512], BF16))
    Qrot = es_.enter_context(nc.sbuf_tensor("Qrot", [128, 2 * 512], BF16))
    Qu = es_.enter_context(nc.sbuf_tensor("Qu", [128, 512], BF16))
    Qsq = es_.enter_context(nc.sbuf_tensor("Qsq", [128, 2 * 1024], BF16))
    Qcs = es_.enter_context(nc.sbuf_tensor("Qcs", [128, 2 * 1024], BF16))
    Qsn = es_.enter_context(nc.sbuf_tensor("Qsn", [128, 2 * 1024], BF16))
    Qn2 = es_.enter_context(nc.sbuf_tensor("Qn2", [8, 4096], F32))
    Qtmp = es_.enter_context(nc.sbuf_tensor("Qtmp", [8, 4096], F32))
    Qmkb = es_.enter_context(nc.sbuf_tensor("Qmkb", [16, 8192], BF16))
    Qm2f = es_.enter_context(nc.sbuf_tensor("Qm2f", [16, 1], F32))
    Qm2b = es_.enter_context(nc.sbuf_tensor("Qm2b", [16, 1], BF16))
    Qmk = es_.enter_context(nc.sbuf_tensor("Qmk", [8, 1], F32))
    Qcqh = es_.enter_context(nc.sbuf_tensor("Qcqh", [8, 4096], BF16))
    Qcql = es_.enter_context(nc.sbuf_tensor("Qcql", [8, 4096], BF16))
    QpR = es_.enter_context(nc.psum_tensor("QpR", [128, 512], F32))
    QpM = es_.enter_context(nc.psum_tensor("QpM", [128, 512], F32))
    QpA = es_.enter_context(nc.psum_tensor("QpA", [128, 512], F32))
    QpB = es_.enter_context(nc.psum_tensor("QpB", [128, 512], F32))
    Qpn = es_.enter_context(nc.psum_tensor("Qpn", [8, 512], F32))
    Qpndbg = es_.enter_context(nc.sbuf_tensor("Qpndbg", [8, 1], F32))
    blk = es_.enter_context(nc.Block())
    Qxt = [Qxa, Qxb]

    XIW_T = st("XIW")
    XIW_ALL = XIW_T.n

    @blk.sync
    def _(sy):
        Q_MKL.inc(sy.dma_start(
            out=Qmkb[:, :], in_=t["KL"][NORM0:NORM0 + 16, :]), 16)
        pidq = sy.partition_id()
        _qb = dict(XL=Q_XL.n, CT=Q_CT.n, QW=Q_QW.n)
        for cidq in nc.Switch(engines=[sy], index=[pidq], n=8):
            Q_XL.n, Q_CT.n, Q_QW.n = _qb["XL"], _qb["CT"], _qb["QW"]
            s_islq = (cidq % 4) // 2
            for tb in range(8):
                if tb >= 2:
                    sy.wait_ge(Q_AP.h, (tb - 1) * 8)
                col0q = 512 * (2 * tb + s_islq)
                Q_XL.inc(sy.dma_start(
                    out=Qxt[tb % 2][:].rearrange("p (c n) -> p c n", c=8),
                    in_=t["xms"][:, col0q:col0q + 512
                                 ].rearrange("(c p) n -> p c n", p=128)), 16)
                if tb >= 2:
                    sy.wait_ge(Q_RO.h, (tb - 1) * 2)
                Q_CT.inc(sy.dma_start(
                    out=Qcs[:, (tb % 2) * 1024:(tb % 2 + 1) * 1024
                            ].rearrange("p (c n) -> p c n", c=2),
                    in_=t["csQ"][:, tb * 512:(tb + 1) * 512
                                 ].rearrange("(c p) n -> p c n", p=128)), 16)
                Q_CT.inc(sy.dma_start(
                    out=Qsn[:, (tb % 2) * 1024:(tb % 2 + 1) * 1024
                            ].rearrange("p (c n) -> p c n", c=2),
                    in_=t["snQ"][:, tb * 512:(tb + 1) * 512
                                 ].rearrange("(c p) n -> p c n", p=128)), 16)
                for pp in range(2):
                    idx = tb * 2 + pp
                    sy.wait_ge(Q_RO.h, idx + 1)
                    for half in range(2):
                        for sub in range(2):
                            sl = (2 * pp if half == 0 else 4 + 2 * pp) + sub
                            Q_QW.inc(sy.dma_start(
                                out=t["QL"][sl * HR:sl * HR + 64,
                                            tb * 512:(tb + 1) * 512],
                                in_=Qrot[sub * 64:(sub + 1) * 64,
                                         half * 512:(half + 1) * 512]), 16)
        sy.wait_ge(Q_CQ.h, 1)
        QLr = t["QL"].rearrange("(sl r) n -> sl r n", r=HR)
        Q_QL.inc(sy.dma_start(out=QLr[:, 64, :], in_=Qcqh[:, :]), 16)
        Q_QL.inc(sy.dma_start(out=QLr[:, 65, :], in_=Qcql[:, :]), 16)
        for k in range(3):
            Q_QL.inc(sy.dma_start(out=QLr[:, 66 + k, :],
                                  in_=t["qa3"][k * 8:(k + 1) * 8, :]), 16)
        for k in range(3):
            Q_QL.inc(sy.dma_start(out=QLr[:, 69 + k, :],
                                  in_=t["ones16"][0:8, 0:NISL]), 16)
        if DBG.get("qdbg"):
            QD = st("QD")
            sy.wait_ge(st("Q_MS").h, 4)
            sy.wait_ge(Q_CQ.h, 1)
            with nc.allow_non_contiguous_dma(reason="debug dump"):
                for r0, src in [(16, Qm2f[:, 0:1]), (32, Qmk[:, 0:1]),
                                (40, Qn2[:, :]), (48, Qtmp[:, :])]:
                    h_, w_ = src.shape
                    QD.inc(sy.dma_start(out=t["qdbg"][r0:r0 + h_, 0:w_], in_=src), 16)
            with nc.allow_non_contiguous_dma(reason="debug dump"):
                QD.inc(sy.dma_start(out=t["qdbg"][56:56 + 8, 0:1],
                                    in_=Qpndbg[:, 0:1]), 16)
            with nc.allow_non_contiguous_dma(reason="debug dump"):
                for r0, src in [(0, Qmkb[:, :]), (16, Qcqh[:, :]), (24, Qcql[:, :]),
                                (32, Qm2b[:, 0:1])]:
                    h_, w_ = src.shape
                    QD.inc(sy.dma_start(out=t["qdbh"][r0:r0 + h_, 0:w_], in_=src), 16)
            sy.wait_ge(QD.h, QD.n)

    @blk.gpsimd
    def _(g):
        g.wait_ge(BDN.h, 16)
        Q_W.inc(g.dma_start(
            out=Qwq[:].rearrange("p (c n) -> p c n", c=8),
            in_=t["wqTm"].rearrange("(c p) n -> p c n", p=128)), 16)

    @blk.tensor
    def _(te):
        def emit_qnorms(nb):
            te.wait_ge(Q_QQ.h, nb * 2 + 2)
            if nb >= 1:
                te.wait_ge(Q_NE.h, nb)
            mm = None
            for pp in range(2):
                for half in range(2):
                    oc = pp + 2 * half
                    mm = te.matmul(Qpn[:, :], hselQ_sb[:, oc * 8:(oc + 1) * 8],
                                   Qsq[:, pp * 1024 + half * 512:
                                       pp * 1024 + (half + 1) * 512],
                                   start=(pp == 0 and half == 0),
                                   stop=(pp == 1 and half == 1))
            Q_NM.inc(mm, 1)

        te.wait_ge(Q_W.h, 16)
        te.wait_ge(LN_MR.h, mr3)
        for tb in range(8):
            te.wait_ge(Q_XL.h, (tb + 1) * 16)
            if tb >= 1:
                te.wait_ge(Q_AP.h, tb * 8)
            te.matmul(QpR[:, :], onesrow_sb[0:1, :],
                      rows16_sb[0:1, tb * 512:(tb + 1) * 512], start=True, stop=True)
            Q_BC.inc(te.matmul(QpM[:, :], onesrow_sb[0:1, :],
                               rows16_sb[0:1, 4096 + tb * 512:4096 + (tb + 1) * 512],
                               start=True, stop=True), 1)
            if tb >= 1:
                emit_qnorms(tb - 1)
            for pp in range(2):
                idx = tb * 2 + pp
                if idx >= 1:
                    te.wait_ge(Q_QE.h, idx)
                for ic in range(8):
                    if pp == 0:
                        te.wait_ge(Q_AP.h, tb * 8 + ic + 1)
                    te.matmul(QpA[:, :],
                              Qwq[:, ic * 512 + pp * 128:ic * 512 + (pp + 1) * 128],
                              Qxn[:, ic * 512:(ic + 1) * 512],
                              start=(ic == 0), stop=(ic == 7))
                for ic in range(8):
                    mm = te.matmul(QpB[:, :],
                                   Qwq[:, ic * 512 + (pp + 2) * 128:ic * 512 + (pp + 3) * 128],
                                   Qxn[:, ic * 512:(ic + 1) * 512],
                                   start=(ic == 0), stop=(ic == 7))
                Q_QP.inc(mm, 1)
        emit_qnorms(7)
        te.wait_ge(Q_NE.h, 8)
        te.wait_ge(Q_MS.h, 1)
        Q_SEL.inc(te.matmul(Qpn[0:8, 0:1], selhh_sb[:, :], Qm2b[:, 0:1],
                            start=True, stop=True), 1)

    @blk.vector
    def _(ve):
        for tb in range(8):
            ve.wait_ge(Q_BC.h, tb + 1)
            if tb >= 1:
                ve.wait_ge(Q_QP.h, tb * 2)
            for ic in range(8):
                ve.tensor_mul(Qxsc[:, :],
                              Qxt[tb % 2][:, ic * 512:(ic + 1) * 512], QpR[:, :])
                Q_AP.inc(ve.tensor_sub(Qxn[:, ic * 512:(ic + 1) * 512],
                                       Qxsc[:, :], QpM[:, :]), 1)
            for pp in range(2):
                idx = tb * 2 + pp
                ve.wait_ge(Q_QE.h, idx + 1)
                ve.wait_ge(Q_CT.h, (tb + 1) * 32)
                if idx >= 1:
                    ve.wait_ge(Q_QW.h, idx * 64)
                    ve.wait_ge(Q_QQ.h, idx)
                cs = Qcs[:, (tb % 2) * 1024 + pp * 512:(tb % 2) * 1024 + (pp + 1) * 512]
                sn = Qsn[:, (tb % 2) * 1024 + pp * 512:(tb % 2) * 1024 + (pp + 1) * 512]
                ve.tensor_mul(Qu[:, :], Qraw[:, 512:1024], sn)
                ve.tensor_mul(Qrot[:, 0:512], Qraw[:, 0:512], cs)
                ve.tensor_sub(Qrot[:, 0:512], Qrot[:, 0:512], Qu[:, :])
                ve.tensor_mul(Qu[:, :], Qraw[:, 0:512], sn)
                ve.tensor_mul(Qrot[:, 512:1024], Qraw[:, 512:1024], cs)
                Q_RO.inc(ve.tensor_add(Qrot[:, 512:1024], Qrot[:, 512:1024],
                                       Qu[:, :]), 1)
        ve.wait_ge(Q_MKL.h, 16)
        Q_RM.inc(ve.reduce_max(Qm2f[:, 0:1], Qmkb[:, :],
                               axis=mybir.AxisListType.X), 1)
        ve.wait_ge(Q_MS.h, 2)
        ve.wait_ge(Q_NE.h, 8)
        ve.wait_ge(Q_MS.h, 3)
        ve.tensor_scalar(Qcqh[:, :] if False else Qtmp[:, :], Qtmp[:, :],
                         Qmk[:, 0:1], -1.0, ALU.mult, ALU.mult)
        Q_MS.inc(ve.tensor_copy(Qcqh[:, :], Qtmp[:, :]), 1)
        if DBG.get("qdbg"):
            Q_MS.inc(ve.tensor_copy(Qpndbg[:, 0:1], Qpn[0:8, 0:1]), 1)
        Q_CQ.inc(ve.tensor_sub(Qcql[:, :], Qtmp[:, :], Qcqh[:, :]), 1)

    @blk.scalar
    def _(s):
        for tb in range(8):
            for pp in range(2):
                idx = tb * 2 + pp
                s.wait_ge(Q_QP.h, idx + 1)
                if idx >= 1:
                    s.wait_ge(Q_RO.h, idx)
                s.activation(out=Qraw[:, 0:512], in_=QpA[:, :], func=AF.Copy)
                Q_QE.inc(s.activation(out=Qraw[:, 512:1024], in_=QpB[:, :],
                                      func=AF.Copy), 1)
                s.wait_ge(Q_RO.h, idx + 1)
                if idx >= 2:
                    s.wait_ge(Q_NM.h, tb)
                Q_QQ.inc(s.activation(out=Qsq[:, pp * 1024:(pp + 1) * 1024],
                                      in_=Qrot[:, :], func=AF.Square), 1)
            if tb >= 1:
                s.wait_ge(Q_NM.h, tb)
                Q_NE.inc(s.activation(out=Qn2[:, (tb - 1) * 512:tb * 512],
                                      in_=Qpn[:, :], func=AF.Copy), 1)
        s.wait_ge(Q_NM.h, 8)
        Q_NE.inc(s.activation(out=Qn2[:, 7 * 512:8 * 512],
                              in_=Qpn[:, :], func=AF.Copy), 1)
        s.wait_ge(Q_RM.h, 1)
        Q_MS.inc(s.activation(out=Qm2b[:, 0:1], in_=Qm2f[:, 0:1],
                              func=AF.Copy), 1)
        s.wait_ge(Q_SEL.h, 1)
        s.activation(out=Qmk[:, 0:1], in_=Qpn[0:8, 0:1], func=AF.Ln)
        Q_MS.inc(s.activation(out=Qmk[:, 0:1], in_=Qmk[:, 0:1],
                              func=AF.Exp, scale=0.5), 1)
        s.wait_ge(Q_NE.h, 8)
        s.activation(out=Qtmp[:, :], in_=Qn2[:, :], func=AF.Ln)
        Q_MS.inc(s.activation(out=Qtmp[:, :], in_=Qtmp[:, :],
                              func=AF.Exp, scale=0.5), 1)
    es_.close()
    return mr3


def emit_attn(nc, t, st, VLr, i128_sb):
    CST = st("CST")
    BDN = st("BDN")
    Q_QL = st("Q_QL"); Q_QW = st("Q_QW")
    AT_MP = st("AT_MP")
    AT_QS = st("AT_QS")
    AT_KB = st("AT_KB")
    AT_VB = st("AT_VB")
    AT_SC = st("AT_SC")
    AT_EX = st("AT_EX")
    AT_AV = st("AT_AV")
    AT_OE = st("AT_OE")
    AT_SU = st("AT_SU")
    AT_OW = st("AT_OW")

    chains = []
    gidx = 0
    for j in range(8):
        for sl in range(8):
            lo, hi = chain_quads(j, sl)
            kts = list(range(lo * 4, hi * 4))
            groups = []
            pos = 0
            while pos < len(kts):
                gsz = 4 if gidx % 2 == 0 else 3
                gsz = min(gsz, len(kts) - pos)
                groups.append((gidx, kts[pos:pos + gsz]))
                gidx += 1
                pos += gsz
            chunks = list(range(lo, hi, 2))
            chains.append(dict(j=j, sl=sl, lo=lo, hi=hi, groups=groups,
                               chunks=chunks))
    NG = gidx
    chunk_ord = {}
    ordn = 0
    for c, ch in enumerate(chains):
        for q0 in ch["chunks"]:
            chunk_ord[(c, q0)] = ordn
            ordn += 1
    NCHUNK = ordn

    def chunk_of_kt(ch, kt):
        q0 = ch["lo"] + ((kt // 4 - ch["lo"]) // 2) * 2
        return q0

    last_group_using_chunk = [0] * NCHUNK
    last_av_group_of_chunk = [0] * NCHUNK
    for c, ch in enumerate(chains):
        for gi, kts in ch["groups"]:
            for kt in kts:
                o = chunk_ord[(c, chunk_of_kt(ch, kt))]
                last_group_using_chunk[o] = max(last_group_using_chunk[o], gi)
                last_av_group_of_chunk[o] = max(last_av_group_of_chunk[o], gi)
    last_group_of_chain = [ch["groups"][-1][0] for ch in chains]

    es_ = contextlib.ExitStack()
    ATmp = es_.enter_context(nc.sbuf_tensor("ATmp", [128, 8 * 512], BF16))
    ATq0 = es_.enter_context(nc.sbuf_tensor("ATq0", [72, 512], BF16))
    ATq1 = es_.enter_context(nc.sbuf_tensor("ATq1", [72, 512], BF16))
    ATk0 = es_.enter_context(nc.sbuf_tensor("ATk0", [72, 1024], BF16))
    ATk1 = es_.enter_context(nc.sbuf_tensor("ATk1", [72, 1024], BF16))
    ATk2 = es_.enter_context(nc.sbuf_tensor("ATk2", [72, 1024], BF16))
    ATv0 = es_.enter_context(nc.sbuf_tensor("ATv0", [128, 8 * 65], BF16))
    ATv1 = es_.enter_context(nc.sbuf_tensor("ATv1", [128, 8 * 65], BF16))
    ATv2 = es_.enter_context(nc.sbuf_tensor("ATv2", [128, 8 * 65], BF16))
    ATp0 = es_.enter_context(nc.sbuf_tensor("ATp0", [128, 4 * 512], BF16))
    ATp1 = es_.enter_context(nc.sbuf_tensor("ATp1", [128, 3 * 512], BF16))
    ATo0 = es_.enter_context(nc.sbuf_tensor("ATo0", [64, 512], BF16))
    ATo1 = es_.enter_context(nc.sbuf_tensor("ATo1", [64, 512], BF16))
    ATs0 = es_.enter_context(nc.sbuf_tensor("ATs0", [1, 1024], BF16))
    ATs1 = es_.enter_context(nc.sbuf_tensor("ATs1", [1, 1024], BF16))
    ATsqA = es_.enter_context(nc.psum_tensor("ATsqA", [128, 4 * 512], F32))
    ATsqB = es_.enter_context(nc.psum_tensor("ATsqB", [128, 3 * 512], F32))
    ATav = es_.enter_context(nc.psum_tensor("ATav", [65, 512], F32))
    blk = es_.enter_context(nc.Block())
    ATq = [ATq0, ATq1]
    ATk = [ATk0, ATk1, ATk2]
    ATv = [ATv0, ATv1, ATv2]
    ATp = [ATp0, ATp1]
    ATsq = [ATsqA, ATsqB]
    ATo = [ATo0, ATo1]
    ATs = [ATs0, ATs1]

    @blk.sync
    def _(sy):
        AT_MP.inc(sy.dma_start(out=ATmp[:, 0:8 * 512], in_=t["maskP"][:]), 16)
        for c, ch in enumerate(chains):
            if c >= 2:
                sy.wait_ge(AT_SC.h, last_group_of_chain[c - 2] + 1)
            sy.wait_ge(Q_QL.h, 128)
            sy.wait_ge(Q_QW.h, 1024)
            AT_QS.inc(sy.dma_start(
                out=ATq[c % 2][:, :],
                in_=t["QL"][ch["sl"] * HR:(ch["sl"] + 1) * HR,
                            ch["j"] * 512:(ch["j"] + 1) * 512]), 16)

    @blk.gpsimd
    def _(g):
        g.wait_ge(BDN.h, 16)
        pid = g.partition_id()
        nkb, nvb, now = AT_KB.n, AT_VB.n, AT_OW.n
        for cid in nc.Switch(engines=[g], index=[pid], n=8):
            AT_KB.n, AT_VB.n, AT_OW.n = nkb, nvb, now
            hh = cid % 2
            for c, ch in enumerate(chains):
                sg = 8 * hh + ch["sl"]
                for q0 in ch["chunks"]:
                    o = chunk_ord[(c, q0)]
                    if o >= 3:
                        g.wait_ge(AT_SC.h, last_group_using_chunk[o - 3] + 1)
                        g.wait_ge(AT_AV.h, last_av_group_of_chunk[o - 3] + 1)
                    kw = min(1024, S - q0 * 512)
                    nblk = kw // 128
                    AT_KB.inc(g.dma_start(
                        out=ATk[o % 3][:, 0:kw],
                        in_=t["KL"][sg * HR:(sg + 1) * HR,
                                    q0 * 512:q0 * 512 + kw]), 16)
                    AT_VB.inc(g.dma_start(
                        out=ATv[o % 3][:, 0:nblk * 65
                                       ].rearrange("p (a c) -> p a c", a=nblk),
                        in_=VLr[q0 * 4:q0 * 4 + nblk, sg, :, :
                                ].rearrange("a k c -> k a c")), 16)
                if c >= 1:
                    emit_ol_writes(g, t, chains[c - 1], hh, c - 1,
                                   AT_OE, AT_SU, AT_OW, ATo, ATs)
            emit_ol_writes(g, t, chains[-1], hh, len(chains) - 1,
                           AT_OE, AT_SU, AT_OW, ATo, ATs)

    @blk.tensor
    def _(te):
        te.wait_ge(CST.h, 112)
        te.wait_ge(AT_MP.h, 16)
        te.wait_ge(AT_QS.h, 16)
        te.wait_ge(AT_KB.h, 16)
        te.wait_ge(AT_VB.h, 16)
        plan_g = []
        for c, ch in enumerate(chains):
            for k2, (gi, kts) in enumerate(ch["groups"]):
                plan_g.append((c, ch, gi, kts, k2 == 0,
                               gi == ch["groups"][-1][0]))

        def emit_av(item):
            c, ch, gi, kts, is_first, is_last = item
            te.wait_ge(AT_EX.h, gi + 1)
            if is_first:
                te.wait_ge(AT_OE.h, c)
                te.wait_ge(AT_SU.h, c)
            gslot = gi % 2
            for u, kt in enumerate(kts):
                o = chunk_ord[(c, chunk_of_kt(ch, kt))]
                te.wait_ge(AT_VB.h, (o + 1) * 16)
                klo = (kt - chunk_of_kt(ch, kt) * 4)
                mm = te.matmul(ATav[:, :],
                               ATv[o % 3][:, klo * 65:(klo + 1) * 65],
                               ATp[gslot][:, u * 512:(u + 1) * 512],
                               start=(is_first and u == 0),
                               stop=(is_last and u == len(kts) - 1))
            AT_AV.inc(mm, 1)

        pend = None
        for item in plan_g:
            c, ch, gi, kts, is_first, is_last = item
            if is_first:
                te.wait_ge(AT_QS.h, (c + 1) * 16)
            gslot = gi % 2
            if gi >= 2:
                te.wait_ge(AT_EX.h, gi - 1)
            for u, kt in enumerate(kts):
                o = chunk_ord[(c, chunk_of_kt(ch, kt))]
                te.wait_ge(AT_KB.h, (o + 1) * 16)
                ktg = kt
                masked = 8 * ch["j"] <= ktg < 8 * ch["j"] + 8
                mm = te.matmul(ATsq[gslot][:, u * 512:(u + 1) * 512],
                               ATk[o % 3][:, (kt - chunk_of_kt(ch, kt) * 4)
                                          * 128:(kt - chunk_of_kt(ch, kt) * 4 + 1) * 128],
                               ATq[c % 2][:, :],
                               start=True, stop=(not masked))
                if masked:
                    mm = te.matmul(ATsq[gslot][:, u * 512:(u + 1) * 512],
                                   i128_sb[:, :],
                                   ATmp[:, (ktg - 8 * ch["j"]) * 512:
                                        (ktg - 8 * ch["j"] + 1) * 512],
                                   start=False, stop=True)
            AT_SC.inc(mm, 1)
            if pend is not None:
                emit_av(pend)
            pend = item
        emit_av(pend)

    @blk.scalar
    def _(s):
        for c, ch in enumerate(chains):
            for gi, kts in ch["groups"]:
                gslot = gi % 2
                s.wait_ge(AT_SC.h, gi + 1)
                if gi >= 2:
                    s.wait_ge(AT_AV.h, gi - 1)
                AT_EX.inc(s.activation(
                    out=ATp[gslot][:, 0:len(kts) * 512],
                    in_=ATsq[gslot][:, 0:len(kts) * 512],
                    func=AF.Exp, scale=0.125), 1)

    @blk.vector
    def _(ve):
        for c, ch in enumerate(chains):
            ve.wait_ge(AT_AV.h, last_group_of_chain[c] + 1)
            if c >= 2:
                ve.wait_ge(AT_OW.h, (c - 1) * 48)
            AT_OE.inc(ve.tensor_copy(ATo[c % 2][:, :], ATav[0:64, :]), 1)
            ve.tensor_copy(ATs[c % 2][0:1, 0:512], ATav[64:65, :])
            AT_SU.inc(ve.tensor_sub(ATs[c % 2][0:1, 512:1024], ATav[64:65, :],
                                    ATs[c % 2][0:1, 0:512]), 1)


    es_.close()
def emit_ol_writes(g, t, ch, hh, c, AT_OE, AT_SU, AT_OW, ATo, ATs):
    sg = 8 * hh + ch["sl"]
    g.wait_ge(AT_OE.h, c + 1)
    g.wait_ge(AT_SU.h, c + 1)
    col = ch["j"] * 512
    AT_OW.inc(g.dma_start(out=t["OL"][sg * 64:(sg + 1) * 64, col:col + 512],
                          in_=ATo[c % 2][:, :]), 16)
    AT_OW.inc(g.dma_start(out=t["OL"][1024 + sg:1024 + sg + 1, col:col + 512],
                          in_=ATs[c % 2][0:1, 0:512]), 16)
    AT_OW.inc(g.dma_start(out=t["OL"][1040 + sg:1040 + sg + 1, col:col + 512],
                          in_=ATs[c % 2][0:1, 512:1024]), 16)


def emit_tail(nc, t, st, ln_block, ffn_block,
              onescol_sb, onescolb_sb, sel16_sb, BDN, WGU_CACHED):
    WGU = st("WGU")
    TL_OL = st("TL_OL")
    TL_WO = st("TL_WO")
    TL_RC = st("TL_RC")
    TL_XM = st("TL_XM")
    TL_RB = st("TL_RB")
    TL_ON = st("TL_ON")
    TL_PO = st("TL_PO")
    TL_XE = st("TL_XE")
    TL_X3 = st("TL_X3")
    TL_YS = st("TL_YS")

    es_ = contextlib.ExitStack()
    TLos = es_.enter_context(nc.sbuf_tensor("TLos", [128, 8 * 2048], BF16))
    TLsh = es_.enter_context(nc.sbuf_tensor("TLsh", [16, 2048], BF16))
    TLsl = es_.enter_context(nc.sbuf_tensor("TLsl", [16, 2048], BF16))
    TLrr = es_.enter_context(nc.sbuf_tensor("TLrr", [16, 2048], F32))
    TLrh = es_.enter_context(nc.sbuf_tensor("TLrh", [16, 2048], BF16))
    TLrl = es_.enter_context(nc.sbuf_tensor("TLrl", [16, 2048], BF16))
    TLwo = es_.enter_context(nc.sbuf_tensor("TLwo", [128, 8 * 1024], BF16))
    TLob = es_.enter_context(nc.sbuf_tensor("TLob", [128, 8], F32))
    TLxm0 = es_.enter_context(nc.sbuf_tensor("TLxm0", [128, 8 * 512], F32))
    TLxm1 = es_.enter_context(nc.sbuf_tensor("TLxm1", [128, 8 * 512], F32))
    TLon = es_.enter_context(nc.sbuf_tensor("TLon", [128, 8 * 512], BF16))
    TLpb0 = es_.enter_context(nc.psum_tensor("TLpb0", [128, 512], F32))
    TLpb1 = es_.enter_context(nc.psum_tensor("TLpb1", [128, 512], F32))
    TLpo0 = es_.enter_context(nc.psum_tensor("TLpo0", [128, 512], F32))
    TLpo1 = es_.enter_context(nc.psum_tensor("TLpo1", [128, 512], F32))
    blk = es_.enter_context(nc.Block())
    TLxm = [TLxm0, TLxm1]
    TLpb = [TLpb0, TLpb1]
    TLpo = [TLpo0, TLpo1]

    @blk.gpsimd
    def _(g):
        g.wait_ge(BDN.h, 32)
        TL_WO.inc(g.dma_start(
            out=TLwo[:].rearrange("p (c n) -> p c n", c=8),
            in_=t["woTs"].rearrange("(c p) n -> p c n", p=128)), 16)
        TL_WO.inc(g.dma_start(out=TLob[:, :], in_=t["outb"][:, :]), 16)
        pid = g.partition_id()
        nol, nxm = TL_OL.n, TL_XM.n
        for cid in nc.Switch(engines=[g], index=[pid], n=8):
            TL_OL.n, TL_XM.n = nol, nxm
            s_isl = (cid % 4) // 2
            hh = cid % 2
            c4 = s_isl + 2 * hh
            for m in range(4):
                icol = 512 * (2 * m + hh)
                TL_OL.inc(g.dma_start(
                    out=TLos[:].rearrange("p (c n) -> p c n", c=8)
                    [:, :, m * 512:(m + 1) * 512],
                    in_=t["OL"][0:1024, icol:icol + 512
                                ].rearrange("(c p) n -> p c n", p=128)), 16)
                TL_OL.inc(g.dma_start(
                    out=TLsh[:, m * 512:(m + 1) * 512],
                    in_=t["OL"][1024:1040, icol:icol + 512]), 16)
                TL_OL.inc(g.dma_start(
                    out=TLsl[:, m * 512:(m + 1) * 512],
                    in_=t["OL"][1040:1056, icol:icol + 512]), 16)
            for m in range(4):
                gcol = 512 * (4 * m + c4)
                if m >= 2:
                    g.wait_ge(TL_X3.h, (m - 1) * 16)
                TL_XM.inc(g.dma_start(
                    out=TLxm[m % 2][:].rearrange("p (c n) -> p c n", c=8),
                    in_=t["xms"][:, gcol:gcol + 512
                                 ].rearrange("(c p) n -> p c n", p=128)), 16)

    @blk.vector
    def _(ve):
        ve.wait_ge(TL_OL.h, 192)
        ve.tensor_add(TLrr[:, :], TLsh[:, :], TLsl[:, :])
        ve.reciprocal(TLrr[:, :], TLrr[:, :])
        ve.tensor_copy(TLrh[:, :], TLrr[:, :])
        TL_RC.inc(ve.tensor_sub(TLrl[:, :], TLrr[:, :], TLrh[:, :]), 1)
        for m in range(4):
            if m >= 1:
                ve.wait_ge(TL_PO.h, m * 8)
            for ic in range(8):
                i2 = m * 8 + ic
                ve.wait_ge(TL_RB.h, i2 + 1)
                TL_ON.inc(ve.tensor_mul(
                    TLon[:, ic * 512:(ic + 1) * 512],
                    TLos[:].rearrange("p (c n) -> p c n", c=8)
                    [:, ic, m * 512:(m + 1) * 512],
                    TLpb[i2 % 2][:, :]), 1)
            for oc in range(8):
                k = m * 8 + oc
                ve.wait_ge(TL_PO.h, k + 1)
                xv = TLxm[m % 2][:].rearrange("p (c n) -> p c n", c=8)[:, oc, :]
                TL_XE.inc(ve.scalar_tensor_tensor(
                    xv, TLpo[k % 2][:, :], TLob[:, oc:oc + 1], xv,
                    ALU.add, ALU.add), 1)

    @blk.tensor
    def _(te):
        te.wait_ge(TL_WO.h, 32)
        te.wait_ge(TL_RC.h, 1)
        for m in range(4):
            for ic in range(8):
                i2 = m * 8 + ic
                if i2 >= 2:
                    te.wait_ge(TL_ON.h, i2 - 1)
                te.matmul(TLpb[i2 % 2][:, :], sel16_sb[:, ic * 128:(ic + 1) * 128],
                          TLrh[:, m * 512:(m + 1) * 512], start=True, stop=False)
                TL_RB.inc(te.matmul(TLpb[i2 % 2][:, :],
                                    sel16_sb[:, ic * 128:(ic + 1) * 128],
                                    TLrl[:, m * 512:(m + 1) * 512],
                                    start=False, stop=True), 1)
            te.wait_ge(TL_XM.h, (m + 1) * 16)
            for oc in range(8):
                k = m * 8 + oc
                if k >= 2:
                    te.wait_ge(TL_XE.h, k - 1)
                for ic in range(8):
                    te.wait_ge(TL_ON.h, m * 8 + ic + 1)
                    mm = te.matmul(TLpo[k % 2][:, :],
                                   TLwo[:, ic * 1024 + oc * 128:ic * 1024 + (oc + 1) * 128],
                                   TLon[:, ic * 512:(ic + 1) * 512],
                                   start=(ic == 0), stop=(ic == 7))
                TL_PO.inc(mm, 1)

    @blk.sync
    def _(sy):
        for m in range(4):
            sy.wait_ge(TL_XE.h, (m + 1) * 8)
            TL_X3.inc(sy.dma_start(
                out=t["x3d"][:, m * 512:(m + 1) * 512
                             ].rearrange("(c p) n -> p c n", p=128),
                in_=TLxm[m % 2][:].rearrange("p (c n) -> p c n", c=8)), 16)

    es_.close()
    def x3_cols(tb):
        return t["x3d"][:, tb * 512:(tb + 1) * 512].rearrange("(c p) n -> p c n", p=128)

    mr4 = ln_block("L3_", 4, xsrc=x3_cols, pre_wait=[(st("TL_X3"), 64)])

    W2 = st("W2")
    es_ = contextlib.ExitStack()
    wg2_sb = es_.enter_context(nc.sbuf_tensor("wg2_sb", [128, 8 * HID], BF16))
    wu2_sb = es_.enter_context(nc.sbuf_tensor("wu2_sb", [128, 8 * HID], BF16))
    with nc.Block() as blk:
        @blk.sync
        def _(sy):
            sy.wait_ge(WGU.h, WGU_CACHED)
            W2.inc(sy.dma_start(out=wg2_sb[:, :], in_=t["wg_bf"][:, :]), 16)
            W2.inc(sy.dma_start(out=wu2_sb[:, :], in_=t["wu_bf"][:, :]), 16)

    F_XS2 = st("F_XS")

    def y_sink(g, cid, tb, buf):
        F_XS2.inc(g.dma_start(
            out=t["y"][:, tb * 512:(tb + 1) * 512
                       ].rearrange("(c p) n -> p c n", p=128),
            in_=buf[:].rearrange("p (c n) -> p c n", c=8)), 16)

    ffn_block("F2_", 4, x3_cols, wg2_sb, wu2_sb, y_sink, False, mr4,
              extra_pe_wait=[(W2, 32)],
              sync_pre_wait=[(st("TL_X3"), 64)])


    es_.close()
def _triple_split(v):
    a = v.astype(BF).astype(np.float64)
    b = (v - a).astype(BF).astype(np.float64)
    c = (v - a - b).astype(BF)
    return a.astype(BF), b.astype(BF), c

def _rope_tables(pos, nfreq=512):
    freqs = 1.0 / 10000.0 ** (np.arange(nfreq, dtype=np.float64) / 512.0)
    ang = pos[None, :] * freqs[:, None]
    return np.cos(ang).astype(BF), np.sin(ang).astype(BF)


def _host_inputs(L0_attn_x, wq, wk, wv, wo, ob, wg, wu, wd, nonce):
    xs = np.asarray(L0_attn_x)
    wgT = np.ascontiguousarray(np.asarray(wg).T)
    wuT = np.ascontiguousarray(np.asarray(wu).T)
    wdT = np.ascontiguousarray(np.asarray(wd).T)
    wkT = np.ascontiguousarray(np.asarray(wk).T)
    wvT = np.ascontiguousarray(np.asarray(wv).T)
    wqT = np.asarray(wq).T
    woT = np.asarray(wo).T
    ob = np.ascontiguousarray(
        np.asarray(ob).reshape(8, 128).T).astype(np.float32)

    perm = np.zeros(D, dtype=np.int64)
    for sg in range(16):
        h = head_of_slot(sg)
        perm[sg * 64:(sg + 1) * 64] = np.arange(h * 64, (h + 1) * 64)
    woTs = np.ascontiguousarray(woT[perm, :]).astype(np.float32)

    ones16 = np.ones((16, S), BF)
    i128 = np.eye(128, dtype=BF)
    onescol = np.ones((1, 128), np.float32)
    onescolf = np.ones((128, 1), np.float32)
    onescolb = np.ones((128, 1), BF)

    hselK = np.zeros((128, 8 * 16), BF)
    for oc in range(8):
        for r in range(128):
            h = (128 * oc + r) // 64
            hselK[r, oc * 16 + slot_of_head(h)] = 1.0
    sel16 = np.zeros((16, 8 * 128), BF)
    for ic in range(8):
        for p in range(128):
            sg = 2 * ic + p // 64
            sel16[sg, ic * 128 + p] = 1.0

    fconst = np.zeros((1, 16), np.float32)
    fconst[0, 0] = EPS
    nonce_arr = np.zeros((1, 16), np.uint32)
    nonce_arr[0, 0] = np.uint32(nonce + 1)
    nonce_arr[0, 1] = np.uint32(nonce + 2)

    ins = []
    for core in range(8):
        b = core // 4
        s_isl = (core % 4) // 2
        hh = core % 2
        c4 = s_isl + 2 * hh
        x_half = np.ascontiguousarray(xs[b, hh * NTOK:(hh + 1) * NTOK, :].T
                                      ).astype(np.float32)
        kpos = np.arange(hh * NTOK, (hh + 1) * NTOK, dtype=np.float64)
        ipos = (512.0 * (2 * (np.arange(NISL) // 512) + s_isl)
                + (np.arange(NISL) % 512)).astype(np.float64)

        csK, snK = _rope_tables(kpos)
        myh1 = [hh, hh + 2, hh + 4, hh + 6]
        cols = []
        for h in myh1:
            cols.extend(range(64 * h, 64 * h + 64))
        for h in myh1:
            cols.extend(range(64 * (h + 8), 64 * (h + 8) + 64))
        wqTm = np.ascontiguousarray(wqT[:, cols]).astype(np.float32)
        freqs_idx = np.array(cols[:256])
        fr = 1.0 / 10000.0 ** (freqs_idx.astype(np.float64) / 512.0)
        angQ = ipos[None, :] * fr[:, None]
        csQ = np.cos(angQ).astype(BF)
        snQ = np.sin(angQ).astype(BF)

        a3K = np.zeros((48, NTOK), BF)
        for sg in range(16):
            h = head_of_slot(sg)
            v = 8.0 * SLOPES[h] * kpos
            aa, bb, cc = _triple_split(v)
            a3K[sg] = aa
            a3K[16 + sg] = bb
            a3K[32 + sg] = cc
        qa3 = np.zeros((24, NISL), BF)
        for sl in range(8):
            h = 2 * sl + hh
            v = -8.0 * SLOPES[h] * ipos
            aa, bb, cc = _triple_split(v)
            qa3[sl] = aa
            qa3[8 + sl] = bb
            qa3[16 + sl] = cc

        hselQ = np.zeros((128, 32), BF)
        for oc in range(4):
            for r in range(128):
                if oc < 2:
                    sl = 2 * oc + r // 64
                else:
                    sl = 4 + 2 * (oc - 2) + r // 64
                hselQ[r, oc * 8 + sl] = 1.0

        selhh = np.zeros((16, 8), BF)
        for c in range(8):
            selhh[8 * hh + c, c] = 1.0

        maskP = np.zeros((128, 8 * 512), BF)
        pq = np.arange(512)
        for slot in range(8):
            pp = np.arange(128)[:, None]
            m = (128 * slot + pp) > (512 * s_isl + pq[None, :])
            maskP[:, slot * 512:(slot + 1) * 512] = np.where(m, NEGBIG, 0.0).astype(BF)

        ins.append(dict(
            x=x_half, wgT=wgT, wuT=wuT, wdT=wdT, wkT=wkT, wvT=wvT,
            wqTm=wqTm, woTs=woTs, outb=ob,
            csK=np.ascontiguousarray(csK), snK=np.ascontiguousarray(snK),
            csQ=np.ascontiguousarray(csQ), snQ=np.ascontiguousarray(snQ),
            a3K=a3K, qa3=qa3, ones16=ones16, maskP=maskP,
            hselK=hselK, hselQ=hselQ, sel16=sel16, selhh=selhh, i128=i128,
            onescol=onescol, onescolf=onescolf, onescolb=onescolb,
            nonce=nonce_arr, fconst=fconst))
    return ins


_NC_CACHE = {}


def kernel(L0_attn_x, L0_attn_wq, L0_attn_wk, L0_attn_wv, L0_attn_out_w,
           L0_attn_out_b, L0_ffn_w_gate, L0_ffn_w_up, L0_ffn_w_down):
    import time
    nonce = (int(time.time_ns() // 100) % (1 << 28)) * 4
    ins = _host_inputs(L0_attn_x, L0_attn_wq, L0_attn_wk, L0_attn_wv,
                       L0_attn_out_w, L0_attn_out_b,
                       L0_ffn_w_gate, L0_ffn_w_up, L0_ffn_w_down, nonce)
    if "nc" not in _NC_CACHE:
        _NC_CACHE["nc"] = build_program()
    nc, _ = _NC_CACHE["nc"]
    res = run_bass_kernel_spmd(nc, ins, core_ids=list(range(8)))
    out = np.zeros((B, S, D), np.float32)
    for core in range(8):
        b = core // 4
        s_isl = (core % 4) // 2
        hh = core % 2
        c4 = s_isl + 2 * hh
        y = res.results[core]["y"]
        for m in range(4):
            g0 = 512 * (4 * m + c4)
            out[b, g0:g0 + 512, :] = y[:, m * 512:(m + 1) * 512].T
    return out


# version 08
# speedup vs baseline: 1.4424x; 1.4424x over previous
import math
import contextlib
import numpy as np
import ml_dtypes
import concourse.bass as bass
import concourse.mybir as mybir
from concourse.bass_utils import run_bass_kernel_spmd

F32 = mybir.dt.float32
BF16 = mybir.dt.bfloat16
I32 = mybir.dt.int32
U32 = mybir.dt.uint32
AF = mybir.ActivationFunctionType
ALU = mybir.AluOpType
BF = ml_dtypes.bfloat16

B, S, D = 2, 8192, 1024
H, DH = 16, 64
HID = 3072
NTOK = 4096
NOWN = 2048
NISL = 4096
HR = 72
DEQ = 1.0 / 127.0
FFN_SC = 0.5 * DEQ * DEQ
EPS = 1e-5
NEGBIG = -1.0e9
SLOPES = [2.0 ** (-8.0 * (i + 1) / H) for i in range(H)]


def slot_of_head(h):
    return 8 * (h % 2) + h // 2


def head_of_slot(sg):
    return 2 * (sg % 8) + sg // 8


T_BAND = 64.0
WINQ = []
for _sl in range(8):
    _w = T_BAND / SLOPES[2 * _sl + 1]
    WINQ.append(None if _w >= S else int(math.ceil(_w / 512.0)))


def chain_quads(j, sl):
    hi = 2 * j + 2
    lo = 0 if WINQ[sl] is None else max(0, 2 * j - WINQ[sl])
    return lo, hi


class Stream:
    def __init__(self, handle):
        self.h = handle
        self.n = 0

    def inc(self, ins, k):
        ins.then_inc(self.h, k)
        self.n += k
        return self.n


DBG = {"kv_vl": True, "kv_kl": True}


def build_program(dumps=(), stop_after=None):
    nc = bass.Bass("TRN2", target_bir_lowering=False)
    t = {}

    def din(name, shape, dt):
        t[name] = nc.dram_tensor(name, shape, dt, kind="ExternalInput").ap()

    def dout(name, shape, dt):
        t[name] = nc.dram_tensor(name, shape, dt, kind="ExternalOutput").ap()

    def dint(name, shape, dt, shared=False):
        t[name] = nc.dram_tensor(name, shape, dt, kind="Internal",
                                 addr_space="Shared" if shared else "Local").ap()

    din("x", [D, NTOK], F32)
    din("wgT", [D, HID], I32)
    din("wuT", [D, HID], I32)
    din("wdT", [HID, D], I32)
    din("wkT", [D, D], F32)
    din("wvT", [D, D], F32)
    din("wqTm", [D, 512], F32)
    din("woTs", [D, D], F32)
    din("outb", [128, 8], F32)
    din("csK", [512, NTOK], BF16)
    din("snK", [512, NTOK], BF16)
    din("csQ", [256, NISL], BF16)
    din("snQ", [256, NISL], BF16)
    din("a3K", [3 * 16, NTOK], BF16)
    din("qa3", [3 * 8, NISL], BF16)
    din("ones16", [16, S], BF16)
    din("maskP", [128, 8 * 512], BF16)
    din("hselK", [128, 8 * 16], BF16)
    din("hselQ", [128, 4 * 8], BF16)
    din("sel16", [16, 8 * 128], BF16)
    din("selhh", [16, 8], BF16)
    din("i128", [128, 128], BF16)
    din("onescol", [1, 128], F32)
    din("onescolf", [128, 1], F32)
    din("onescolb", [128, 1], BF16)
    din("nonce", [1, 16], U32)
    din("fconst", [1, 16], F32)
    dout("y", [D, NOWN], F32)

    dint("wg_bf", [128, 8 * HID], BF16)
    dint("wu_bf", [128, 8 * HID], BF16)
    dint("wd_bf", [8 * 128, 24 * 128], BF16)
    dint("QL", [8 * HR, NISL], BF16)
    dint("x3d", [D, NOWN], F32)
    dint("xmp", [D, NTOK], F32)
    dint("xip", [D, NISL], F32)

    dint("xms", [D, S], F32, shared=True)
    dint("KL", [H * HR + 16, S], BF16, shared=True)
    dint("VL", [64 * 16 * 128, 65], BF16, shared=True)
    dint("OL", [16 * 64 + 32, NISL], BF16, shared=True)
    dint("flags", [2, 16], U32, shared=True)
    dint("qdbg", [64, 8192], F32)
    dint("qdbh", [48, 8192], BF16)

    VLr = t["VL"].rearrange("(kc k s) c -> kc k s c", k=128, s=16)
    t["kvdbg"] = nc.dram_tensor("kvdbg", [128, 1040], BF16, kind="Internal").ap()
    KLs = t["KL"].rearrange("(s r) n -> s r n", r=HR) if False else None
    NORM0 = H * HR

    for nm in dumps:
        shp = {"xms": ([D, S], F32), "KL": ([H * HR + 16, S], BF16),
               "qdbg": ([64, 8192], F32), "qdbh": ([48, 8192], BF16),
               "VL": ([64 * 16 * 128, 65], BF16), "QL": ([8 * HR, NISL], BF16),
               "OL": ([16 * 64 + 32, NISL], BF16), "x3d": ([D, NOWN], F32)}[nm]
        dout("d_" + nm, *shp)

    est = contextlib.ExitStack()
    _streams = {}

    def st(name):
        if name not in _streams:
            _streams[name] = Stream(est.enter_context(nc.semaphore(name)))
        return _streams[name]

    sbp = lambda n, shp, dt: est.enter_context(nc.sbuf_tensor(n, shp, dt))

    def emit_dumps():
        if not dumps:
            return
        DU = st("DU")
        with nc.Block() as blk2:
            @blk2.gpsimd
            def _(g):
                for snm, s_ in list(_streams.items()):
                    if snm != "DU" and s_.n > 0:
                        g.wait_ge(s_.h, s_.n)
                for nm in dumps:
                    DU.inc(g.dma_start(out=t["d_" + nm][:], in_=t[nm][:]), 16)
                g.wait_ge(DU.h, DU.n)

    onescol_sb = sbp("onescol_sb", [1, 128], F32)
    onescolf_sb = sbp("onescolf_sb", [128, 1], F32)
    onescolb_sb = sbp("onescolb_sb", [128, 1], BF16)
    hselK_sb = sbp("hselK_sb", [128, 8 * 16], BF16)
    hselQ_sb = sbp("hselQ_sb", [128, 4 * 8], BF16)
    sel16_sb = sbp("sel16_sb", [16, 8 * 128], BF16)
    selhh_sb = sbp("selhh_sb", [16, 8], BF16)
    i128_sb = sbp("i128_sb", [128, 128], BF16)
    nonce_sb = sbp("nonce_sb", [1, 16], U32)
    fconst_sb = sbp("fconst_sb", [1, 16], F32)
    flag_sb = sbp("flag_sb", [1, 16], U32)
    onesrow_sb = sbp("onesrow_sb", [1, 128], BF16)
    roww_sb = sbp("roww_sb", [1, 2048], F32)
    rows16_sb = sbp("rows16_sb", [1, 8192], BF16)

    CST = st("CST")
    WCH = st("WCH")
    BDN = st("BDN")

    es_ = contextlib.ExitStack()
    wdtmp = es_.enter_context(nc.sbuf_tensor("wdtmp", [128, 24 * D], BF16))
    blk = es_.enter_context(nc.Block())
    @blk.sync
    def _(sy):
        for dst, src in [(onescol_sb, t["onescol"][:]),
                         (onescolf_sb, t["onescolf"][:]),
                         (onescolb_sb, t["onescolb"][:]),
                         (hselK_sb, t["hselK"][:]),
                         (hselQ_sb, t["hselQ"][:]),
                         (sel16_sb, t["sel16"][:]),
                         (i128_sb, t["i128"][:]),
                         (nonce_sb, t["nonce"][:]),
                         (fconst_sb, t["fconst"][:]),
                         (onesrow_sb, t["ones16"][0:1, 0:128]),
                         (selhh_sb, t["selhh"][:])]:
            CST.inc(sy.dma_start(out=dst[:], in_=src), 16)

    @blk.gpsimd
    def _(g):
        WCH.inc(g.dma_start(out=wdtmp[:].rearrange("p (c n) -> p c n", c=24),
                            in_=t["wdT"].rearrange("(c p) n -> p c n", p=128)), 16)
        g.wait_ge(WCH.h, WCH.n)
        wtr = wdtmp[:].rearrange("p (h n) -> p h n", h=24)
        for dc in range(8):
            WCH.inc(g.dma_start(
                out=t["wd_bf"][dc * 128:(dc + 1) * 128, :
                               ].rearrange("p (h n) -> p h n", h=24),
                in_=wtr[:, :, dc * 128:(dc + 1) * 128]), 16)
        g.wait_ge(WCH.h, WCH.n)
    es_.close()
    CST_ALL = CST.n
    WD_CACHED = WCH.n

    LN_LD = st("LN_LD"); LN_SQ = st("LN_SQ"); LN_MM = st("LN_MM")
    LN_EV = st("LN_EV"); LN_VAR = st("LN_VAR"); LN_R = st("LN_R")
    LN_MR = st("LN_MR"); LN_CB = st("LN_CB")

    def ln_block(pfx, n_tb, xsrc=None, gp_loader=None, pre_wait=()):
        b_ld, b_sq, b_mm, b_ev = LN_LD.n, LN_SQ.n, LN_MM.n, LN_EV.n
        b_var, b_r, b_mr = LN_VAR.n, LN_R.n, LN_MR.n
        b_cb = LN_CB.n
        es_ = contextlib.ExitStack()
        xa = es_.enter_context(nc.sbuf_tensor(pfx + "xa", [128, 8 * 512], F32))
        xb = es_.enter_context(nc.sbuf_tensor(pfx + "xb", [128, 8 * 512], F32))
        xb16 = es_.enter_context(nc.sbuf_tensor(pfx + "xb16", [128, 8 * 512], BF16))
        sq = es_.enter_context(nc.sbuf_tensor(pfx + "sq", [128, 8 * 512], BF16))
        p1 = es_.enter_context(nc.psum_tensor(pfx + "p1", [1, 512], F32))
        p2 = es_.enter_context(nc.psum_tensor(pfx + "p2", [1, 512], F32))
        blk = es_.enter_context(nc.Block())
        xt = [xa, xb]

        def loads(eng, cid=None):
            for ww in pre_wait:
                eng.wait_ge(ww[0].h, ww[1])
            for tb in range(n_tb):
                if tb >= 2:
                    eng.wait_ge(LN_MM.h, b_mm + tb - 1)
                if gp_loader is None:
                    LN_LD.inc(eng.dma_start(
                        out=xt[tb % 2][:].rearrange("p (c n) -> p c n", c=8),
                        in_=xsrc(tb)), 16)
                else:
                    gp_loader(eng, cid, tb, xt[tb % 2], LN_LD)

        if gp_loader is None:
            @blk.sync
            def _(sy):
                loads(sy)
        else:
            @blk.sync
            def _(sy):
                pid = sy.partition_id()
                n0 = LN_LD.n
                for cid in nc.Switch(engines=[sy], index=[pid], n=8):
                    LN_LD.n = n0
                    loads(sy, cid)

        @blk.scalar
        def _(s):
            for tb in range(n_tb):
                s.wait_ge(LN_CB.h, b_cb + tb + 1)
                if tb >= 1:
                    s.wait_ge(LN_MM.h, b_mm + tb)
                LN_SQ.inc(s.activation(out=sq[:], in_=xb16[:],
                                       func=AF.Square), 1)
                s.wait_ge(LN_VAR.h, b_var + tb + 1)
                s.activation(out=roww_sb[0:1, 1024:1536], in_=roww_sb[0:1, 512:1024],
                             func=AF.Ln, bias=fconst_sb[0:1, 0:1])
                LN_R.inc(s.activation(out=roww_sb[0:1, 1024:1536],
                                      in_=roww_sb[0:1, 1024:1536],
                                      func=AF.Exp, scale=-0.5), 1)

        @blk.tensor
        def _(te):
            te.wait_ge(CST.h, CST_ALL)
            for tb in range(n_tb):
                te.wait_ge(LN_SQ.h, b_sq + tb + 1)
                if tb >= 1:
                    te.wait_ge(LN_EV.h, b_ev + tb)
                for ic in range(8):
                    te.matmul(p1[0:1, :], onescolb_sb[:, 0:1],
                              xb16[:, ic * 512:(ic + 1) * 512],
                              start=(ic == 0), stop=(ic == 7))
                for ic in range(8):
                    mm = te.matmul(p2[0:1, :], onescolb_sb[:, 0:1],
                                   sq[:, ic * 512:(ic + 1) * 512],
                                   start=(ic == 0), stop=(ic == 7))
                LN_MM.inc(mm, 1)

        @blk.vector
        def _(ve):
            for tb in range(n_tb):
                ve.wait_ge(LN_LD.h, b_ld + (tb + 1) * 16)
                LN_CB.inc(ve.tensor_copy(xb16[:, :], xt[tb % 2][:, :]), 1)
                ve.wait_ge(LN_MM.h, b_mm + tb + 1)
                if tb >= 1:
                    ve.wait_ge(LN_R.h, b_r + tb)
                ve.tensor_copy(roww_sb[0:1, 0:512], p1[0:1, :])
                LN_EV.inc(ve.tensor_copy(roww_sb[0:1, 512:1024], p2[0:1, :]), 1)
                ve.tensor_scalar_mul(roww_sb[0:1, 0:512], roww_sb[0:1, 0:512],
                                     1.0 / 1024.0)
                ve.tensor_scalar_mul(roww_sb[0:1, 512:1024], roww_sb[0:1, 512:1024],
                                     1.0 / 1024.0)
                ve.tensor_mul(roww_sb[0:1, 1024:1536], roww_sb[0:1, 0:512],
                              roww_sb[0:1, 0:512])
                LN_VAR.inc(ve.tensor_sub(roww_sb[0:1, 512:1024],
                                         roww_sb[0:1, 512:1024],
                                         roww_sb[0:1, 1024:1536]), 1)
                ve.wait_ge(LN_R.h, b_r + tb + 1)
                ve.tensor_copy(rows16_sb[0:1, tb * 512:(tb + 1) * 512],
                               roww_sb[0:1, 1024:1536])
                LN_MR.inc(ve.tensor_mul(
                    rows16_sb[0:1, 4096 + tb * 512:4096 + (tb + 1) * 512],
                    roww_sb[0:1, 0:512], roww_sb[0:1, 1024:1536]), 1)

        es_.close()
        return LN_MR.n

    F_XL = st("F_XL"); F_WD = st("F_WD"); F_BC = st("F_BC"); F_AP = st("F_AP")
    F_GU = st("F_GU"); F_SI = st("F_SI"); F_MU = st("F_MU"); F_DM = st("F_DM")
    F_XE = st("F_XE"); F_XS = st("F_XS")

    def ffn_block(pfx, n_tb, xsrc, wg_sb, wu_sb, sink, sink_switch,
                  mr_count, extra_pe_wait=(), sync_pre_wait=(), xs_per_tb=16):
        b_xl, b_wd, b_bc, b_ap = F_XL.n, F_WD.n, F_BC.n, F_AP.n
        b_gu, b_si, b_mu, b_dm = F_GU.n, F_SI.n, F_MU.n, F_DM.n
        b_xe, b_xs = F_XE.n, F_XS.n
        es_ = contextlib.ExitStack()
        xa = es_.enter_context(nc.sbuf_tensor(pfx + "xa", [128, 8 * 512], F32))
        xb = es_.enter_context(nc.sbuf_tensor(pfx + "xb", [128, 8 * 512], F32))
        xn = es_.enter_context(nc.sbuf_tensor(pfx + "xn", [128, 8 * 512], BF16))
        xsc = es_.enter_context(nc.sbuf_tensor(pfx + "xsc", [128, 512], F32))
        sg0 = es_.enter_context(nc.sbuf_tensor(pfx + "sg0", [128, 512], BF16))
        sg1 = es_.enter_context(nc.sbuf_tensor(pfx + "sg1", [128, 512], BF16))
        ha = es_.enter_context(nc.sbuf_tensor(pfx + "ha", [128, 24 * 512], BF16))
        wdb = [es_.enter_context(nc.sbuf_tensor(pfx + "wd%d" % i,
                                                 [128, 24 * 128], BF16))
               for i in range(2)]
        pR = es_.enter_context(nc.psum_tensor(pfx + "pR", [128, 512], F32))
        pM = es_.enter_context(nc.psum_tensor(pfx + "pM", [128, 512], F32))
        pg0 = es_.enter_context(nc.psum_tensor(pfx + "pg0", [128, 512], F32))
        pg1 = es_.enter_context(nc.psum_tensor(pfx + "pg1", [128, 512], F32))
        pu0 = es_.enter_context(nc.psum_tensor(pfx + "pu0", [128, 512], F32))
        pu1 = es_.enter_context(nc.psum_tensor(pfx + "pu1", [128, 512], F32))
        pd0 = es_.enter_context(nc.psum_tensor(pfx + "pd0", [128, 512], F32))
        pd1 = es_.enter_context(nc.psum_tensor(pfx + "pd1", [128, 512], F32))
        blk = es_.enter_context(nc.Block())
        xt = [xa, xb]
        sg = [sg0, sg1]
        pg = [pg0, pg1]
        pu = [pu0, pu1]
        pd = [pd0, pd1]

        @blk.sync
        def _(sy):
            sy.wait_ge(WCH.h, WD_CACHED)
            for ww in sync_pre_wait:
                sy.wait_ge(ww[0].h, ww[1])
            for tb in range(n_tb):
                if tb >= 2:
                    sy.wait_ge(F_XS.h, b_xs + (tb - 1) * xs_per_tb)
                F_XL.inc(sy.dma_start(
                    out=xt[tb % 2][:].rearrange("p (c n) -> p c n", c=8),
                    in_=xsrc(tb)), 16)
                for dc in range(8):
                    k = tb * 8 + dc
                    if k >= 2:
                        sy.wait_ge(F_DM.h, b_dm + k - 1)
                    F_WD.inc(sy.dma_start(
                        out=wdb[k % 2][:, :],
                        in_=t["wd_bf"][dc * 128:(dc + 1) * 128, :]), 16)

        @blk.tensor
        def _(te):
            te.wait_ge(CST.h, CST_ALL)
            te.wait_ge(LN_MR.h, mr_count)
            for ww in extra_pe_wait:
                te.wait_ge(ww[0].h, ww[1])
            for tb in range(n_tb):
                nb = tb * 24
                te.wait_ge(F_XL.h, b_xl + (tb + 1) * 16)
                if tb >= 1:
                    te.wait_ge(F_AP.h, b_ap + tb * 8)
                te.matmul(pR[:, :], onesrow_sb[0:1, :],
                          rows16_sb[0:1, tb * 512:(tb + 1) * 512],
                          start=True, stop=True)
                F_BC.inc(te.matmul(pM[:, :], onesrow_sb[0:1, :],
                                   rows16_sb[0:1, 4096 + tb * 512:4096 + (tb + 1) * 512],
                                   start=True, stop=True), 1)
                for h in range(24):
                    hp = h % 2
                    if nb + h >= 2:
                        te.wait_ge(F_SI.h, b_si + nb + h - 1)
                        te.wait_ge(F_MU.h, b_mu + nb + h - 1)
                    for ic in range(8):
                        if h == 0:
                            te.wait_ge(F_AP.h, b_ap + tb * 8 + ic + 1)
                        te.matmul(pg[hp][:, :],
                                  wg_sb[:, ic * HID + h * 128:ic * HID + (h + 1) * 128],
                                  xn[:, ic * 512:(ic + 1) * 512],
                                  start=(ic == 0), stop=(ic == 7))
                    for ic in range(8):
                        mm = te.matmul(pu[hp][:, :],
                                       wu_sb[:, ic * HID + h * 128:ic * HID + (h + 1) * 128],
                                       xn[:, ic * 512:(ic + 1) * 512],
                                       start=(ic == 0), stop=(ic == 7))
                    F_GU.inc(mm, 1)
                te.wait_ge(F_MU.h, b_mu + nb + 24)
                for dc in range(8):
                    k = tb * 8 + dc
                    te.wait_ge(F_WD.h, b_wd + (k + 1) * 16)
                    if k >= 2:
                        te.wait_ge(F_XE.h, b_xe + k - 1)
                    for h in range(24):
                        mm = te.matmul(pd[k % 2][:, :],
                                       wdb[k % 2][:, h * 128:(h + 1) * 128],
                                       ha[:, h * 512:(h + 1) * 512],
                                       start=(h == 0), stop=(h == 23))
                    F_DM.inc(mm, 1)

        @blk.vector
        def _(ve):
            for tb in range(n_tb):
                nb = tb * 24
                ve.wait_ge(F_BC.h, b_bc + tb + 1)
                if tb >= 1:
                    ve.wait_ge(F_GU.h, b_gu + nb)
                for ic in range(8):
                    ve.tensor_mul(xsc[:, :],
                                  xt[tb % 2][:, ic * 512:(ic + 1) * 512], pR[:, :])
                    F_AP.inc(ve.tensor_sub(xn[:, ic * 512:(ic + 1) * 512],
                                           xsc[:, :], pM[:, :]), 1)
                for h in range(24):
                    ve.wait_ge(F_SI.h, b_si + nb + h + 1)
                    if tb >= 1 and h == 0:
                        ve.wait_ge(F_DM.h, b_dm + tb * 8)
                    F_MU.inc(ve.tensor_mul(ha[:, h * 512:(h + 1) * 512],
                                           sg[h % 2][:, :], pu[h % 2][:, :]), 1)
                for dc in range(8):
                    k = tb * 8 + dc
                    ve.wait_ge(F_DM.h, b_dm + k + 1)
                    xv = xt[tb % 2][:].rearrange("p (c n) -> p c n", c=8)[:, dc, :]
                    F_XE.inc(ve.scalar_tensor_tensor(
                        xv, pd[k % 2][:, :], FFN_SC, xv, ALU.mult, ALU.add), 1)

        @blk.scalar
        def _(s):
            for tb in range(n_tb):
                nb = tb * 24
                for h in range(24):
                    s.wait_ge(F_GU.h, b_gu + nb + h + 1)
                    if nb + h >= 2:
                        s.wait_ge(F_MU.h, b_mu + nb + h - 1)
                    F_SI.inc(s.activation(out=sg[h % 2][:, :], in_=pg[h % 2][:, :],
                                          func=AF.Silu, scale=DEQ), 1)

        @blk.gpsimd
        def _(g):
            if sink_switch:
                pid = g.partition_id()
                n0 = F_XS.n
                for cid in nc.Switch(engines=[g], index=[pid], n=8):
                    F_XS.n = n0
                    for tb in range(n_tb):
                        g.wait_ge(F_XE.h, b_xe + (tb + 1) * 8)
                        sink(g, cid, tb, xt[tb % 2])
            else:
                for tb in range(n_tb):
                    g.wait_ge(F_XE.h, b_xe + (tb + 1) * 8)
                    sink(g, None, tb, xt[tb % 2])
            g.wait_ge(F_XS.h, F_XS.n)

        es_.close()
    def x_cols(tb):
        return t["x"][:, tb * 512:(tb + 1) * 512].rearrange("(c p) n -> p c n", p=128)

    if stop_after == "b0":
        emit_dumps()
        est.close()
        return nc, t
    mr1 = ln_block("L1_", 8, xsrc=x_cols)
    if stop_after == "ln1":
        emit_dumps()
        est.close()
        return nc, t

    WGU = st("WGU")
    es_ = contextlib.ExitStack()
    wgT_sb = es_.enter_context(nc.sbuf_tensor("wgT_sb", [128, 8 * HID], BF16))
    wuT_sb = es_.enter_context(nc.sbuf_tensor("wuT_sb", [128, 8 * HID], BF16))
    with nc.Block() as blk:
        @blk.gpsimd
        def _(g):
            WGU.inc(g.dma_start(
                out=wgT_sb[:].rearrange("p (c n) -> p c n", c=8),
                in_=t["wgT"].rearrange("(c p) n -> p c n", p=128)), 16)
            WGU.inc(g.dma_start(
                out=wuT_sb[:].rearrange("p (c n) -> p c n", c=8),
                in_=t["wuT"].rearrange("(c p) n -> p c n", p=128)), 16)
            g.wait_ge(WGU.h, WGU.n)
            WGU.inc(g.dma_start(out=t["wg_bf"][:, :], in_=wgT_sb[:, :]), 16)
            WGU.inc(g.dma_start(out=t["wu_bf"][:, :], in_=wuT_sb[:, :]), 16)

    def ffn1_sink(g, cid, tb, buf):
        hh = cid % 2
        F_XS.inc(g.dma_start(
            out=t["xms"][:, hh * NTOK + tb * 512:hh * NTOK + (tb + 1) * 512
                         ].rearrange("(c p) n -> p c n", p=128),
            in_=buf[:].rearrange("p (c n) -> p c n", c=8)), 16)

    ffn_block("F1_", 8, x_cols, wgT_sb, wuT_sb, ffn1_sink, True, mr1,
              extra_pe_wait=[(WGU, 32)], xs_per_tb=16)
    es_.close()
    F1_XS_ALL = F_XS.n
    WGU_CACHED = WGU.n
    if stop_after == "ffn1":
        emit_dumps()
        est.close()
        return nc, t

    def ln2_loader(eng, cid, tb, dst, LD):
        hh = cid % 2
        col0 = hh * NTOK + tb * 512
        LD.inc(eng.dma_start(
            out=dst[:].rearrange("p (c n) -> p c n", c=8),
            in_=t["xms"][:, col0:col0 + 512
                         ].rearrange("(c p) n -> p c n", p=128)), 16)

    mr2 = ln_block("L2_", 8, gp_loader=ln2_loader, pre_wait=[(F_XS, F1_XS_ALL)])
    if stop_after == "ln2":
        emit_dumps()
        est.close()
        return nc, t

    emit_kv(nc, t, st, VLr, NORM0, mr2,
            onesrow_sb, onescolb_sb, hselK_sb, rows16_sb)

    if stop_after == "kv":
        emit_dumps()
        est.close()
        return nc, t
    kvdone = [(st("KV_WR"), st("KV_WR").n), (st("KV_KLW"), st("KV_KLW").n),
              (st("KV_KNW"), st("KV_KNW").n), (st("KV_VLW"), st("KV_VLW").n),
              (st("F_XS"), F_XS.n)]
    emit_barrier(nc, t, st, nonce_sb, flag_sb, 0, kvdone, BDN)

    if stop_after == "bar1":
        emit_dumps()
        est.close()
        return nc, t
    mr3 = emit_q(nc, t, st, ln_block, NORM0,
                 onesrow_sb, onescolb_sb, hselQ_sb, rows16_sb, selhh_sb, BDN)

    if stop_after == "q":
        emit_dumps()
        est.close()
        return nc, t
    emit_attn(nc, t, st, VLr, i128_sb)

    if stop_after == "attn":
        emit_dumps()
        est.close()
        return nc, t
    emit_barrier(nc, t, st, nonce_sb, flag_sb, 1,
                 [(st("AT_OW"), st("AT_OW").n)], BDN)

    if stop_after == "bar2":
        emit_dumps()
        est.close()
        return nc, t
    emit_tail(nc, t, st, ln_block, ffn_block,
              onescol_sb, onescolb_sb, sel16_sb, BDN, WGU_CACHED)

    emit_dumps()

    est.close()
    return nc, t


def emit_kv(nc, t, st, VLr, NORM0, mr2,
            onesrow_sb, onescolb_sb, hselK_sb, rows16_sb):
    LN_MR = st("LN_MR")
    KV_W = st("KV_W"); KV_XL = st("KV_XL"); KV_CT = st("KV_CT")
    KV_BC = st("KV_BC"); KV_AP = st("KV_AP"); KV_KP = st("KV_KP")
    KV_KE = st("KV_KE"); KV_RO = st("KV_RO"); KV_KQ = st("KV_KQ")
    KV_NM = st("KV_NM"); KV_NE = st("KV_NE"); KV_VM = st("KV_VM")
    KV_VE = st("KV_VE"); KV_VO = st("KV_VO")
    KV_KLW = st("KV_KLW"); KV_KNW = st("KV_KNW"); KV_VLW = st("KV_VLW")
    KV_WR = st("KV_WR")
    es = contextlib.ExitStack()
    wkT_sb = es.enter_context(nc.sbuf_tensor("wkT_sb", [128, 8 * 1024], BF16))
    wvT_sb = es.enter_context(nc.sbuf_tensor("wvT_sb", [128, 8 * 1024], BF16))
    KVxa = es.enter_context(nc.sbuf_tensor("KVxa", [128, 8 * 512], F32))
    KVxb = es.enter_context(nc.sbuf_tensor("KVxb", [128, 8 * 512], F32))
    KVxn = es.enter_context(nc.sbuf_tensor("KVxn", [128, 8 * 512], BF16))
    KVxsc = es.enter_context(nc.sbuf_tensor("KVxsc", [128, 512], F32))
    KVkraw = es.enter_context(nc.sbuf_tensor("KVkraw", [128, 2 * 512], BF16))
    KVkrot = es.enter_context(nc.sbuf_tensor("KVkrot", [128, 2 * 512], BF16))
    KVu = es.enter_context(nc.sbuf_tensor("KVu", [128, 512], BF16))
    KVksq = es.enter_context(nc.sbuf_tensor("KVksq", [128, 4 * 1024], BF16))
    KVcs = es.enter_context(nc.sbuf_tensor("KVcs", [128, 2 * 2048], BF16))
    KVsn = es.enter_context(nc.sbuf_tensor("KVsn", [128, 2 * 2048], BF16))
    KVvs0 = es.enter_context(nc.sbuf_tensor("KVvs0", [128, 1040], BF16))
    KVvs1 = es.enter_context(nc.sbuf_tensor("KVvs1", [128, 1040], BF16))
    KVkn = es.enter_context(nc.sbuf_tensor("KVkn", [16, 512], BF16))
    KVpR = es.enter_context(nc.psum_tensor("KVpR", [128, 512], F32))
    KVpM = es.enter_context(nc.psum_tensor("KVpM", [128, 512], F32))
    KVpkA = es.enter_context(nc.psum_tensor("KVpkA", [128, 512], F32))
    KVpkB = es.enter_context(nc.psum_tensor("KVpkB", [128, 512], F32))
    KVpvA = es.enter_context(nc.psum_tensor("KVpvA", [128, 512], F32))
    KVpvB = es.enter_context(nc.psum_tensor("KVpvB", [128, 512], F32))
    KVpn = es.enter_context(nc.psum_tensor("KVpn", [16, 512], F32))
    blk = es.enter_context(nc.Block())
    KVxt = [KVxa, KVxb]
    KVvs = [KVvs0, KVvs1]

    @blk.sync
    def _(sy):
        for tb in range(DBG.get("kv_tbs", 8)):
            if tb >= 2:
                sy.wait_ge(KV_RO.h, (tb - 1) * 4)
            if DBG.get("kv_ct", True):
                KV_CT.inc(sy.dma_start(
                    out=KVcs[:, (tb % 2) * 2048:(tb % 2 + 1) * 2048
                             ].rearrange("p (c n) -> p c n", c=4),
                    in_=t["csK"][:, tb * 512:(tb + 1) * 512
                                 ].rearrange("(c p) n -> p c n", p=128)), 16)
                KV_CT.inc(sy.dma_start(
                    out=KVsn[:, (tb % 2) * 2048:(tb % 2 + 1) * 2048
                             ].rearrange("p (c n) -> p c n", c=4),
                    in_=t["snK"][:, tb * 512:(tb + 1) * 512
                                 ].rearrange("(c p) n -> p c n", p=128)), 16)
            else:
                KV_CT.inc(sy.dma_start(
                    out=KVcs[:, (tb % 2) * 2048:(tb % 2) * 2048 + 512],
                    in_=t["csK"][0:128, 0:512]), 16)
                KV_CT.inc(sy.dma_start(
                    out=KVsn[:, (tb % 2) * 2048:(tb % 2) * 2048 + 512],
                    in_=t["snK"][0:128, 0:512]), 16)

    @blk.gpsimd
    def _(g):
        KV_W.inc(g.dma_start(
            out=wkT_sb[:].rearrange("p (c n) -> p c n", c=8),
            in_=t["wkT"].rearrange("(c p) n -> p c n", p=128)), 16)
        KV_W.inc(g.dma_start(
            out=wvT_sb[:].rearrange("p (c n) -> p c n", c=8),
            in_=t["wvT"].rearrange("(c p) n -> p c n", p=128)), 16)
        pid = g.partition_id()
        base = dict(XL=KV_XL.n, KLW=KV_KLW.n, KNW=KV_KNW.n,
                    VLW=KV_VLW.n, WR=KV_WR.n)
        for cid in nc.Switch(engines=[g], index=[pid], n=8):
            KV_XL.n, KV_KLW.n = base["XL"], base["KLW"]
            KV_KNW.n, KV_VLW.n, KV_WR.n = base["KNW"], base["VLW"], base["WR"]
            hh = cid % 2
            col0 = hh * NTOK
            KLh = t["KL"][0:H * HR, :].rearrange("(s r) n -> s r n", r=HR)
            for k in range(5):
                KV_WR.inc(g.dma_start(
                    out=KLh[0:16, 64 + k, col0:col0 + NTOK],
                    in_=t["ones16"][:, 0:NTOK]), 16)
            for k in range(3):
                KV_WR.inc(g.dma_start(
                    out=KLh[0:16, 69 + k, col0:col0 + NTOK],
                    in_=t["a3K"][k * 16:(k + 1) * 16, :]), 16)
            for tb in range(DBG.get("kv_tbs", 8)):
                if tb >= 2:
                    g.wait_ge(KV_AP.h, (tb - 1) * 8)
                if DBG.get("kv_xl", True):
                    KV_XL.inc(g.dma_start(
                        out=KVxt[tb % 2][:].rearrange("p (c n) -> p c n", c=8),
                        in_=t["xms"][:, col0 + tb * 512:col0 + (tb + 1) * 512
                                     ].rearrange("(c p) n -> p c n", p=128)), 16)
                else:
                    KV_XL.inc(g.dma_start(
                        out=KVxt[tb % 2][:, :],
                        in_=t["xms"][0:128, 0:4096]), 16)
                for ocp in range(4):
                    idx = tb * 4 + ocp
                    g.wait_ge(KV_RO.h, idx + 1)
                    for half in range(2):
                        for sub in range(2):
                            h = 2 * ocp + sub + 8 * half
                            sg_ = 8 * (h % 2) + h // 2
                            if DBG["kv_kl"]:
                                KV_KLW.inc(g.dma_start(
                                    out=t["KL"][sg_ * HR + 0:sg_ * HR + 64,
                                                col0 + tb * 512:col0 + (tb + 1) * 512],
                                    in_=KVkrot[sub * 64:(sub + 1) * 64,
                                               half * 512:(half + 1) * 512]), 16)
                            else:
                                KV_KLW.inc(g.dma_start(
                                    out=t["kvdbg"][sub * 64:(sub + 1) * 64, 0:512],
                                    in_=KVkrot[sub * 64:(sub + 1) * 64,
                                               half * 512:(half + 1) * 512]), 16)
                if tb >= 1:
                    g.wait_ge(KV_NE.h, tb)
                    KV_KNW.inc(g.dma_start(
                        out=t["KL"][NORM0:NORM0 + 16,
                                    col0 + (tb - 1) * 512:col0 + tb * 512],
                        in_=KVkn[:, :]), 16)
                for tc in range(4):
                    idx = tb * 4 + tc
                    g.wait_ge(KV_VE.h, idx + 1)
                    g.wait_ge(KV_VO.h, 2)
                    kc = hh * 32 + tb * 4 + tc
                    if DBG["kv_vl"]:
                        KV_VLW.inc(g.dma_start(
                            out=VLr[kc, :, :, :],
                            in_=KVvs[tc % 2][:].rearrange("p (s c) -> p s c", s=16)), 16)
                    else:
                        KV_VLW.inc(g.dma_start(
                            out=t["kvdbg"][:, :], in_=KVvs[tc % 2][:, :]), 16)
            NTBg = DBG.get("kv_tbs", 8)
            if NTBg >= 1:
                g.wait_ge(KV_NE.h, NTBg)
                KV_KNW.inc(g.dma_start(
                    out=t["KL"][NORM0:NORM0 + 16,
                                col0 + (NTBg - 1) * 512:col0 + NTBg * 512],
                    in_=KVkn[:, :]), 16)

    @blk.tensor
    def _(te):
        def emit_norms(nb):
            te.wait_ge(KV_KQ.h, nb * 4 + 4)
            if nb >= 1:
                te.wait_ge(KV_NE.h, nb)
            mm = None
            for ocp in range(4):
                for half in range(2):
                    oc = ocp + 4 * half
                    mm = te.matmul(KVpn[:, :], hselK_sb[:, oc * 16:(oc + 1) * 16],
                                   KVksq[:, ocp * 1024 + half * 512:
                                         ocp * 1024 + (half + 1) * 512],
                                   start=(ocp == 0 and half == 0),
                                   stop=(ocp == 3 and half == 1))
            KV_NM.inc(mm, 1)

        te.wait_ge(KV_W.h, 32)
        te.wait_ge(LN_MR.h, mr2)
        NTB_KV = DBG.get("kv_tbs", 8)
        for tb in range(NTB_KV):
            te.wait_ge(KV_XL.h, (tb + 1) * 16)
            if tb >= 1:
                te.wait_ge(KV_AP.h, tb * 8)
            te.matmul(KVpR[:, :], onesrow_sb[0:1, :],
                      rows16_sb[0:1, tb * 512:(tb + 1) * 512], start=True, stop=True)
            KV_BC.inc(te.matmul(KVpM[:, :], onesrow_sb[0:1, :],
                                rows16_sb[0:1, 4096 + tb * 512:4096 + (tb + 1) * 512],
                                start=True, stop=True), 1)
            if tb >= 1:
                emit_norms(tb - 1)
            for ocp in range(4):
                idx = tb * 4 + ocp
                if idx >= 1:
                    te.wait_ge(KV_KE.h, idx)
                nic = 8 if DBG.get("kv_kproj", True) else 1
                for ic in range(nic):
                    if ocp == 0:
                        te.wait_ge(KV_AP.h, tb * 8 + ic + 1)
                    te.matmul(KVpkA[:, :],
                              wkT_sb[:, ic * 1024 + ocp * 128:ic * 1024 + (ocp + 1) * 128],
                              KVxn[:, ic * 512:(ic + 1) * 512],
                              start=(ic == 0), stop=(ic == nic - 1))
                for ic in range(nic):
                    mm = te.matmul(KVpkB[:, :],
                                   wkT_sb[:, ic * 1024 + (ocp + 4) * 128:ic * 1024 + (ocp + 5) * 128],
                                   KVxn[:, ic * 512:(ic + 1) * 512],
                                   start=(ic == 0), stop=(ic == nic - 1))
                KV_KP.inc(mm, 1)
            for tc in range(4):
                idx = tb * 4 + tc
                if idx >= 1:
                    te.wait_ge(KV_VE.h, idx)
                niv = 8 if DBG.get("kv_vproj", True) else 1
                for ic in range(niv):
                    te.matmul(KVpvA[:, :],
                              KVxn[:, ic * 512 + tc * 128:ic * 512 + (tc + 1) * 128],
                              wvT_sb[:, ic * 1024:ic * 1024 + 512],
                              start=(ic == 0), stop=(ic == niv - 1))
                for ic in range(niv):
                    mm = te.matmul(KVpvB[:, :],
                                   KVxn[:, ic * 512 + tc * 128:ic * 512 + (tc + 1) * 128],
                                   wvT_sb[:, ic * 1024 + 512:ic * 1024 + 1024],
                                   start=(ic == 0), stop=(ic == niv - 1))
                KV_VM.inc(mm, 1)
        if NTB_KV >= 1:
            emit_norms(NTB_KV - 1)

    @blk.vector
    def _(ve):
        KV_VO.inc(ve.memset(
            KVvs0[:].rearrange("p (s c) -> p s c", s=16)[:, :, 64:65], 1.0), 1)
        KV_VO.inc(ve.memset(
            KVvs1[:].rearrange("p (s c) -> p s c", s=16)[:, :, 64:65], 1.0), 1)
        for tb in range(DBG.get("kv_tbs", 8)):
            ve.wait_ge(KV_BC.h, tb + 1)
            if tb >= 1:
                ve.wait_ge(KV_KP.h, tb * 4)
                ve.wait_ge(KV_VM.h, tb * 4)
            for ic in range(8):
                ve.tensor_mul(KVxsc[:, :],
                              KVxt[tb % 2][:, ic * 512:(ic + 1) * 512], KVpR[:, :])
                KV_AP.inc(ve.tensor_sub(KVxn[:, ic * 512:(ic + 1) * 512],
                                        KVxsc[:, :], KVpM[:, :]), 1)
            ve.wait_ge(KV_CT.h, (tb + 1) * 32)
            for ocp in range(4):
                idx = tb * 4 + ocp
                ve.wait_ge(KV_KE.h, idx + 1)
                if idx >= 1:
                    ve.wait_ge(KV_KLW.h, idx * 64)
                    ve.wait_ge(KV_KQ.h, idx)
                cs = KVcs[:, (tb % 2) * 2048 + ocp * 512:(tb % 2) * 2048 + (ocp + 1) * 512]
                sn = KVsn[:, (tb % 2) * 2048 + ocp * 512:(tb % 2) * 2048 + (ocp + 1) * 512]
                if DBG.get("kv_rope", True):
                    ve.tensor_mul(KVu[:, :], KVkraw[:, 512:1024], sn)
                    ve.tensor_mul(KVkrot[:, 0:512], KVkraw[:, 0:512], cs)
                    ve.tensor_sub(KVkrot[:, 0:512], KVkrot[:, 0:512], KVu[:, :])
                    ve.tensor_mul(KVu[:, :], KVkraw[:, 0:512], sn)
                    ve.tensor_mul(KVkrot[:, 512:1024], KVkraw[:, 512:1024], cs)
                    KV_RO.inc(ve.tensor_add(KVkrot[:, 512:1024], KVkrot[:, 512:1024],
                                            KVu[:, :]), 1)
                else:
                    KV_RO.inc(ve.tensor_copy(KVkrot[:, :], KVkraw[:, :]), 1)

    @blk.scalar
    def _(s):
        for tb in range(DBG.get("kv_tbs", 8)):
            for ocp in range(4):
                idx = tb * 4 + ocp
                s.wait_ge(KV_KP.h, idx + 1)
                if idx >= 1:
                    s.wait_ge(KV_RO.h, idx)
                s.activation(out=KVkraw[:, 0:512], in_=KVpkA[:, :], func=AF.Copy)
                KV_KE.inc(s.activation(out=KVkraw[:, 512:1024], in_=KVpkB[:, :],
                                       func=AF.Copy), 1)
                s.wait_ge(KV_RO.h, idx + 1)
                if idx >= 4:
                    s.wait_ge(KV_NM.h, tb)
                if DBG.get("kv_sq", True):
                    KV_KQ.inc(s.activation(out=KVksq[:, ocp * 1024:(ocp + 1) * 1024],
                                           in_=KVkrot[:, :], func=AF.Square), 1)
                else:
                    KV_KQ.inc(s.activation(out=KVksq[:, ocp * 1024:(ocp + 1) * 1024],
                                           in_=KVkrot[:, :], func=AF.Copy), 1)
            if tb >= 1:
                s.wait_ge(KV_NM.h, tb)
                if tb >= 2:
                    s.wait_ge(KV_KNW.h, (tb - 1) * 16)
                KV_NE.inc(s.activation(out=KVkn[:, :], in_=KVpn[:, :],
                                       func=AF.Copy), 1)
            for tc in range(4):
                idx = tb * 4 + tc
                s.wait_ge(KV_VM.h, idx + 1)
                if idx >= 2:
                    s.wait_ge(KV_VLW.h, (idx - 1) * 16)
                if DBG.get("kv_vevac", True):
                    for gi, (pv, lohi, s0) in enumerate([
                            (KVpvA, 0, 0), (KVpvA, 1, 8),
                            (KVpvB, 0, 4), (KVpvB, 1, 12)]):
                        cp = s.activation(
                            out=KVvs[tc % 2][:, s0 * 65:(s0 + 4) * 65
                                             ].rearrange("p (s c) -> p s c", s=4)[:, :, 0:64],
                            in_=pv[:, :].rearrange("p (g e) -> p g e", g=4)
                            [:, :, lohi * 64:(lohi + 1) * 64],
                            func=AF.Copy)
                    KV_VE.inc(cp, 1)
                else:
                    s.activation(out=KVvs[tc % 2][:, 0:512], in_=KVpvA[:, :],
                                 func=AF.Copy)
                    KV_VE.inc(s.activation(out=KVvs[tc % 2][:, 512:1024],
                                           in_=KVpvB[:, :], func=AF.Copy), 1)
        NTB_KV2 = DBG.get("kv_tbs", 8)
        if NTB_KV2 >= 1:
            s.wait_ge(KV_NM.h, NTB_KV2)
            if NTB_KV2 >= 2:
                s.wait_ge(KV_KNW.h, (NTB_KV2 - 1) * 16)
            KV_NE.inc(s.activation(out=KVkn[:, :], in_=KVpn[:, :],
                                   func=AF.Copy), 1)


    es.close()
def emit_barrier(nc, t, st, nonce_sb, flag_sb, widx, wait_list, BDN):
    FW = st("BFW%d" % widx)
    PLm = st("BPL%d" % widx)
    b_pl = 0
    with nc.Block() as blk:
        def body(g):
            for s_, cnt in wait_list:
                g.wait_ge(s_.h, cnt)
            pid = g.partition_id()
            nfw = FW.n
            for cid in nc.Switch(engines=[g], index=[pid], n=8):
                FW.n = nfw
                hh = cid % 2
                FW.inc(g.dma_start(out=t["flags"][hh:hh + 1, widx:widx + 1],
                                   in_=nonce_sb[0:1, widx:widx + 1]), 16)
            g.wait_ge(FW.h, FW.n)
            with g.Fori(0, 1000000) as it:
                for cid in nc.Switch(engines=[g], index=[pid], n=8):
                    other = 1 - (cid % 2)
                    g.dma_start(out=flag_sb[0:1, 8:9],
                                in_=t["flags"][other:other + 1, widx:widx + 1]
                                ).then_inc(PLm.h, 16)
                g.wait_ge(PLm.h, (it + 1) * 16 + b_pl)
                val = nc.values_load(flag_sb[0:1, 8:9], engines=[g.engine],
                                     min_val=0, max_val=(1 << 31) - 1,
                                     skip_runtime_bounds_check=True)
                tgt = nc.values_load(nonce_sb[0:1, widx:widx + 1], engines=[g.engine],
                                     min_val=0, max_val=(1 << 31) - 1,
                                     skip_runtime_bounds_check=True)
                with g.If(val == tgt):
                    g.br("bar%d_done" % widx)
            nc.switch_bb("bar%d_done" % widx)
            PLm.n = b_pl
            BDN.inc(g.dma_start(out=flag_sb[0:1, 12:16], in_=nonce_sb[0:1, 0:4]), 16)
            g.wait_ge(BDN.h, BDN.n)

        if widx == 0:
            @blk.sync
            def _(sy):
                body(sy)
        else:
            @blk.gpsimd
            def _(g):
                body(g)


def emit_q(nc, t, st, ln_block, NORM0,
           onesrow_sb, onescolb_sb, hselQ_sb, rows16_sb, selhh_sb, BDN):
    LN_MR = st("LN_MR")

    XIW = st("XIW")
    xiw_base = XIW.n

    def isl_loader(g, cid, tb, dst, LD):
        s_isl = (cid % 4) // 2
        col0 = 512 * (2 * tb + s_isl)
        LD.inc(g.dma_start(
            out=dst[:].rearrange("p (c n) -> p c n", c=8),
            in_=t["xms"][:, col0:col0 + 512].rearrange("(c p) n -> p c n", p=128)), 16)

    mr3 = ln_block("Li_", 8, gp_loader=isl_loader, pre_wait=[(BDN, 16)])

    Q_W = st("Q_W"); Q_XL = st("Q_XL"); Q_CT = st("Q_CT")
    Q_BC = st("Q_BC"); Q_AP = st("Q_AP"); Q_QP = st("Q_QP")
    Q_QE = st("Q_QE"); Q_RO = st("Q_RO"); Q_QQ = st("Q_QQ")
    Q_NM = st("Q_NM"); Q_NE = st("Q_NE")
    Q_QW = st("Q_QW")
    Q_MKL = st("Q_MKL")
    Q_MS = st("Q_MS")
    Q_SEL = st("Q_SEL")
    Q_CQ = st("Q_CQ")
    Q_RM = st("Q_RM")
    Q_QL = st("Q_QL")
    es_ = contextlib.ExitStack()
    Qwq = es_.enter_context(nc.sbuf_tensor("Qwq", [128, 8 * 512], BF16))
    Qxa = es_.enter_context(nc.sbuf_tensor("Qxa", [128, 8 * 512], F32))
    Qxb = es_.enter_context(nc.sbuf_tensor("Qxb", [128, 8 * 512], F32))
    Qxn = es_.enter_context(nc.sbuf_tensor("Qxn", [128, 8 * 512], BF16))
    Qxsc = es_.enter_context(nc.sbuf_tensor("Qxsc", [128, 512], F32))
    Qraw = es_.enter_context(nc.sbuf_tensor("Qraw", [128, 2 * 512], BF16))
    Qrot = es_.enter_context(nc.sbuf_tensor("Qrot", [128, 2 * 512], BF16))
    Qu = es_.enter_context(nc.sbuf_tensor("Qu", [128, 512], BF16))
    Qsq = es_.enter_context(nc.sbuf_tensor("Qsq", [128, 2 * 1024], BF16))
    Qcs = es_.enter_context(nc.sbuf_tensor("Qcs", [128, 2 * 1024], BF16))
    Qsn = es_.enter_context(nc.sbuf_tensor("Qsn", [128, 2 * 1024], BF16))
    Qn2 = es_.enter_context(nc.sbuf_tensor("Qn2", [8, 4096], F32))
    Qtmp = es_.enter_context(nc.sbuf_tensor("Qtmp", [8, 4096], F32))
    Qmkb = es_.enter_context(nc.sbuf_tensor("Qmkb", [16, 8192], BF16))
    Qm2f = es_.enter_context(nc.sbuf_tensor("Qm2f", [16, 1], F32))
    Qm2b = es_.enter_context(nc.sbuf_tensor("Qm2b", [16, 1], BF16))
    Qmk = es_.enter_context(nc.sbuf_tensor("Qmk", [8, 1], F32))
    Qcqh = es_.enter_context(nc.sbuf_tensor("Qcqh", [8, 4096], BF16))
    Qcql = es_.enter_context(nc.sbuf_tensor("Qcql", [8, 4096], BF16))
    QpR = es_.enter_context(nc.psum_tensor("QpR", [128, 512], F32))
    QpM = es_.enter_context(nc.psum_tensor("QpM", [128, 512], F32))
    QpA = es_.enter_context(nc.psum_tensor("QpA", [128, 512], F32))
    QpB = es_.enter_context(nc.psum_tensor("QpB", [128, 512], F32))
    Qpn = es_.enter_context(nc.psum_tensor("Qpn", [8, 512], F32))
    Qpndbg = es_.enter_context(nc.sbuf_tensor("Qpndbg", [8, 1], F32))
    blk = es_.enter_context(nc.Block())
    Qxt = [Qxa, Qxb]

    XIW_T = st("XIW")
    XIW_ALL = XIW_T.n

    @blk.sync
    def _(sy):
        Q_MKL.inc(sy.dma_start(
            out=Qmkb[:, :], in_=t["KL"][NORM0:NORM0 + 16, :]), 16)
        pidq = sy.partition_id()
        _qb = dict(XL=Q_XL.n, CT=Q_CT.n, QW=Q_QW.n)
        for cidq in nc.Switch(engines=[sy], index=[pidq], n=8):
            Q_XL.n, Q_CT.n, Q_QW.n = _qb["XL"], _qb["CT"], _qb["QW"]
            s_islq = (cidq % 4) // 2
            for tb in range(8):
                if tb >= 2:
                    sy.wait_ge(Q_AP.h, (tb - 1) * 8)
                col0q = 512 * (2 * tb + s_islq)
                Q_XL.inc(sy.dma_start(
                    out=Qxt[tb % 2][:].rearrange("p (c n) -> p c n", c=8),
                    in_=t["xms"][:, col0q:col0q + 512
                                 ].rearrange("(c p) n -> p c n", p=128)), 16)
                if tb >= 2:
                    sy.wait_ge(Q_RO.h, (tb - 1) * 2)
                Q_CT.inc(sy.dma_start(
                    out=Qcs[:, (tb % 2) * 1024:(tb % 2 + 1) * 1024
                            ].rearrange("p (c n) -> p c n", c=2),
                    in_=t["csQ"][:, tb * 512:(tb + 1) * 512
                                 ].rearrange("(c p) n -> p c n", p=128)), 16)
                Q_CT.inc(sy.dma_start(
                    out=Qsn[:, (tb % 2) * 1024:(tb % 2 + 1) * 1024
                            ].rearrange("p (c n) -> p c n", c=2),
                    in_=t["snQ"][:, tb * 512:(tb + 1) * 512
                                 ].rearrange("(c p) n -> p c n", p=128)), 16)
                for pp in range(2):
                    idx = tb * 2 + pp
                    sy.wait_ge(Q_RO.h, idx + 1)
                    for half in range(2):
                        for sub in range(2):
                            sl = (2 * pp if half == 0 else 4 + 2 * pp) + sub
                            Q_QW.inc(sy.dma_start(
                                out=t["QL"][sl * HR:sl * HR + 64,
                                            tb * 512:(tb + 1) * 512],
                                in_=Qrot[sub * 64:(sub + 1) * 64,
                                         half * 512:(half + 1) * 512]), 16)
        sy.wait_ge(Q_CQ.h, 1)
        QLr = t["QL"].rearrange("(sl r) n -> sl r n", r=HR)
        Q_QL.inc(sy.dma_start(out=QLr[:, 64, :], in_=Qcqh[:, :]), 16)
        Q_QL.inc(sy.dma_start(out=QLr[:, 65, :], in_=Qcql[:, :]), 16)
        for k in range(3):
            Q_QL.inc(sy.dma_start(out=QLr[:, 66 + k, :],
                                  in_=t["qa3"][k * 8:(k + 1) * 8, :]), 16)
        for k in range(3):
            Q_QL.inc(sy.dma_start(out=QLr[:, 69 + k, :],
                                  in_=t["ones16"][0:8, 0:NISL]), 16)
        if DBG.get("qdbg"):
            QD = st("QD")
            sy.wait_ge(st("Q_MS").h, 4)
            sy.wait_ge(Q_CQ.h, 1)
            with nc.allow_non_contiguous_dma(reason="debug dump"):
                for r0, src in [(16, Qm2f[:, 0:1]), (32, Qmk[:, 0:1]),
                                (40, Qn2[:, :]), (48, Qtmp[:, :])]:
                    h_, w_ = src.shape
                    QD.inc(sy.dma_start(out=t["qdbg"][r0:r0 + h_, 0:w_], in_=src), 16)
            with nc.allow_non_contiguous_dma(reason="debug dump"):
                QD.inc(sy.dma_start(out=t["qdbg"][56:56 + 8, 0:1],
                                    in_=Qpndbg[:, 0:1]), 16)
            with nc.allow_non_contiguous_dma(reason="debug dump"):
                for r0, src in [(0, Qmkb[:, :]), (16, Qcqh[:, :]), (24, Qcql[:, :]),
                                (32, Qm2b[:, 0:1])]:
                    h_, w_ = src.shape
                    QD.inc(sy.dma_start(out=t["qdbh"][r0:r0 + h_, 0:w_], in_=src), 16)
            sy.wait_ge(QD.h, QD.n)

    @blk.gpsimd
    def _(g):
        g.wait_ge(BDN.h, 16)
        Q_W.inc(g.dma_start(
            out=Qwq[:].rearrange("p (c n) -> p c n", c=8),
            in_=t["wqTm"].rearrange("(c p) n -> p c n", p=128)), 16)

    @blk.tensor
    def _(te):
        def emit_qnorms(nb):
            te.wait_ge(Q_QQ.h, nb * 2 + 2)
            if nb >= 1:
                te.wait_ge(Q_NE.h, nb)
            mm = None
            for pp in range(2):
                for half in range(2):
                    oc = pp + 2 * half
                    mm = te.matmul(Qpn[:, :], hselQ_sb[:, oc * 8:(oc + 1) * 8],
                                   Qsq[:, pp * 1024 + half * 512:
                                       pp * 1024 + (half + 1) * 512],
                                   start=(pp == 0 and half == 0),
                                   stop=(pp == 1 and half == 1))
            Q_NM.inc(mm, 1)

        te.wait_ge(Q_W.h, 16)
        te.wait_ge(LN_MR.h, mr3)
        for tb in range(8):
            te.wait_ge(Q_XL.h, (tb + 1) * 16)
            if tb >= 1:
                te.wait_ge(Q_AP.h, tb * 8)
            te.matmul(QpR[:, :], onesrow_sb[0:1, :],
                      rows16_sb[0:1, tb * 512:(tb + 1) * 512], start=True, stop=True)
            Q_BC.inc(te.matmul(QpM[:, :], onesrow_sb[0:1, :],
                               rows16_sb[0:1, 4096 + tb * 512:4096 + (tb + 1) * 512],
                               start=True, stop=True), 1)
            if tb >= 1:
                emit_qnorms(tb - 1)
            for pp in range(2):
                idx = tb * 2 + pp
                if idx >= 1:
                    te.wait_ge(Q_QE.h, idx)
                for ic in range(8):
                    if pp == 0:
                        te.wait_ge(Q_AP.h, tb * 8 + ic + 1)
                    te.matmul(QpA[:, :],
                              Qwq[:, ic * 512 + pp * 128:ic * 512 + (pp + 1) * 128],
                              Qxn[:, ic * 512:(ic + 1) * 512],
                              start=(ic == 0), stop=(ic == 7))
                for ic in range(8):
                    mm = te.matmul(QpB[:, :],
                                   Qwq[:, ic * 512 + (pp + 2) * 128:ic * 512 + (pp + 3) * 128],
                                   Qxn[:, ic * 512:(ic + 1) * 512],
                                   start=(ic == 0), stop=(ic == 7))
                Q_QP.inc(mm, 1)
        emit_qnorms(7)
        te.wait_ge(Q_NE.h, 8)
        te.wait_ge(Q_MS.h, 1)
        Q_SEL.inc(te.matmul(Qpn[0:8, 0:1], selhh_sb[:, :], Qm2b[:, 0:1],
                            start=True, stop=True), 1)

    @blk.vector
    def _(ve):
        for tb in range(8):
            ve.wait_ge(Q_BC.h, tb + 1)
            if tb >= 1:
                ve.wait_ge(Q_QP.h, tb * 2)
            for ic in range(8):
                ve.tensor_mul(Qxsc[:, :],
                              Qxt[tb % 2][:, ic * 512:(ic + 1) * 512], QpR[:, :])
                Q_AP.inc(ve.tensor_sub(Qxn[:, ic * 512:(ic + 1) * 512],
                                       Qxsc[:, :], QpM[:, :]), 1)
            for pp in range(2):
                idx = tb * 2 + pp
                ve.wait_ge(Q_QE.h, idx + 1)
                ve.wait_ge(Q_CT.h, (tb + 1) * 32)
                if idx >= 1:
                    ve.wait_ge(Q_QW.h, idx * 64)
                    ve.wait_ge(Q_QQ.h, idx)
                cs = Qcs[:, (tb % 2) * 1024 + pp * 512:(tb % 2) * 1024 + (pp + 1) * 512]
                sn = Qsn[:, (tb % 2) * 1024 + pp * 512:(tb % 2) * 1024 + (pp + 1) * 512]
                ve.tensor_mul(Qu[:, :], Qraw[:, 512:1024], sn)
                ve.tensor_mul(Qrot[:, 0:512], Qraw[:, 0:512], cs)
                ve.tensor_sub(Qrot[:, 0:512], Qrot[:, 0:512], Qu[:, :])
                ve.tensor_mul(Qu[:, :], Qraw[:, 0:512], sn)
                ve.tensor_mul(Qrot[:, 512:1024], Qraw[:, 512:1024], cs)
                Q_RO.inc(ve.tensor_add(Qrot[:, 512:1024], Qrot[:, 512:1024],
                                       Qu[:, :]), 1)
        ve.wait_ge(Q_MKL.h, 16)
        Q_RM.inc(ve.reduce_max(Qm2f[:, 0:1], Qmkb[:, :],
                               axis=mybir.AxisListType.X), 1)
        ve.wait_ge(Q_MS.h, 2)
        ve.wait_ge(Q_NE.h, 8)
        ve.wait_ge(Q_MS.h, 3)
        ve.tensor_scalar(Qcqh[:, :] if False else Qtmp[:, :], Qtmp[:, :],
                         Qmk[:, 0:1], -1.0, ALU.mult, ALU.mult)
        Q_MS.inc(ve.tensor_copy(Qcqh[:, :], Qtmp[:, :]), 1)
        if DBG.get("qdbg"):
            Q_MS.inc(ve.tensor_copy(Qpndbg[:, 0:1], Qpn[0:8, 0:1]), 1)
        Q_CQ.inc(ve.tensor_sub(Qcql[:, :], Qtmp[:, :], Qcqh[:, :]), 1)

    @blk.scalar
    def _(s):
        for tb in range(8):
            for pp in range(2):
                idx = tb * 2 + pp
                s.wait_ge(Q_QP.h, idx + 1)
                if idx >= 1:
                    s.wait_ge(Q_RO.h, idx)
                s.activation(out=Qraw[:, 0:512], in_=QpA[:, :], func=AF.Copy)
                Q_QE.inc(s.activation(out=Qraw[:, 512:1024], in_=QpB[:, :],
                                      func=AF.Copy), 1)
                s.wait_ge(Q_RO.h, idx + 1)
                if idx >= 2:
                    s.wait_ge(Q_NM.h, tb)
                Q_QQ.inc(s.activation(out=Qsq[:, pp * 1024:(pp + 1) * 1024],
                                      in_=Qrot[:, :], func=AF.Square), 1)
            if tb >= 1:
                s.wait_ge(Q_NM.h, tb)
                Q_NE.inc(s.activation(out=Qn2[:, (tb - 1) * 512:tb * 512],
                                      in_=Qpn[:, :], func=AF.Copy), 1)
        s.wait_ge(Q_NM.h, 8)
        Q_NE.inc(s.activation(out=Qn2[:, 7 * 512:8 * 512],
                              in_=Qpn[:, :], func=AF.Copy), 1)
        s.wait_ge(Q_RM.h, 1)
        Q_MS.inc(s.activation(out=Qm2b[:, 0:1], in_=Qm2f[:, 0:1],
                              func=AF.Copy), 1)
        s.wait_ge(Q_SEL.h, 1)
        s.activation(out=Qmk[:, 0:1], in_=Qpn[0:8, 0:1], func=AF.Ln)
        Q_MS.inc(s.activation(out=Qmk[:, 0:1], in_=Qmk[:, 0:1],
                              func=AF.Exp, scale=0.5), 1)
        s.wait_ge(Q_NE.h, 8)
        s.activation(out=Qtmp[:, :], in_=Qn2[:, :], func=AF.Ln)
        Q_MS.inc(s.activation(out=Qtmp[:, :], in_=Qtmp[:, :],
                              func=AF.Exp, scale=0.5), 1)
    es_.close()
    return mr3


def emit_attn(nc, t, st, VLr, i128_sb):
    CST = st("CST")
    BDN = st("BDN")
    Q_QL = st("Q_QL"); Q_QW = st("Q_QW")
    AT_MP = st("AT_MP")
    AT_QS = st("AT_QS")
    AT_KB = st("AT_KB")
    AT_VB = st("AT_VB")
    AT_SC = st("AT_SC")
    AT_EX = st("AT_EX")
    AT_AV = st("AT_AV")
    AT_OE = st("AT_OE")
    AT_SU = st("AT_SU")
    AT_OW = st("AT_OW")

    chains = []
    gidx = 0
    for j in range(8):
        for sl in range(8):
            lo, hi = chain_quads(j, sl)
            kts = list(range(lo * 4, hi * 4))
            groups = []
            pos = 0
            while pos < len(kts):
                gsz = 4 if gidx % 2 == 0 else 3
                gsz = min(gsz, len(kts) - pos)
                groups.append((gidx, kts[pos:pos + gsz]))
                gidx += 1
                pos += gsz
            chunks = list(range(lo, hi, 2))
            chains.append(dict(j=j, sl=sl, lo=lo, hi=hi, groups=groups,
                               chunks=chunks))
    NG = gidx
    chunk_ord = {}
    ordn = 0
    for c, ch in enumerate(chains):
        for q0 in ch["chunks"]:
            chunk_ord[(c, q0)] = ordn
            ordn += 1
    NCHUNK = ordn

    def chunk_of_kt(ch, kt):
        q0 = ch["lo"] + ((kt // 4 - ch["lo"]) // 2) * 2
        return q0

    last_group_using_chunk = [0] * NCHUNK
    last_av_group_of_chunk = [0] * NCHUNK
    for c, ch in enumerate(chains):
        for gi, kts in ch["groups"]:
            for kt in kts:
                o = chunk_ord[(c, chunk_of_kt(ch, kt))]
                last_group_using_chunk[o] = max(last_group_using_chunk[o], gi)
                last_av_group_of_chunk[o] = max(last_av_group_of_chunk[o], gi)
    last_group_of_chain = [ch["groups"][-1][0] for ch in chains]

    es_ = contextlib.ExitStack()
    ATmp = es_.enter_context(nc.sbuf_tensor("ATmp", [128, 8 * 512], BF16))
    ATq0 = es_.enter_context(nc.sbuf_tensor("ATq0", [72, 512], BF16))
    ATq1 = es_.enter_context(nc.sbuf_tensor("ATq1", [72, 512], BF16))
    ATk0 = es_.enter_context(nc.sbuf_tensor("ATk0", [72, 1024], BF16))
    ATk1 = es_.enter_context(nc.sbuf_tensor("ATk1", [72, 1024], BF16))
    ATk2 = es_.enter_context(nc.sbuf_tensor("ATk2", [72, 1024], BF16))
    ATv0 = es_.enter_context(nc.sbuf_tensor("ATv0", [128, 8 * 65], BF16))
    ATv1 = es_.enter_context(nc.sbuf_tensor("ATv1", [128, 8 * 65], BF16))
    ATv2 = es_.enter_context(nc.sbuf_tensor("ATv2", [128, 8 * 65], BF16))
    ATp0 = es_.enter_context(nc.sbuf_tensor("ATp0", [128, 4 * 512], BF16))
    ATp1 = es_.enter_context(nc.sbuf_tensor("ATp1", [128, 3 * 512], BF16))
    ATo0 = es_.enter_context(nc.sbuf_tensor("ATo0", [64, 512], BF16))
    ATo1 = es_.enter_context(nc.sbuf_tensor("ATo1", [64, 512], BF16))
    ATs0 = es_.enter_context(nc.sbuf_tensor("ATs0", [1, 1024], BF16))
    ATs1 = es_.enter_context(nc.sbuf_tensor("ATs1", [1, 1024], BF16))
    ATsqA = es_.enter_context(nc.psum_tensor("ATsqA", [128, 4 * 512], F32))
    ATsqB = es_.enter_context(nc.psum_tensor("ATsqB", [128, 3 * 512], F32))
    ATav = es_.enter_context(nc.psum_tensor("ATav", [65, 512], F32))
    blk = es_.enter_context(nc.Block())
    ATq = [ATq0, ATq1]
    ATk = [ATk0, ATk1, ATk2]
    ATv = [ATv0, ATv1, ATv2]
    ATp = [ATp0, ATp1]
    ATsq = [ATsqA, ATsqB]
    ATo = [ATo0, ATo1]
    ATs = [ATs0, ATs1]

    @blk.sync
    def _(sy):
        AT_MP.inc(sy.dma_start(out=ATmp[:, 0:8 * 512], in_=t["maskP"][:]), 16)
        for c, ch in enumerate(chains):
            if c >= 2:
                sy.wait_ge(AT_SC.h, last_group_of_chain[c - 2] + 1)
            sy.wait_ge(Q_QL.h, 128)
            sy.wait_ge(Q_QW.h, 1024)
            AT_QS.inc(sy.dma_start(
                out=ATq[c % 2][:, :],
                in_=t["QL"][ch["sl"] * HR:(ch["sl"] + 1) * HR,
                            ch["j"] * 512:(ch["j"] + 1) * 512]), 16)

    @blk.gpsimd
    def _(g):
        g.wait_ge(BDN.h, 16)
        pid = g.partition_id()
        nkb, nvb, now = AT_KB.n, AT_VB.n, AT_OW.n
        for cid in nc.Switch(engines=[g], index=[pid], n=8):
            AT_KB.n, AT_VB.n, AT_OW.n = nkb, nvb, now
            hh = cid % 2
            for c, ch in enumerate(chains):
                sg = 8 * hh + ch["sl"]
                for q0 in ch["chunks"]:
                    o = chunk_ord[(c, q0)]
                    if o >= 3:
                        g.wait_ge(AT_SC.h, last_group_using_chunk[o - 3] + 1)
                        g.wait_ge(AT_AV.h, last_av_group_of_chunk[o - 3] + 1)
                    kw = min(1024, S - q0 * 512)
                    nblk = kw // 128
                    AT_KB.inc(g.dma_start(
                        out=ATk[o % 3][:, 0:kw],
                        in_=t["KL"][sg * HR:(sg + 1) * HR,
                                    q0 * 512:q0 * 512 + kw]), 16)
                    AT_VB.inc(g.dma_start(
                        out=ATv[o % 3][:, 0:nblk * 65
                                       ].rearrange("p (a c) -> p a c", a=nblk),
                        in_=VLr[q0 * 4:q0 * 4 + nblk, :, sg, :
                                ].rearrange("a k c -> k a c")), 16)
                if c >= 1:
                    emit_ol_writes(g, t, chains[c - 1], hh, c - 1,
                                   AT_OE, AT_SU, AT_OW, ATo, ATs)
            emit_ol_writes(g, t, chains[-1], hh, len(chains) - 1,
                           AT_OE, AT_SU, AT_OW, ATo, ATs)

    @blk.tensor
    def _(te):
        te.wait_ge(CST.h, 112)
        te.wait_ge(AT_MP.h, 16)
        te.wait_ge(AT_QS.h, 16)
        te.wait_ge(AT_KB.h, 16)
        te.wait_ge(AT_VB.h, 16)
        plan_g = []
        for c, ch in enumerate(chains):
            for k2, (gi, kts) in enumerate(ch["groups"]):
                plan_g.append((c, ch, gi, kts, k2 == 0,
                               gi == ch["groups"][-1][0]))

        def emit_av(item):
            c, ch, gi, kts, is_first, is_last = item
            te.wait_ge(AT_EX.h, gi + 1)
            if is_first:
                te.wait_ge(AT_OE.h, c)
                te.wait_ge(AT_SU.h, c)
            gslot = gi % 2
            for u, kt in enumerate(kts):
                o = chunk_ord[(c, chunk_of_kt(ch, kt))]
                te.wait_ge(AT_VB.h, (o + 1) * 16)
                klo = (kt - chunk_of_kt(ch, kt) * 4)
                mm = te.matmul(ATav[:, :],
                               ATv[o % 3][:, klo * 65:(klo + 1) * 65],
                               ATp[gslot][:, u * 512:(u + 1) * 512],
                               start=(is_first and u == 0),
                               stop=(is_last and u == len(kts) - 1))
            AT_AV.inc(mm, 1)

        pend = None
        for item in plan_g:
            c, ch, gi, kts, is_first, is_last = item
            if is_first:
                te.wait_ge(AT_QS.h, (c + 1) * 16)
            gslot = gi % 2
            if gi >= 2:
                te.wait_ge(AT_EX.h, gi - 1)
            for u, kt in enumerate(kts):
                o = chunk_ord[(c, chunk_of_kt(ch, kt))]
                te.wait_ge(AT_KB.h, (o + 1) * 16)
                ktg = kt
                masked = 8 * ch["j"] <= ktg < 8 * ch["j"] + 8
                mm = te.matmul(ATsq[gslot][:, u * 512:(u + 1) * 512],
                               ATk[o % 3][:, (kt - chunk_of_kt(ch, kt) * 4)
                                          * 128:(kt - chunk_of_kt(ch, kt) * 4 + 1) * 128],
                               ATq[c % 2][:, :],
                               start=True, stop=(not masked))
                if masked:
                    mm = te.matmul(ATsq[gslot][:, u * 512:(u + 1) * 512],
                                   i128_sb[:, :],
                                   ATmp[:, (ktg - 8 * ch["j"]) * 512:
                                        (ktg - 8 * ch["j"] + 1) * 512],
                                   start=False, stop=True)
            AT_SC.inc(mm, 1)
            if pend is not None:
                emit_av(pend)
            pend = item
        emit_av(pend)

    @blk.scalar
    def _(s):
        for c, ch in enumerate(chains):
            for gi, kts in ch["groups"]:
                gslot = gi % 2
                s.wait_ge(AT_SC.h, gi + 1)
                if gi >= 2:
                    s.wait_ge(AT_AV.h, gi - 1)
                AT_EX.inc(s.activation(
                    out=ATp[gslot][:, 0:len(kts) * 512],
                    in_=ATsq[gslot][:, 0:len(kts) * 512],
                    func=AF.Exp, scale=0.125), 1)

    @blk.vector
    def _(ve):
        for c, ch in enumerate(chains):
            ve.wait_ge(AT_AV.h, last_group_of_chain[c] + 1)
            if c >= 2:
                ve.wait_ge(AT_OW.h, (c - 1) * 48)
            AT_OE.inc(ve.tensor_copy(ATo[c % 2][:, :], ATav[0:64, :]), 1)
            ve.tensor_copy(ATs[c % 2][0:1, 0:512], ATav[64:65, :])
            AT_SU.inc(ve.tensor_sub(ATs[c % 2][0:1, 512:1024], ATav[64:65, :],
                                    ATs[c % 2][0:1, 0:512]), 1)


    es_.close()
def emit_ol_writes(g, t, ch, hh, c, AT_OE, AT_SU, AT_OW, ATo, ATs):
    sg = 8 * hh + ch["sl"]
    g.wait_ge(AT_OE.h, c + 1)
    g.wait_ge(AT_SU.h, c + 1)
    col = ch["j"] * 512
    AT_OW.inc(g.dma_start(out=t["OL"][sg * 64:(sg + 1) * 64, col:col + 512],
                          in_=ATo[c % 2][:, :]), 16)
    AT_OW.inc(g.dma_start(out=t["OL"][1024 + sg:1024 + sg + 1, col:col + 512],
                          in_=ATs[c % 2][0:1, 0:512]), 16)
    AT_OW.inc(g.dma_start(out=t["OL"][1040 + sg:1040 + sg + 1, col:col + 512],
                          in_=ATs[c % 2][0:1, 512:1024]), 16)


def emit_tail(nc, t, st, ln_block, ffn_block,
              onescol_sb, onescolb_sb, sel16_sb, BDN, WGU_CACHED):
    WGU = st("WGU")
    TL_OL = st("TL_OL")
    TL_WO = st("TL_WO")
    TL_RC = st("TL_RC")
    TL_XM = st("TL_XM")
    TL_RB = st("TL_RB")
    TL_ON = st("TL_ON")
    TL_PO = st("TL_PO")
    TL_XE = st("TL_XE")
    TL_X3 = st("TL_X3")
    TL_YS = st("TL_YS")

    es_ = contextlib.ExitStack()
    TLos = es_.enter_context(nc.sbuf_tensor("TLos", [128, 8 * 2048], BF16))
    TLsh = es_.enter_context(nc.sbuf_tensor("TLsh", [16, 2048], BF16))
    TLsl = es_.enter_context(nc.sbuf_tensor("TLsl", [16, 2048], BF16))
    TLrr = es_.enter_context(nc.sbuf_tensor("TLrr", [16, 2048], F32))
    TLrh = es_.enter_context(nc.sbuf_tensor("TLrh", [16, 2048], BF16))
    TLrl = es_.enter_context(nc.sbuf_tensor("TLrl", [16, 2048], BF16))
    TLwo = es_.enter_context(nc.sbuf_tensor("TLwo", [128, 8 * 1024], BF16))
    TLob = es_.enter_context(nc.sbuf_tensor("TLob", [128, 8], F32))
    TLxm0 = es_.enter_context(nc.sbuf_tensor("TLxm0", [128, 8 * 512], F32))
    TLxm1 = es_.enter_context(nc.sbuf_tensor("TLxm1", [128, 8 * 512], F32))
    TLon = es_.enter_context(nc.sbuf_tensor("TLon", [128, 8 * 512], BF16))
    TLpb0 = es_.enter_context(nc.psum_tensor("TLpb0", [128, 512], F32))
    TLpb1 = es_.enter_context(nc.psum_tensor("TLpb1", [128, 512], F32))
    TLpo0 = es_.enter_context(nc.psum_tensor("TLpo0", [128, 512], F32))
    TLpo1 = es_.enter_context(nc.psum_tensor("TLpo1", [128, 512], F32))
    blk = es_.enter_context(nc.Block())
    TLxm = [TLxm0, TLxm1]
    TLpb = [TLpb0, TLpb1]
    TLpo = [TLpo0, TLpo1]

    @blk.gpsimd
    def _(g):
        g.wait_ge(BDN.h, 32)
        TL_WO.inc(g.dma_start(
            out=TLwo[:].rearrange("p (c n) -> p c n", c=8),
            in_=t["woTs"].rearrange("(c p) n -> p c n", p=128)), 16)
        TL_WO.inc(g.dma_start(out=TLob[:, :], in_=t["outb"][:, :]), 16)
        pid = g.partition_id()
        nol, nxm = TL_OL.n, TL_XM.n
        for cid in nc.Switch(engines=[g], index=[pid], n=8):
            TL_OL.n, TL_XM.n = nol, nxm
            s_isl = (cid % 4) // 2
            hh = cid % 2
            c4 = s_isl + 2 * hh
            for m in range(4):
                icol = 512 * (2 * m + hh)
                TL_OL.inc(g.dma_start(
                    out=TLos[:].rearrange("p (c n) -> p c n", c=8)
                    [:, :, m * 512:(m + 1) * 512],
                    in_=t["OL"][0:1024, icol:icol + 512
                                ].rearrange("(c p) n -> p c n", p=128)), 16)
                TL_OL.inc(g.dma_start(
                    out=TLsh[:, m * 512:(m + 1) * 512],
                    in_=t["OL"][1024:1040, icol:icol + 512]), 16)
                TL_OL.inc(g.dma_start(
                    out=TLsl[:, m * 512:(m + 1) * 512],
                    in_=t["OL"][1040:1056, icol:icol + 512]), 16)
            for m in range(4):
                gcol = 512 * (4 * m + c4)
                if m >= 2:
                    g.wait_ge(TL_X3.h, (m - 1) * 16)
                TL_XM.inc(g.dma_start(
                    out=TLxm[m % 2][:].rearrange("p (c n) -> p c n", c=8),
                    in_=t["xms"][:, gcol:gcol + 512
                                 ].rearrange("(c p) n -> p c n", p=128)), 16)

    @blk.vector
    def _(ve):
        ve.wait_ge(TL_OL.h, 192)
        ve.tensor_add(TLrr[:, :], TLsh[:, :], TLsl[:, :])
        ve.reciprocal(TLrr[:, :], TLrr[:, :])
        ve.tensor_copy(TLrh[:, :], TLrr[:, :])
        TL_RC.inc(ve.tensor_sub(TLrl[:, :], TLrr[:, :], TLrh[:, :]), 1)
        for m in range(4):
            if m >= 1:
                ve.wait_ge(TL_PO.h, m * 8)
            for ic in range(8):
                i2 = m * 8 + ic
                ve.wait_ge(TL_RB.h, i2 + 1)
                TL_ON.inc(ve.tensor_mul(
                    TLon[:, ic * 512:(ic + 1) * 512],
                    TLos[:].rearrange("p (c n) -> p c n", c=8)
                    [:, ic, m * 512:(m + 1) * 512],
                    TLpb[i2 % 2][:, :]), 1)
            for oc in range(8):
                k = m * 8 + oc
                ve.wait_ge(TL_PO.h, k + 1)
                xv = TLxm[m % 2][:].rearrange("p (c n) -> p c n", c=8)[:, oc, :]
                TL_XE.inc(ve.scalar_tensor_tensor(
                    xv, TLpo[k % 2][:, :], TLob[:, oc:oc + 1], xv,
                    ALU.add, ALU.add), 1)

    @blk.tensor
    def _(te):
        te.wait_ge(TL_WO.h, 32)
        te.wait_ge(TL_RC.h, 1)
        for m in range(4):
            for ic in range(8):
                i2 = m * 8 + ic
                if i2 >= 2:
                    te.wait_ge(TL_ON.h, i2 - 1)
                te.matmul(TLpb[i2 % 2][:, :], sel16_sb[:, ic * 128:(ic + 1) * 128],
                          TLrh[:, m * 512:(m + 1) * 512], start=True, stop=False)
                TL_RB.inc(te.matmul(TLpb[i2 % 2][:, :],
                                    sel16_sb[:, ic * 128:(ic + 1) * 128],
                                    TLrl[:, m * 512:(m + 1) * 512],
                                    start=False, stop=True), 1)
            te.wait_ge(TL_XM.h, (m + 1) * 16)
            for oc in range(8):
                k = m * 8 + oc
                if k >= 2:
                    te.wait_ge(TL_XE.h, k - 1)
                for ic in range(8):
                    te.wait_ge(TL_ON.h, m * 8 + ic + 1)
                    mm = te.matmul(TLpo[k % 2][:, :],
                                   TLwo[:, ic * 1024 + oc * 128:ic * 1024 + (oc + 1) * 128],
                                   TLon[:, ic * 512:(ic + 1) * 512],
                                   start=(ic == 0), stop=(ic == 7))
                TL_PO.inc(mm, 1)

    @blk.sync
    def _(sy):
        for m in range(4):
            sy.wait_ge(TL_XE.h, (m + 1) * 8)
            TL_X3.inc(sy.dma_start(
                out=t["x3d"][:, m * 512:(m + 1) * 512
                             ].rearrange("(c p) n -> p c n", p=128),
                in_=TLxm[m % 2][:].rearrange("p (c n) -> p c n", c=8)), 16)

    es_.close()
    def x3_cols(tb):
        return t["x3d"][:, tb * 512:(tb + 1) * 512].rearrange("(c p) n -> p c n", p=128)

    mr4 = ln_block("L3_", 4, xsrc=x3_cols, pre_wait=[(st("TL_X3"), 64)])

    W2 = st("W2")
    es_ = contextlib.ExitStack()
    wg2_sb = es_.enter_context(nc.sbuf_tensor("wg2_sb", [128, 8 * HID], BF16))
    wu2_sb = es_.enter_context(nc.sbuf_tensor("wu2_sb", [128, 8 * HID], BF16))
    with nc.Block() as blk:
        @blk.sync
        def _(sy):
            sy.wait_ge(WGU.h, WGU_CACHED)
            W2.inc(sy.dma_start(out=wg2_sb[:, :], in_=t["wg_bf"][:, :]), 16)
            W2.inc(sy.dma_start(out=wu2_sb[:, :], in_=t["wu_bf"][:, :]), 16)

    F_XS2 = st("F_XS")

    def y_sink(g, cid, tb, buf):
        F_XS2.inc(g.dma_start(
            out=t["y"][:, tb * 512:(tb + 1) * 512
                       ].rearrange("(c p) n -> p c n", p=128),
            in_=buf[:].rearrange("p (c n) -> p c n", c=8)), 16)

    ffn_block("F2_", 4, x3_cols, wg2_sb, wu2_sb, y_sink, False, mr4,
              extra_pe_wait=[(W2, 32)],
              sync_pre_wait=[(st("TL_X3"), 64)])


    es_.close()
def _triple_split(v):
    a = v.astype(BF).astype(np.float64)
    b = (v - a).astype(BF).astype(np.float64)
    c = (v - a - b).astype(BF)
    return a.astype(BF), b.astype(BF), c

def _rope_tables(pos, nfreq=512):
    freqs = 1.0 / 10000.0 ** (np.arange(nfreq, dtype=np.float64) / 512.0)
    ang = pos[None, :] * freqs[:, None]
    return np.cos(ang).astype(BF), np.sin(ang).astype(BF)


def _host_inputs(L0_attn_x, wq, wk, wv, wo, ob, wg, wu, wd, nonce):
    xs = np.asarray(L0_attn_x)
    wgT = np.ascontiguousarray(np.asarray(wg).T)
    wuT = np.ascontiguousarray(np.asarray(wu).T)
    wdT = np.ascontiguousarray(np.asarray(wd).T)
    wkT = np.ascontiguousarray(np.asarray(wk).T)
    wvT = np.ascontiguousarray(np.asarray(wv).T)
    wqT = np.asarray(wq).T
    woT = np.asarray(wo).T
    ob = np.ascontiguousarray(
        np.asarray(ob).reshape(8, 128).T).astype(np.float32)

    perm = np.zeros(D, dtype=np.int64)
    for sg in range(16):
        h = head_of_slot(sg)
        perm[sg * 64:(sg + 1) * 64] = np.arange(h * 64, (h + 1) * 64)
    woTs = np.ascontiguousarray(woT[perm, :]).astype(np.float32)

    ones16 = np.ones((16, S), BF)
    i128 = np.eye(128, dtype=BF)
    onescol = np.ones((1, 128), np.float32)
    onescolf = np.ones((128, 1), np.float32)
    onescolb = np.ones((128, 1), BF)

    hselK = np.zeros((128, 8 * 16), BF)
    for oc in range(8):
        for r in range(128):
            h = (128 * oc + r) // 64
            hselK[r, oc * 16 + slot_of_head(h)] = 1.0
    sel16 = np.zeros((16, 8 * 128), BF)
    for ic in range(8):
        for p in range(128):
            sg = 2 * ic + p // 64
            sel16[sg, ic * 128 + p] = 1.0

    fconst = np.zeros((1, 16), np.float32)
    fconst[0, 0] = EPS
    nonce_arr = np.zeros((1, 16), np.uint32)
    nonce_arr[0, 0] = np.uint32(nonce + 1)
    nonce_arr[0, 1] = np.uint32(nonce + 2)

    ins = []
    for core in range(8):
        b = core // 4
        s_isl = (core % 4) // 2
        hh = core % 2
        c4 = s_isl + 2 * hh
        x_half = np.ascontiguousarray(xs[b, hh * NTOK:(hh + 1) * NTOK, :].T
                                      ).astype(np.float32)
        kpos = np.arange(hh * NTOK, (hh + 1) * NTOK, dtype=np.float64)
        ipos = (512.0 * (2 * (np.arange(NISL) // 512) + s_isl)
                + (np.arange(NISL) % 512)).astype(np.float64)

        csK, snK = _rope_tables(kpos)
        myh1 = [hh, hh + 2, hh + 4, hh + 6]
        cols = []
        for h in myh1:
            cols.extend(range(64 * h, 64 * h + 64))
        for h in myh1:
            cols.extend(range(64 * (h + 8), 64 * (h + 8) + 64))
        wqTm = np.ascontiguousarray(wqT[:, cols]).astype(np.float32)
        freqs_idx = np.array(cols[:256])
        fr = 1.0 / 10000.0 ** (freqs_idx.astype(np.float64) / 512.0)
        angQ = ipos[None, :] * fr[:, None]
        csQ = np.cos(angQ).astype(BF)
        snQ = np.sin(angQ).astype(BF)

        a3K = np.zeros((48, NTOK), BF)
        for sg in range(16):
            h = head_of_slot(sg)
            v = 8.0 * SLOPES[h] * kpos
            aa, bb, cc = _triple_split(v)
            a3K[sg] = aa
            a3K[16 + sg] = bb
            a3K[32 + sg] = cc
        qa3 = np.zeros((24, NISL), BF)
        for sl in range(8):
            h = 2 * sl + hh
            v = -8.0 * SLOPES[h] * ipos
            aa, bb, cc = _triple_split(v)
            qa3[sl] = aa
            qa3[8 + sl] = bb
            qa3[16 + sl] = cc

        hselQ = np.zeros((128, 32), BF)
        for oc in range(4):
            for r in range(128):
                if oc < 2:
                    sl = 2 * oc + r // 64
                else:
                    sl = 4 + 2 * (oc - 2) + r // 64
                hselQ[r, oc * 8 + sl] = 1.0

        selhh = np.zeros((16, 8), BF)
        for c in range(8):
            selhh[8 * hh + c, c] = 1.0

        maskP = np.zeros((128, 8 * 512), BF)
        pq = np.arange(512)
        for slot in range(8):
            pp = np.arange(128)[:, None]
            m = (128 * slot + pp) > (512 * s_isl + pq[None, :])
            maskP[:, slot * 512:(slot + 1) * 512] = np.where(m, NEGBIG, 0.0).astype(BF)

        ins.append(dict(
            x=x_half, wgT=wgT, wuT=wuT, wdT=wdT, wkT=wkT, wvT=wvT,
            wqTm=wqTm, woTs=woTs, outb=ob,
            csK=np.ascontiguousarray(csK), snK=np.ascontiguousarray(snK),
            csQ=np.ascontiguousarray(csQ), snQ=np.ascontiguousarray(snQ),
            a3K=a3K, qa3=qa3, ones16=ones16, maskP=maskP,
            hselK=hselK, hselQ=hselQ, sel16=sel16, selhh=selhh, i128=i128,
            onescol=onescol, onescolf=onescolf, onescolb=onescolb,
            nonce=nonce_arr, fconst=fconst))
    return ins


_NC_CACHE = {}


def kernel(L0_attn_x, L0_attn_wq, L0_attn_wk, L0_attn_wv, L0_attn_out_w,
           L0_attn_out_b, L0_ffn_w_gate, L0_ffn_w_up, L0_ffn_w_down):
    import time
    nonce = (int(time.time_ns() // 100) % (1 << 28)) * 4
    ins = _host_inputs(L0_attn_x, L0_attn_wq, L0_attn_wk, L0_attn_wv,
                       L0_attn_out_w, L0_attn_out_b,
                       L0_ffn_w_gate, L0_ffn_w_up, L0_ffn_w_down, nonce)
    if "nc" not in _NC_CACHE:
        _NC_CACHE["nc"] = build_program()
    nc, _ = _NC_CACHE["nc"]
    res = run_bass_kernel_spmd(nc, ins, core_ids=list(range(8)))
    out = np.zeros((B, S, D), np.float32)
    for core in range(8):
        b = core // 4
        s_isl = (core % 4) // 2
        hh = core % 2
        c4 = s_isl + 2 * hh
        y = res.results[core]["y"]
        for m in range(4):
            g0 = 512 * (4 * m + c4)
            out[b, g0:g0 + 512, :] = y[:, m * 512:(m + 1) * 512].T
    return out


# version 09
# speedup vs baseline: 1.4429x; 1.0003x over previous
import math
import contextlib
import numpy as np
import ml_dtypes
import concourse.bass as bass
import concourse.mybir as mybir
from concourse.bass_utils import run_bass_kernel_spmd

F32 = mybir.dt.float32
BF16 = mybir.dt.bfloat16
I32 = mybir.dt.int32
U32 = mybir.dt.uint32
AF = mybir.ActivationFunctionType
ALU = mybir.AluOpType
BF = ml_dtypes.bfloat16

B, S, D = 2, 8192, 1024
H, DH = 16, 64
HID = 3072
NTOK = 4096
NOWN = 2048
NISL = 4096
HR = 72
DEQ = 1.0 / 127.0
FFN_SC = 0.5 * DEQ * DEQ
EPS = 1e-5
NEGBIG = -1.0e9
SLOPES = [2.0 ** (-8.0 * (i + 1) / H) for i in range(H)]


def slot_of_head(h):
    return 8 * (h % 2) + h // 2


def head_of_slot(sg):
    return 2 * (sg % 8) + sg // 8


T_BAND = 64.0
WINQ = []
for _sl in range(8):
    _w = T_BAND / SLOPES[2 * _sl + 1]
    WINQ.append(None if _w >= S else int(math.ceil(_w / 512.0)))


def chain_quads(j, sl):
    hi = 2 * j + 2
    lo = 0 if WINQ[sl] is None else max(0, 2 * j - WINQ[sl])
    return lo, hi


class Stream:
    def __init__(self, handle):
        self.h = handle
        self.n = 0

    def inc(self, ins, k):
        ins.then_inc(self.h, k)
        self.n += k
        return self.n


DBG = {"kv_vl": True, "kv_kl": True}


def build_program(dumps=(), stop_after=None):
    nc = bass.Bass("TRN2", target_bir_lowering=False)
    t = {}

    def din(name, shape, dt):
        t[name] = nc.dram_tensor(name, shape, dt, kind="ExternalInput").ap()

    def dout(name, shape, dt):
        t[name] = nc.dram_tensor(name, shape, dt, kind="ExternalOutput").ap()

    def dint(name, shape, dt, shared=False):
        t[name] = nc.dram_tensor(name, shape, dt, kind="Internal",
                                 addr_space="Shared" if shared else "Local").ap()

    din("x", [D, NTOK], F32)
    din("wgT", [D, HID], I32)
    din("wuT", [D, HID], I32)
    din("wdT", [HID, D], I32)
    din("wkT", [D, D], F32)
    din("wvT", [D, D], F32)
    din("wqTm", [D, 512], F32)
    din("woTs", [D, D], F32)
    din("outb", [128, 8], F32)
    din("csK", [512, NTOK], BF16)
    din("snK", [512, NTOK], BF16)
    din("csQ", [256, NISL], BF16)
    din("snQ", [256, NISL], BF16)
    din("a3K", [3 * 16, NTOK], BF16)
    din("qa3", [3 * 8, NISL], BF16)
    din("ones16", [16, S], BF16)
    din("maskP", [128, 8 * 512], BF16)
    din("hselK", [128, 8 * 16], BF16)
    din("hselQ", [128, 4 * 8], BF16)
    din("sel16", [16, 8 * 128], BF16)
    din("selhh", [16, 8], BF16)
    din("i128", [128, 128], BF16)
    din("onescol", [1, 128], F32)
    din("onescolf", [128, 1], F32)
    din("onescolb", [128, 1], BF16)
    din("nonce", [1, 16], U32)
    din("fconst", [1, 16], F32)
    dout("y", [D, NOWN], F32)

    dint("wg_bf", [128, 8 * HID], BF16)
    dint("wu_bf", [128, 8 * HID], BF16)
    dint("wd_bf", [8 * 128, 24 * 128], BF16)
    dint("QL", [8 * HR, NISL], BF16)
    dint("x3d", [D, NOWN], F32)
    dint("xmp", [D, NTOK], F32)
    dint("xip", [D, NISL], F32)

    dint("xms", [128, 16 * 4096], F32, shared=True)
    dint("KL", [H * HR + 16, S], BF16, shared=True)
    dint("VL", [64 * 16 * 128, 65], BF16, shared=True)
    dint("OL", [16 * 64 + 32, NISL], BF16, shared=True)
    dint("flags", [2, 16], U32, shared=True)
    dint("qdbg", [64, 8192], F32)
    dint("qdbh", [48, 8192], BF16)

    VLr = t["VL"].rearrange("(kc s k) c -> kc s k c", s=16, k=128)
    t["kvdbg"] = nc.dram_tensor("kvdbg", [128, 1040], BF16, kind="Internal").ap()
    KLs = t["KL"].rearrange("(s r) n -> s r n", r=HR) if False else None
    NORM0 = H * HR

    for nm in dumps:
        shp = {"xms": ([128, 16 * 4096], F32), "KL": ([H * HR + 16, S], BF16),
               "qdbg": ([64, 8192], F32), "qdbh": ([48, 8192], BF16),
               "VL": ([64 * 16 * 128, 65], BF16), "QL": ([8 * HR, NISL], BF16),
               "OL": ([16 * 64 + 32, NISL], BF16), "x3d": ([D, NOWN], F32)}[nm]
        dout("d_" + nm, *shp)

    est = contextlib.ExitStack()
    _streams = {}

    def st(name):
        if name not in _streams:
            _streams[name] = Stream(est.enter_context(nc.semaphore(name)))
        return _streams[name]

    sbp = lambda n, shp, dt: est.enter_context(nc.sbuf_tensor(n, shp, dt))

    def emit_dumps():
        if not dumps:
            return
        DU = st("DU")
        with nc.Block() as blk2:
            @blk2.gpsimd
            def _(g):
                for snm, s_ in list(_streams.items()):
                    if snm != "DU" and s_.n > 0:
                        g.wait_ge(s_.h, s_.n)
                for nm in dumps:
                    DU.inc(g.dma_start(out=t["d_" + nm][:], in_=t[nm][:]), 16)
                g.wait_ge(DU.h, DU.n)

    onescol_sb = sbp("onescol_sb", [1, 128], F32)
    onescolf_sb = sbp("onescolf_sb", [128, 1], F32)
    onescolb_sb = sbp("onescolb_sb", [128, 1], BF16)
    hselK_sb = sbp("hselK_sb", [128, 8 * 16], BF16)
    hselQ_sb = sbp("hselQ_sb", [128, 4 * 8], BF16)
    sel16_sb = sbp("sel16_sb", [16, 8 * 128], BF16)
    selhh_sb = sbp("selhh_sb", [16, 8], BF16)
    i128_sb = sbp("i128_sb", [128, 128], BF16)
    nonce_sb = sbp("nonce_sb", [1, 16], U32)
    fconst_sb = sbp("fconst_sb", [1, 16], F32)
    flag_sb = sbp("flag_sb", [1, 16], U32)
    onesrow_sb = sbp("onesrow_sb", [1, 128], BF16)
    roww_sb = sbp("roww_sb", [1, 2048], F32)
    rows16_sb = sbp("rows16_sb", [1, 8192], BF16)

    CST = st("CST")
    WCH = st("WCH")
    BDN = st("BDN")

    es_ = contextlib.ExitStack()
    wdtmp = es_.enter_context(nc.sbuf_tensor("wdtmp", [128, 24 * D], BF16))
    blk = es_.enter_context(nc.Block())
    @blk.sync
    def _(sy):
        for dst, src in [(onescol_sb, t["onescol"][:]),
                         (onescolf_sb, t["onescolf"][:]),
                         (onescolb_sb, t["onescolb"][:]),
                         (hselK_sb, t["hselK"][:]),
                         (hselQ_sb, t["hselQ"][:]),
                         (sel16_sb, t["sel16"][:]),
                         (i128_sb, t["i128"][:]),
                         (nonce_sb, t["nonce"][:]),
                         (fconst_sb, t["fconst"][:]),
                         (onesrow_sb, t["ones16"][0:1, 0:128]),
                         (selhh_sb, t["selhh"][:])]:
            CST.inc(sy.dma_start(out=dst[:], in_=src), 16)

    @blk.gpsimd
    def _(g):
        WCH.inc(g.dma_start(out=wdtmp[:].rearrange("p (c n) -> p c n", c=24),
                            in_=t["wdT"].rearrange("(c p) n -> p c n", p=128)), 16)
        g.wait_ge(WCH.h, WCH.n)
        wtr = wdtmp[:].rearrange("p (h n) -> p h n", h=24)
        for dc in range(8):
            WCH.inc(g.dma_start(
                out=t["wd_bf"][dc * 128:(dc + 1) * 128, :
                               ].rearrange("p (h n) -> p h n", h=24),
                in_=wtr[:, :, dc * 128:(dc + 1) * 128]), 16)
        g.wait_ge(WCH.h, WCH.n)
    es_.close()
    CST_ALL = CST.n
    WD_CACHED = WCH.n

    LN_LD = st("LN_LD"); LN_SQ = st("LN_SQ"); LN_MM = st("LN_MM")
    LN_EV = st("LN_EV"); LN_VAR = st("LN_VAR"); LN_R = st("LN_R")
    LN_MR = st("LN_MR"); LN_CB = st("LN_CB")

    def ln_block(pfx, n_tb, xsrc=None, gp_loader=None, pre_wait=()):
        b_ld, b_sq, b_mm, b_ev = LN_LD.n, LN_SQ.n, LN_MM.n, LN_EV.n
        b_var, b_r, b_mr = LN_VAR.n, LN_R.n, LN_MR.n
        b_cb = LN_CB.n
        es_ = contextlib.ExitStack()
        xa = es_.enter_context(nc.sbuf_tensor(pfx + "xa", [128, 8 * 512], F32))
        xb = es_.enter_context(nc.sbuf_tensor(pfx + "xb", [128, 8 * 512], F32))
        xb16 = es_.enter_context(nc.sbuf_tensor(pfx + "xb16", [128, 8 * 512], BF16))
        sq = es_.enter_context(nc.sbuf_tensor(pfx + "sq", [128, 8 * 512], BF16))
        p1 = es_.enter_context(nc.psum_tensor(pfx + "p1", [1, 512], F32))
        p2 = es_.enter_context(nc.psum_tensor(pfx + "p2", [1, 512], F32))
        blk = es_.enter_context(nc.Block())
        xt = [xa, xb]

        def loads(eng, cid=None):
            for ww in pre_wait:
                eng.wait_ge(ww[0].h, ww[1])
            for tb in range(n_tb):
                if tb >= 2:
                    eng.wait_ge(LN_MM.h, b_mm + tb - 1)
                if gp_loader is None:
                    LN_LD.inc(eng.dma_start(
                        out=xt[tb % 2][:].rearrange("p (c n) -> p c n", c=8),
                        in_=xsrc(tb)), 16)
                else:
                    gp_loader(eng, cid, tb, xt[tb % 2], LN_LD)

        if gp_loader is None:
            @blk.sync
            def _(sy):
                loads(sy)
        else:
            @blk.sync
            def _(sy):
                pid = sy.partition_id()
                n0 = LN_LD.n
                for cid in nc.Switch(engines=[sy], index=[pid], n=8):
                    LN_LD.n = n0
                    loads(sy, cid)

        @blk.scalar
        def _(s):
            for tb in range(n_tb):
                s.wait_ge(LN_CB.h, b_cb + tb + 1)
                if tb >= 1:
                    s.wait_ge(LN_MM.h, b_mm + tb)
                LN_SQ.inc(s.activation(out=sq[:], in_=xb16[:],
                                       func=AF.Square), 1)
                s.wait_ge(LN_VAR.h, b_var + tb + 1)
                s.activation(out=roww_sb[0:1, 1024:1536], in_=roww_sb[0:1, 512:1024],
                             func=AF.Ln, bias=fconst_sb[0:1, 0:1])
                LN_R.inc(s.activation(out=roww_sb[0:1, 1024:1536],
                                      in_=roww_sb[0:1, 1024:1536],
                                      func=AF.Exp, scale=-0.5), 1)

        @blk.tensor
        def _(te):
            te.wait_ge(CST.h, CST_ALL)
            for tb in range(n_tb):
                te.wait_ge(LN_SQ.h, b_sq + tb + 1)
                if tb >= 1:
                    te.wait_ge(LN_EV.h, b_ev + tb)
                for ic in range(8):
                    te.matmul(p1[0:1, :], onescolb_sb[:, 0:1],
                              xb16[:, ic * 512:(ic + 1) * 512],
                              start=(ic == 0), stop=(ic == 7))
                for ic in range(8):
                    mm = te.matmul(p2[0:1, :], onescolb_sb[:, 0:1],
                                   sq[:, ic * 512:(ic + 1) * 512],
                                   start=(ic == 0), stop=(ic == 7))
                LN_MM.inc(mm, 1)

        @blk.vector
        def _(ve):
            for tb in range(n_tb):
                ve.wait_ge(LN_LD.h, b_ld + (tb + 1) * 16)
                LN_CB.inc(ve.tensor_copy(xb16[:, :], xt[tb % 2][:, :]), 1)
                ve.wait_ge(LN_MM.h, b_mm + tb + 1)
                if tb >= 1:
                    ve.wait_ge(LN_R.h, b_r + tb)
                ve.tensor_copy(roww_sb[0:1, 0:512], p1[0:1, :])
                LN_EV.inc(ve.tensor_copy(roww_sb[0:1, 512:1024], p2[0:1, :]), 1)
                ve.tensor_scalar_mul(roww_sb[0:1, 0:512], roww_sb[0:1, 0:512],
                                     1.0 / 1024.0)
                ve.tensor_scalar_mul(roww_sb[0:1, 512:1024], roww_sb[0:1, 512:1024],
                                     1.0 / 1024.0)
                ve.tensor_mul(roww_sb[0:1, 1024:1536], roww_sb[0:1, 0:512],
                              roww_sb[0:1, 0:512])
                LN_VAR.inc(ve.tensor_sub(roww_sb[0:1, 512:1024],
                                         roww_sb[0:1, 512:1024],
                                         roww_sb[0:1, 1024:1536]), 1)
                ve.wait_ge(LN_R.h, b_r + tb + 1)
                ve.tensor_copy(rows16_sb[0:1, tb * 512:(tb + 1) * 512],
                               roww_sb[0:1, 1024:1536])
                LN_MR.inc(ve.tensor_mul(
                    rows16_sb[0:1, 4096 + tb * 512:4096 + (tb + 1) * 512],
                    roww_sb[0:1, 0:512], roww_sb[0:1, 1024:1536]), 1)

        es_.close()
        return LN_MR.n

    F_XL = st("F_XL"); F_WD = st("F_WD"); F_BC = st("F_BC"); F_AP = st("F_AP")
    F_GU = st("F_GU"); F_SI = st("F_SI"); F_MU = st("F_MU"); F_DM = st("F_DM")
    F_XE = st("F_XE"); F_XS = st("F_XS")

    def ffn_block(pfx, n_tb, xsrc, wg_sb, wu_sb, sink, sink_switch,
                  mr_count, extra_pe_wait=(), sync_pre_wait=(), xs_per_tb=16):
        b_xl, b_wd, b_bc, b_ap = F_XL.n, F_WD.n, F_BC.n, F_AP.n
        b_gu, b_si, b_mu, b_dm = F_GU.n, F_SI.n, F_MU.n, F_DM.n
        b_xe, b_xs = F_XE.n, F_XS.n
        es_ = contextlib.ExitStack()
        xa = es_.enter_context(nc.sbuf_tensor(pfx + "xa", [128, 8 * 512], F32))
        xb = es_.enter_context(nc.sbuf_tensor(pfx + "xb", [128, 8 * 512], F32))
        xn = es_.enter_context(nc.sbuf_tensor(pfx + "xn", [128, 8 * 512], BF16))
        xsc = es_.enter_context(nc.sbuf_tensor(pfx + "xsc", [128, 512], F32))
        sg0 = es_.enter_context(nc.sbuf_tensor(pfx + "sg0", [128, 512], BF16))
        sg1 = es_.enter_context(nc.sbuf_tensor(pfx + "sg1", [128, 512], BF16))
        ha = es_.enter_context(nc.sbuf_tensor(pfx + "ha", [128, 24 * 512], BF16))
        wdb = [es_.enter_context(nc.sbuf_tensor(pfx + "wd%d" % i,
                                                 [128, 24 * 128], BF16))
               for i in range(2)]
        pR = es_.enter_context(nc.psum_tensor(pfx + "pR", [128, 512], F32))
        pM = es_.enter_context(nc.psum_tensor(pfx + "pM", [128, 512], F32))
        pg0 = es_.enter_context(nc.psum_tensor(pfx + "pg0", [128, 512], F32))
        pg1 = es_.enter_context(nc.psum_tensor(pfx + "pg1", [128, 512], F32))
        pu0 = es_.enter_context(nc.psum_tensor(pfx + "pu0", [128, 512], F32))
        pu1 = es_.enter_context(nc.psum_tensor(pfx + "pu1", [128, 512], F32))
        pd0 = es_.enter_context(nc.psum_tensor(pfx + "pd0", [128, 512], F32))
        pd1 = es_.enter_context(nc.psum_tensor(pfx + "pd1", [128, 512], F32))
        blk = es_.enter_context(nc.Block())
        xt = [xa, xb]
        sg = [sg0, sg1]
        pg = [pg0, pg1]
        pu = [pu0, pu1]
        pd = [pd0, pd1]

        @blk.sync
        def _(sy):
            sy.wait_ge(WCH.h, WD_CACHED)
            for ww in sync_pre_wait:
                sy.wait_ge(ww[0].h, ww[1])
            for tb in range(n_tb):
                if tb >= 2:
                    sy.wait_ge(F_XS.h, b_xs + (tb - 1) * xs_per_tb)
                F_XL.inc(sy.dma_start(
                    out=xt[tb % 2][:].rearrange("p (c n) -> p c n", c=8),
                    in_=xsrc(tb)), 16)
                for dc in range(8):
                    k = tb * 8 + dc
                    if k >= 2:
                        sy.wait_ge(F_DM.h, b_dm + k - 1)
                    F_WD.inc(sy.dma_start(
                        out=wdb[k % 2][:, :],
                        in_=t["wd_bf"][dc * 128:(dc + 1) * 128, :]), 16)

        @blk.tensor
        def _(te):
            te.wait_ge(CST.h, CST_ALL)
            te.wait_ge(LN_MR.h, mr_count)
            for ww in extra_pe_wait:
                te.wait_ge(ww[0].h, ww[1])
            for tb in range(n_tb):
                nb = tb * 24
                te.wait_ge(F_XL.h, b_xl + (tb + 1) * 16)
                if tb >= 1:
                    te.wait_ge(F_AP.h, b_ap + tb * 8)
                te.matmul(pR[:, :], onesrow_sb[0:1, :],
                          rows16_sb[0:1, tb * 512:(tb + 1) * 512],
                          start=True, stop=True)
                F_BC.inc(te.matmul(pM[:, :], onesrow_sb[0:1, :],
                                   rows16_sb[0:1, 4096 + tb * 512:4096 + (tb + 1) * 512],
                                   start=True, stop=True), 1)
                for h in range(24):
                    hp = h % 2
                    if nb + h >= 2:
                        te.wait_ge(F_SI.h, b_si + nb + h - 1)
                        te.wait_ge(F_MU.h, b_mu + nb + h - 1)
                    for ic in range(8):
                        if h == 0:
                            te.wait_ge(F_AP.h, b_ap + tb * 8 + ic + 1)
                        te.matmul(pg[hp][:, :],
                                  wg_sb[:, ic * HID + h * 128:ic * HID + (h + 1) * 128],
                                  xn[:, ic * 512:(ic + 1) * 512],
                                  start=(ic == 0), stop=(ic == 7))
                    for ic in range(8):
                        mm = te.matmul(pu[hp][:, :],
                                       wu_sb[:, ic * HID + h * 128:ic * HID + (h + 1) * 128],
                                       xn[:, ic * 512:(ic + 1) * 512],
                                       start=(ic == 0), stop=(ic == 7))
                    F_GU.inc(mm, 1)
                te.wait_ge(F_MU.h, b_mu + nb + 24)
                for dc in range(8):
                    k = tb * 8 + dc
                    te.wait_ge(F_WD.h, b_wd + (k + 1) * 16)
                    if k >= 2:
                        te.wait_ge(F_XE.h, b_xe + k - 1)
                    for h in range(24):
                        mm = te.matmul(pd[k % 2][:, :],
                                       wdb[k % 2][:, h * 128:(h + 1) * 128],
                                       ha[:, h * 512:(h + 1) * 512],
                                       start=(h == 0), stop=(h == 23))
                    F_DM.inc(mm, 1)

        @blk.vector
        def _(ve):
            for tb in range(n_tb):
                nb = tb * 24
                ve.wait_ge(F_BC.h, b_bc + tb + 1)
                if tb >= 1:
                    ve.wait_ge(F_GU.h, b_gu + nb)
                for ic in range(8):
                    ve.tensor_mul(xsc[:, :],
                                  xt[tb % 2][:, ic * 512:(ic + 1) * 512], pR[:, :])
                    F_AP.inc(ve.tensor_sub(xn[:, ic * 512:(ic + 1) * 512],
                                           xsc[:, :], pM[:, :]), 1)
                for h in range(24):
                    ve.wait_ge(F_SI.h, b_si + nb + h + 1)
                    if tb >= 1 and h == 0:
                        ve.wait_ge(F_DM.h, b_dm + tb * 8)
                    F_MU.inc(ve.tensor_mul(ha[:, h * 512:(h + 1) * 512],
                                           sg[h % 2][:, :], pu[h % 2][:, :]), 1)
                for dc in range(8):
                    k = tb * 8 + dc
                    ve.wait_ge(F_DM.h, b_dm + k + 1)
                    xv = xt[tb % 2][:].rearrange("p (c n) -> p c n", c=8)[:, dc, :]
                    F_XE.inc(ve.scalar_tensor_tensor(
                        xv, pd[k % 2][:, :], FFN_SC, xv, ALU.mult, ALU.add), 1)

        @blk.scalar
        def _(s):
            for tb in range(n_tb):
                nb = tb * 24
                for h in range(24):
                    s.wait_ge(F_GU.h, b_gu + nb + h + 1)
                    if nb + h >= 2:
                        s.wait_ge(F_MU.h, b_mu + nb + h - 1)
                    F_SI.inc(s.activation(out=sg[h % 2][:, :], in_=pg[h % 2][:, :],
                                          func=AF.Silu, scale=DEQ), 1)

        @blk.gpsimd
        def _(g):
            if sink_switch:
                pid = g.partition_id()
                n0 = F_XS.n
                for cid in nc.Switch(engines=[g], index=[pid], n=8):
                    F_XS.n = n0
                    for tb in range(n_tb):
                        g.wait_ge(F_XE.h, b_xe + (tb + 1) * 8)
                        sink(g, cid, tb, xt[tb % 2])
            else:
                for tb in range(n_tb):
                    g.wait_ge(F_XE.h, b_xe + (tb + 1) * 8)
                    sink(g, None, tb, xt[tb % 2])
            g.wait_ge(F_XS.h, F_XS.n)

        es_.close()
    def x_cols(tb):
        return t["x"][:, tb * 512:(tb + 1) * 512].rearrange("(c p) n -> p c n", p=128)

    if stop_after == "b0":
        emit_dumps()
        est.close()
        return nc, t
    mr1 = ln_block("L1_", 8, xsrc=x_cols)
    if stop_after == "ln1":
        emit_dumps()
        est.close()
        return nc, t

    WGU = st("WGU")
    es_ = contextlib.ExitStack()
    wgT_sb = es_.enter_context(nc.sbuf_tensor("wgT_sb", [128, 8 * HID], BF16))
    wuT_sb = es_.enter_context(nc.sbuf_tensor("wuT_sb", [128, 8 * HID], BF16))
    with nc.Block() as blk:
        @blk.gpsimd
        def _(g):
            WGU.inc(g.dma_start(
                out=wgT_sb[:].rearrange("p (c n) -> p c n", c=8),
                in_=t["wgT"].rearrange("(c p) n -> p c n", p=128)), 16)
            WGU.inc(g.dma_start(
                out=wuT_sb[:].rearrange("p (c n) -> p c n", c=8),
                in_=t["wuT"].rearrange("(c p) n -> p c n", p=128)), 16)
            g.wait_ge(WGU.h, WGU.n)
            WGU.inc(g.dma_start(out=t["wg_bf"][:, :], in_=wgT_sb[:, :]), 16)
            WGU.inc(g.dma_start(out=t["wu_bf"][:, :], in_=wuT_sb[:, :]), 16)

    def ffn1_sink(g, cid, tb, buf):
        hh = cid % 2
        gb = hh * 8 + tb
        F_XS.inc(g.dma_start(
            out=t["xms"][:, gb * 4096:(gb + 1) * 4096],
            in_=buf[:, :]), 16)

    ffn_block("F1_", 8, x_cols, wgT_sb, wuT_sb, ffn1_sink, True, mr1,
              extra_pe_wait=[(WGU, 32)], xs_per_tb=16)
    es_.close()
    F1_XS_ALL = F_XS.n
    WGU_CACHED = WGU.n
    if stop_after == "ffn1":
        emit_dumps()
        est.close()
        return nc, t

    def ln2_loader(eng, cid, tb, dst, LD):
        gb = (cid % 2) * 8 + tb
        LD.inc(eng.dma_start(
            out=dst[:, :],
            in_=t["xms"][:, gb * 4096:(gb + 1) * 4096]), 16)

    mr2 = ln_block("L2_", 8, gp_loader=ln2_loader, pre_wait=[(F_XS, F1_XS_ALL)])
    if stop_after == "ln2":
        emit_dumps()
        est.close()
        return nc, t

    emit_kv(nc, t, st, VLr, NORM0, mr2,
            onesrow_sb, onescolb_sb, hselK_sb, rows16_sb)

    if stop_after == "kv":
        emit_dumps()
        est.close()
        return nc, t
    kvdone = [(st("KV_WR"), st("KV_WR").n), (st("KV_KLW"), st("KV_KLW").n),
              (st("KV_KNW"), st("KV_KNW").n), (st("KV_VLW"), st("KV_VLW").n),
              (st("F_XS"), F_XS.n)]
    emit_barrier(nc, t, st, nonce_sb, flag_sb, 0, kvdone, BDN)

    if stop_after == "bar1":
        emit_dumps()
        est.close()
        return nc, t
    mr3 = emit_q(nc, t, st, ln_block, NORM0,
                 onesrow_sb, onescolb_sb, hselQ_sb, rows16_sb, selhh_sb, BDN)

    if stop_after == "q":
        emit_dumps()
        est.close()
        return nc, t
    emit_attn(nc, t, st, VLr, i128_sb)

    if stop_after == "attn":
        emit_dumps()
        est.close()
        return nc, t
    emit_barrier(nc, t, st, nonce_sb, flag_sb, 1,
                 [(st("AT_OW"), st("AT_OW").n)], BDN)

    if stop_after == "bar2":
        emit_dumps()
        est.close()
        return nc, t
    emit_tail(nc, t, st, ln_block, ffn_block,
              onescol_sb, onescolb_sb, sel16_sb, BDN, WGU_CACHED)

    emit_dumps()

    est.close()
    return nc, t


def emit_kv(nc, t, st, VLr, NORM0, mr2,
            onesrow_sb, onescolb_sb, hselK_sb, rows16_sb):
    LN_MR = st("LN_MR")
    KV_W = st("KV_W"); KV_XL = st("KV_XL"); KV_CT = st("KV_CT")
    KV_BC = st("KV_BC"); KV_AP = st("KV_AP"); KV_KP = st("KV_KP")
    KV_KE = st("KV_KE"); KV_RO = st("KV_RO"); KV_KQ = st("KV_KQ")
    KV_NM = st("KV_NM"); KV_NE = st("KV_NE"); KV_VM = st("KV_VM")
    KV_VE = st("KV_VE"); KV_VO = st("KV_VO")
    KV_KLW = st("KV_KLW"); KV_KNW = st("KV_KNW"); KV_VLW = st("KV_VLW")
    KV_WR = st("KV_WR")
    es = contextlib.ExitStack()
    wkT_sb = es.enter_context(nc.sbuf_tensor("wkT_sb", [128, 8 * 1024], BF16))
    wvT_sb = es.enter_context(nc.sbuf_tensor("wvT_sb", [128, 8 * 1024], BF16))
    KVxa = es.enter_context(nc.sbuf_tensor("KVxa", [128, 8 * 512], F32))
    KVxb = es.enter_context(nc.sbuf_tensor("KVxb", [128, 8 * 512], F32))
    KVxn = es.enter_context(nc.sbuf_tensor("KVxn", [128, 8 * 512], BF16))
    KVxsc = es.enter_context(nc.sbuf_tensor("KVxsc", [128, 512], F32))
    KVkraw = es.enter_context(nc.sbuf_tensor("KVkraw", [128, 2 * 512], BF16))
    KVkrot = es.enter_context(nc.sbuf_tensor("KVkrot", [128, 2 * 512], BF16))
    KVu = es.enter_context(nc.sbuf_tensor("KVu", [128, 512], BF16))
    KVksq = es.enter_context(nc.sbuf_tensor("KVksq", [128, 4 * 1024], BF16))
    KVcs = es.enter_context(nc.sbuf_tensor("KVcs", [128, 2 * 2048], BF16))
    KVsn = es.enter_context(nc.sbuf_tensor("KVsn", [128, 2 * 2048], BF16))
    KVvs0 = es.enter_context(nc.sbuf_tensor("KVvs0", [128, 1040], BF16))
    KVvs1 = es.enter_context(nc.sbuf_tensor("KVvs1", [128, 1040], BF16))
    KVkn = es.enter_context(nc.sbuf_tensor("KVkn", [16, 512], BF16))
    KVpR = es.enter_context(nc.psum_tensor("KVpR", [128, 512], F32))
    KVpM = es.enter_context(nc.psum_tensor("KVpM", [128, 512], F32))
    KVpkA = es.enter_context(nc.psum_tensor("KVpkA", [128, 512], F32))
    KVpkB = es.enter_context(nc.psum_tensor("KVpkB", [128, 512], F32))
    KVpvA = es.enter_context(nc.psum_tensor("KVpvA", [128, 512], F32))
    KVpvB = es.enter_context(nc.psum_tensor("KVpvB", [128, 512], F32))
    KVpn = es.enter_context(nc.psum_tensor("KVpn", [16, 512], F32))
    blk = es.enter_context(nc.Block())
    KVxt = [KVxa, KVxb]
    KVvs = [KVvs0, KVvs1]

    @blk.sync
    def _(sy):
        for tb in range(DBG.get("kv_tbs", 8)):
            if tb >= 2:
                sy.wait_ge(KV_RO.h, (tb - 1) * 4)
            if DBG.get("kv_ct", True):
                KV_CT.inc(sy.dma_start(
                    out=KVcs[:, (tb % 2) * 2048:(tb % 2 + 1) * 2048
                             ].rearrange("p (c n) -> p c n", c=4),
                    in_=t["csK"][:, tb * 512:(tb + 1) * 512
                                 ].rearrange("(c p) n -> p c n", p=128)), 16)
                KV_CT.inc(sy.dma_start(
                    out=KVsn[:, (tb % 2) * 2048:(tb % 2 + 1) * 2048
                             ].rearrange("p (c n) -> p c n", c=4),
                    in_=t["snK"][:, tb * 512:(tb + 1) * 512
                                 ].rearrange("(c p) n -> p c n", p=128)), 16)
            else:
                KV_CT.inc(sy.dma_start(
                    out=KVcs[:, (tb % 2) * 2048:(tb % 2) * 2048 + 512],
                    in_=t["csK"][0:128, 0:512]), 16)
                KV_CT.inc(sy.dma_start(
                    out=KVsn[:, (tb % 2) * 2048:(tb % 2) * 2048 + 512],
                    in_=t["snK"][0:128, 0:512]), 16)

    @blk.gpsimd
    def _(g):
        KV_W.inc(g.dma_start(
            out=wkT_sb[:].rearrange("p (c n) -> p c n", c=8),
            in_=t["wkT"].rearrange("(c p) n -> p c n", p=128)), 16)
        KV_W.inc(g.dma_start(
            out=wvT_sb[:].rearrange("p (c n) -> p c n", c=8),
            in_=t["wvT"].rearrange("(c p) n -> p c n", p=128)), 16)
        pid = g.partition_id()
        base = dict(XL=KV_XL.n, KLW=KV_KLW.n, KNW=KV_KNW.n,
                    VLW=KV_VLW.n, WR=KV_WR.n)
        for cid in nc.Switch(engines=[g], index=[pid], n=8):
            KV_XL.n, KV_KLW.n = base["XL"], base["KLW"]
            KV_KNW.n, KV_VLW.n, KV_WR.n = base["KNW"], base["VLW"], base["WR"]
            hh = cid % 2
            col0 = hh * NTOK
            KLh = t["KL"][0:H * HR, :].rearrange("(s r) n -> s r n", r=HR)
            for k in range(5):
                KV_WR.inc(g.dma_start(
                    out=KLh[0:16, 64 + k, col0:col0 + NTOK],
                    in_=t["ones16"][:, 0:NTOK]), 16)
            for k in range(3):
                KV_WR.inc(g.dma_start(
                    out=KLh[0:16, 69 + k, col0:col0 + NTOK],
                    in_=t["a3K"][k * 16:(k + 1) * 16, :]), 16)
            for tb in range(DBG.get("kv_tbs", 8)):
                if tb >= 2:
                    g.wait_ge(KV_AP.h, (tb - 1) * 8)
                gbx = hh * 8 + tb
                KV_XL.inc(g.dma_start(
                    out=KVxt[tb % 2][:, :],
                    in_=t["xms"][:, gbx * 4096:(gbx + 1) * 4096]), 16)
                for ocp in range(4):
                    idx = tb * 4 + ocp
                    g.wait_ge(KV_RO.h, idx + 1)
                    for half in range(2):
                        for sub in range(2):
                            h = 2 * ocp + sub + 8 * half
                            sg_ = 8 * (h % 2) + h // 2
                            if DBG["kv_kl"]:
                                KV_KLW.inc(g.dma_start(
                                    out=t["KL"][sg_ * HR + 0:sg_ * HR + 64,
                                                col0 + tb * 512:col0 + (tb + 1) * 512],
                                    in_=KVkrot[sub * 64:(sub + 1) * 64,
                                               half * 512:(half + 1) * 512]), 16)
                            else:
                                KV_KLW.inc(g.dma_start(
                                    out=t["kvdbg"][sub * 64:(sub + 1) * 64, 0:512],
                                    in_=KVkrot[sub * 64:(sub + 1) * 64,
                                               half * 512:(half + 1) * 512]), 16)
                if tb >= 1:
                    g.wait_ge(KV_NE.h, tb)
                    KV_KNW.inc(g.dma_start(
                        out=t["KL"][NORM0:NORM0 + 16,
                                    col0 + (tb - 1) * 512:col0 + tb * 512],
                        in_=KVkn[:, :]), 16)
                for tc in range(4):
                    idx = tb * 4 + tc
                    g.wait_ge(KV_VE.h, idx + 1)
                    g.wait_ge(KV_VO.h, 2)
                    kc = hh * 32 + tb * 4 + tc
                    if DBG["kv_vl"]:
                        KV_VLW.inc(g.dma_start(
                            out=VLr[kc, :, :, :].rearrange("s k c -> k s c"),
                            in_=KVvs[tc % 2][:].rearrange("p (s c) -> p s c", s=16)), 16)
                    else:
                        KV_VLW.inc(g.dma_start(
                            out=t["kvdbg"][:, :], in_=KVvs[tc % 2][:, :]), 16)
            NTBg = DBG.get("kv_tbs", 8)
            if NTBg >= 1:
                g.wait_ge(KV_NE.h, NTBg)
                KV_KNW.inc(g.dma_start(
                    out=t["KL"][NORM0:NORM0 + 16,
                                col0 + (NTBg - 1) * 512:col0 + NTBg * 512],
                    in_=KVkn[:, :]), 16)

    @blk.tensor
    def _(te):
        def emit_norms(nb):
            te.wait_ge(KV_KQ.h, nb * 4 + 4)
            if nb >= 1:
                te.wait_ge(KV_NE.h, nb)
            mm = None
            for ocp in range(4):
                for half in range(2):
                    oc = ocp + 4 * half
                    mm = te.matmul(KVpn[:, :], hselK_sb[:, oc * 16:(oc + 1) * 16],
                                   KVksq[:, ocp * 1024 + half * 512:
                                         ocp * 1024 + (half + 1) * 512],
                                   start=(ocp == 0 and half == 0),
                                   stop=(ocp == 3 and half == 1))
            KV_NM.inc(mm, 1)

        te.wait_ge(KV_W.h, 32)
        te.wait_ge(LN_MR.h, mr2)
        NTB_KV = DBG.get("kv_tbs", 8)
        for tb in range(NTB_KV):
            te.wait_ge(KV_XL.h, (tb + 1) * 16)
            if tb >= 1:
                te.wait_ge(KV_AP.h, tb * 8)
            te.matmul(KVpR[:, :], onesrow_sb[0:1, :],
                      rows16_sb[0:1, tb * 512:(tb + 1) * 512], start=True, stop=True)
            KV_BC.inc(te.matmul(KVpM[:, :], onesrow_sb[0:1, :],
                                rows16_sb[0:1, 4096 + tb * 512:4096 + (tb + 1) * 512],
                                start=True, stop=True), 1)
            if tb >= 1:
                emit_norms(tb - 1)
            for ocp in range(4):
                idx = tb * 4 + ocp
                if idx >= 1:
                    te.wait_ge(KV_KE.h, idx)
                nic = 8 if DBG.get("kv_kproj", True) else 1
                for ic in range(nic):
                    if ocp == 0:
                        te.wait_ge(KV_AP.h, tb * 8 + ic + 1)
                    te.matmul(KVpkA[:, :],
                              wkT_sb[:, ic * 1024 + ocp * 128:ic * 1024 + (ocp + 1) * 128],
                              KVxn[:, ic * 512:(ic + 1) * 512],
                              start=(ic == 0), stop=(ic == nic - 1))
                for ic in range(nic):
                    mm = te.matmul(KVpkB[:, :],
                                   wkT_sb[:, ic * 1024 + (ocp + 4) * 128:ic * 1024 + (ocp + 5) * 128],
                                   KVxn[:, ic * 512:(ic + 1) * 512],
                                   start=(ic == 0), stop=(ic == nic - 1))
                KV_KP.inc(mm, 1)
            for tc in range(4):
                idx = tb * 4 + tc
                if idx >= 1:
                    te.wait_ge(KV_VE.h, idx)
                niv = 8 if DBG.get("kv_vproj", True) else 1
                for ic in range(niv):
                    te.matmul(KVpvA[:, :],
                              KVxn[:, ic * 512 + tc * 128:ic * 512 + (tc + 1) * 128],
                              wvT_sb[:, ic * 1024:ic * 1024 + 512],
                              start=(ic == 0), stop=(ic == niv - 1))
                for ic in range(niv):
                    mm = te.matmul(KVpvB[:, :],
                                   KVxn[:, ic * 512 + tc * 128:ic * 512 + (tc + 1) * 128],
                                   wvT_sb[:, ic * 1024 + 512:ic * 1024 + 1024],
                                   start=(ic == 0), stop=(ic == niv - 1))
                KV_VM.inc(mm, 1)
        if NTB_KV >= 1:
            emit_norms(NTB_KV - 1)

    @blk.vector
    def _(ve):
        KV_VO.inc(ve.memset(
            KVvs0[:].rearrange("p (s c) -> p s c", s=16)[:, :, 64:65], 1.0), 1)
        KV_VO.inc(ve.memset(
            KVvs1[:].rearrange("p (s c) -> p s c", s=16)[:, :, 64:65], 1.0), 1)
        for tb in range(DBG.get("kv_tbs", 8)):
            ve.wait_ge(KV_BC.h, tb + 1)
            if tb >= 1:
                ve.wait_ge(KV_KP.h, tb * 4)
                ve.wait_ge(KV_VM.h, tb * 4)
            for ic in range(8):
                ve.tensor_mul(KVxsc[:, :],
                              KVxt[tb % 2][:, ic * 512:(ic + 1) * 512], KVpR[:, :])
                KV_AP.inc(ve.tensor_sub(KVxn[:, ic * 512:(ic + 1) * 512],
                                        KVxsc[:, :], KVpM[:, :]), 1)
            ve.wait_ge(KV_CT.h, (tb + 1) * 32)
            for ocp in range(4):
                idx = tb * 4 + ocp
                ve.wait_ge(KV_KE.h, idx + 1)
                if idx >= 1:
                    ve.wait_ge(KV_KLW.h, idx * 64)
                    ve.wait_ge(KV_KQ.h, idx)
                cs = KVcs[:, (tb % 2) * 2048 + ocp * 512:(tb % 2) * 2048 + (ocp + 1) * 512]
                sn = KVsn[:, (tb % 2) * 2048 + ocp * 512:(tb % 2) * 2048 + (ocp + 1) * 512]
                if DBG.get("kv_rope", True):
                    ve.tensor_mul(KVu[:, :], KVkraw[:, 512:1024], sn)
                    ve.tensor_mul(KVkrot[:, 0:512], KVkraw[:, 0:512], cs)
                    ve.tensor_sub(KVkrot[:, 0:512], KVkrot[:, 0:512], KVu[:, :])
                    ve.tensor_mul(KVu[:, :], KVkraw[:, 0:512], sn)
                    ve.tensor_mul(KVkrot[:, 512:1024], KVkraw[:, 512:1024], cs)
                    KV_RO.inc(ve.tensor_add(KVkrot[:, 512:1024], KVkrot[:, 512:1024],
                                            KVu[:, :]), 1)
                else:
                    KV_RO.inc(ve.tensor_copy(KVkrot[:, :], KVkraw[:, :]), 1)

    @blk.scalar
    def _(s):
        for tb in range(DBG.get("kv_tbs", 8)):
            for ocp in range(4):
                idx = tb * 4 + ocp
                s.wait_ge(KV_KP.h, idx + 1)
                if idx >= 1:
                    s.wait_ge(KV_RO.h, idx)
                s.activation(out=KVkraw[:, 0:512], in_=KVpkA[:, :], func=AF.Copy)
                KV_KE.inc(s.activation(out=KVkraw[:, 512:1024], in_=KVpkB[:, :],
                                       func=AF.Copy), 1)
                s.wait_ge(KV_RO.h, idx + 1)
                if idx >= 4:
                    s.wait_ge(KV_NM.h, tb)
                if DBG.get("kv_sq", True):
                    KV_KQ.inc(s.activation(out=KVksq[:, ocp * 1024:(ocp + 1) * 1024],
                                           in_=KVkrot[:, :], func=AF.Square), 1)
                else:
                    KV_KQ.inc(s.activation(out=KVksq[:, ocp * 1024:(ocp + 1) * 1024],
                                           in_=KVkrot[:, :], func=AF.Copy), 1)
            if tb >= 1:
                s.wait_ge(KV_NM.h, tb)
                if tb >= 2:
                    s.wait_ge(KV_KNW.h, (tb - 1) * 16)
                KV_NE.inc(s.activation(out=KVkn[:, :], in_=KVpn[:, :],
                                       func=AF.Copy), 1)
            for tc in range(4):
                idx = tb * 4 + tc
                s.wait_ge(KV_VM.h, idx + 1)
                if idx >= 2:
                    s.wait_ge(KV_VLW.h, (idx - 1) * 16)
                if DBG.get("kv_vevac", True):
                    for gi, (pv, lohi, s0) in enumerate([
                            (KVpvA, 0, 0), (KVpvA, 1, 8),
                            (KVpvB, 0, 4), (KVpvB, 1, 12)]):
                        cp = s.activation(
                            out=KVvs[tc % 2][:, s0 * 65:(s0 + 4) * 65
                                             ].rearrange("p (s c) -> p s c", s=4)[:, :, 0:64],
                            in_=pv[:, :].rearrange("p (g e) -> p g e", g=4)
                            [:, :, lohi * 64:(lohi + 1) * 64],
                            func=AF.Copy)
                    KV_VE.inc(cp, 1)
                else:
                    s.activation(out=KVvs[tc % 2][:, 0:512], in_=KVpvA[:, :],
                                 func=AF.Copy)
                    KV_VE.inc(s.activation(out=KVvs[tc % 2][:, 512:1024],
                                           in_=KVpvB[:, :], func=AF.Copy), 1)
        NTB_KV2 = DBG.get("kv_tbs", 8)
        if NTB_KV2 >= 1:
            s.wait_ge(KV_NM.h, NTB_KV2)
            if NTB_KV2 >= 2:
                s.wait_ge(KV_KNW.h, (NTB_KV2 - 1) * 16)
            KV_NE.inc(s.activation(out=KVkn[:, :], in_=KVpn[:, :],
                                   func=AF.Copy), 1)


    es.close()
def emit_barrier(nc, t, st, nonce_sb, flag_sb, widx, wait_list, BDN):
    FW = st("BFW%d" % widx)
    PLm = st("BPL%d" % widx)
    b_pl = 0
    with nc.Block() as blk:
        def body(g):
            for s_, cnt in wait_list:
                g.wait_ge(s_.h, cnt)
            pid = g.partition_id()
            nfw = FW.n
            for cid in nc.Switch(engines=[g], index=[pid], n=8):
                FW.n = nfw
                hh = cid % 2
                FW.inc(g.dma_start(out=t["flags"][hh:hh + 1, widx:widx + 1],
                                   in_=nonce_sb[0:1, widx:widx + 1]), 16)
            g.wait_ge(FW.h, FW.n)
            with g.Fori(0, 1000000) as it:
                for cid in nc.Switch(engines=[g], index=[pid], n=8):
                    other = 1 - (cid % 2)
                    g.dma_start(out=flag_sb[0:1, 8:9],
                                in_=t["flags"][other:other + 1, widx:widx + 1]
                                ).then_inc(PLm.h, 16)
                g.wait_ge(PLm.h, (it + 1) * 16 + b_pl)
                val = nc.values_load(flag_sb[0:1, 8:9], engines=[g.engine],
                                     min_val=0, max_val=(1 << 31) - 1,
                                     skip_runtime_bounds_check=True)
                tgt = nc.values_load(nonce_sb[0:1, widx:widx + 1], engines=[g.engine],
                                     min_val=0, max_val=(1 << 31) - 1,
                                     skip_runtime_bounds_check=True)
                with g.If(val == tgt):
                    g.br("bar%d_done" % widx)
            nc.switch_bb("bar%d_done" % widx)
            PLm.n = b_pl
            BDN.inc(g.dma_start(out=flag_sb[0:1, 12:16], in_=nonce_sb[0:1, 0:4]), 16)
            g.wait_ge(BDN.h, BDN.n)

        if widx == 0:
            @blk.sync
            def _(sy):
                body(sy)
        else:
            @blk.gpsimd
            def _(g):
                body(g)


def emit_q(nc, t, st, ln_block, NORM0,
           onesrow_sb, onescolb_sb, hselQ_sb, rows16_sb, selhh_sb, BDN):
    LN_MR = st("LN_MR")

    XIW = st("XIW")
    xiw_base = XIW.n

    def isl_loader(g, cid, tb, dst, LD):
        gb = 2 * tb + (cid % 4) // 2
        LD.inc(g.dma_start(
            out=dst[:, :],
            in_=t["xms"][:, gb * 4096:(gb + 1) * 4096]), 16)

    mr3 = ln_block("Li_", 8, gp_loader=isl_loader, pre_wait=[(BDN, 16)])

    Q_W = st("Q_W"); Q_XL = st("Q_XL"); Q_CT = st("Q_CT")
    Q_BC = st("Q_BC"); Q_AP = st("Q_AP"); Q_QP = st("Q_QP")
    Q_QE = st("Q_QE"); Q_RO = st("Q_RO"); Q_QQ = st("Q_QQ")
    Q_NM = st("Q_NM"); Q_NE = st("Q_NE")
    Q_QW = st("Q_QW")
    Q_MKL = st("Q_MKL")
    Q_MS = st("Q_MS")
    Q_SEL = st("Q_SEL")
    Q_CQ = st("Q_CQ")
    Q_RM = st("Q_RM")
    Q_QL = st("Q_QL")
    es_ = contextlib.ExitStack()
    Qwq = es_.enter_context(nc.sbuf_tensor("Qwq", [128, 8 * 512], BF16))
    Qxa = es_.enter_context(nc.sbuf_tensor("Qxa", [128, 8 * 512], F32))
    Qxb = es_.enter_context(nc.sbuf_tensor("Qxb", [128, 8 * 512], F32))
    Qxn = es_.enter_context(nc.sbuf_tensor("Qxn", [128, 8 * 512], BF16))
    Qxsc = es_.enter_context(nc.sbuf_tensor("Qxsc", [128, 512], F32))
    Qraw = es_.enter_context(nc.sbuf_tensor("Qraw", [128, 2 * 512], BF16))
    Qrot = es_.enter_context(nc.sbuf_tensor("Qrot", [128, 2 * 512], BF16))
    Qu = es_.enter_context(nc.sbuf_tensor("Qu", [128, 512], BF16))
    Qsq = es_.enter_context(nc.sbuf_tensor("Qsq", [128, 2 * 1024], BF16))
    Qcs = es_.enter_context(nc.sbuf_tensor("Qcs", [128, 2 * 1024], BF16))
    Qsn = es_.enter_context(nc.sbuf_tensor("Qsn", [128, 2 * 1024], BF16))
    Qn2 = es_.enter_context(nc.sbuf_tensor("Qn2", [8, 4096], F32))
    Qtmp = es_.enter_context(nc.sbuf_tensor("Qtmp", [8, 4096], F32))
    Qmkb = es_.enter_context(nc.sbuf_tensor("Qmkb", [16, 8192], BF16))
    Qm2f = es_.enter_context(nc.sbuf_tensor("Qm2f", [16, 1], F32))
    Qm2b = es_.enter_context(nc.sbuf_tensor("Qm2b", [16, 1], BF16))
    Qmk = es_.enter_context(nc.sbuf_tensor("Qmk", [8, 1], F32))
    Qcqh = es_.enter_context(nc.sbuf_tensor("Qcqh", [8, 4096], BF16))
    Qcql = es_.enter_context(nc.sbuf_tensor("Qcql", [8, 4096], BF16))
    QpR = es_.enter_context(nc.psum_tensor("QpR", [128, 512], F32))
    QpM = es_.enter_context(nc.psum_tensor("QpM", [128, 512], F32))
    QpA = es_.enter_context(nc.psum_tensor("QpA", [128, 512], F32))
    QpB = es_.enter_context(nc.psum_tensor("QpB", [128, 512], F32))
    Qpn = es_.enter_context(nc.psum_tensor("Qpn", [8, 512], F32))
    Qpndbg = es_.enter_context(nc.sbuf_tensor("Qpndbg", [8, 1], F32))
    blk = es_.enter_context(nc.Block())
    Qxt = [Qxa, Qxb]

    XIW_T = st("XIW")
    XIW_ALL = XIW_T.n

    @blk.sync
    def _(sy):
        Q_MKL.inc(sy.dma_start(
            out=Qmkb[:, :], in_=t["KL"][NORM0:NORM0 + 16, :]), 16)
        pidq = sy.partition_id()
        _qb = dict(XL=Q_XL.n, CT=Q_CT.n, QW=Q_QW.n)
        for cidq in nc.Switch(engines=[sy], index=[pidq], n=8):
            Q_XL.n, Q_CT.n, Q_QW.n = _qb["XL"], _qb["CT"], _qb["QW"]
            s_islq = (cidq % 4) // 2
            for tb in range(8):
                if tb >= 2:
                    sy.wait_ge(Q_AP.h, (tb - 1) * 8)
                gbq = 2 * tb + s_islq
                Q_XL.inc(sy.dma_start(
                    out=Qxt[tb % 2][:, :],
                    in_=t["xms"][:, gbq * 4096:(gbq + 1) * 4096]), 16)
                if tb >= 2:
                    sy.wait_ge(Q_RO.h, (tb - 1) * 2)
                Q_CT.inc(sy.dma_start(
                    out=Qcs[:, (tb % 2) * 1024:(tb % 2 + 1) * 1024
                            ].rearrange("p (c n) -> p c n", c=2),
                    in_=t["csQ"][:, tb * 512:(tb + 1) * 512
                                 ].rearrange("(c p) n -> p c n", p=128)), 16)
                Q_CT.inc(sy.dma_start(
                    out=Qsn[:, (tb % 2) * 1024:(tb % 2 + 1) * 1024
                            ].rearrange("p (c n) -> p c n", c=2),
                    in_=t["snQ"][:, tb * 512:(tb + 1) * 512
                                 ].rearrange("(c p) n -> p c n", p=128)), 16)
                for pp in range(2):
                    idx = tb * 2 + pp
                    sy.wait_ge(Q_RO.h, idx + 1)
                    for half in range(2):
                        for sub in range(2):
                            sl = (2 * pp if half == 0 else 4 + 2 * pp) + sub
                            Q_QW.inc(sy.dma_start(
                                out=t["QL"][sl * HR:sl * HR + 64,
                                            tb * 512:(tb + 1) * 512],
                                in_=Qrot[sub * 64:(sub + 1) * 64,
                                         half * 512:(half + 1) * 512]), 16)
        sy.wait_ge(Q_CQ.h, 1)
        QLr = t["QL"].rearrange("(sl r) n -> sl r n", r=HR)
        Q_QL.inc(sy.dma_start(out=QLr[:, 64, :], in_=Qcqh[:, :]), 16)
        Q_QL.inc(sy.dma_start(out=QLr[:, 65, :], in_=Qcql[:, :]), 16)
        for k in range(3):
            Q_QL.inc(sy.dma_start(out=QLr[:, 66 + k, :],
                                  in_=t["qa3"][k * 8:(k + 1) * 8, :]), 16)
        for k in range(3):
            Q_QL.inc(sy.dma_start(out=QLr[:, 69 + k, :],
                                  in_=t["ones16"][0:8, 0:NISL]), 16)
        if DBG.get("qdbg"):
            QD = st("QD")
            sy.wait_ge(st("Q_MS").h, 4)
            sy.wait_ge(Q_CQ.h, 1)
            with nc.allow_non_contiguous_dma(reason="debug dump"):
                for r0, src in [(16, Qm2f[:, 0:1]), (32, Qmk[:, 0:1]),
                                (40, Qn2[:, :]), (48, Qtmp[:, :])]:
                    h_, w_ = src.shape
                    QD.inc(sy.dma_start(out=t["qdbg"][r0:r0 + h_, 0:w_], in_=src), 16)
            with nc.allow_non_contiguous_dma(reason="debug dump"):
                QD.inc(sy.dma_start(out=t["qdbg"][56:56 + 8, 0:1],
                                    in_=Qpndbg[:, 0:1]), 16)
            with nc.allow_non_contiguous_dma(reason="debug dump"):
                for r0, src in [(0, Qmkb[:, :]), (16, Qcqh[:, :]), (24, Qcql[:, :]),
                                (32, Qm2b[:, 0:1])]:
                    h_, w_ = src.shape
                    QD.inc(sy.dma_start(out=t["qdbh"][r0:r0 + h_, 0:w_], in_=src), 16)
            sy.wait_ge(QD.h, QD.n)

    @blk.gpsimd
    def _(g):
        g.wait_ge(BDN.h, 16)
        Q_W.inc(g.dma_start(
            out=Qwq[:].rearrange("p (c n) -> p c n", c=8),
            in_=t["wqTm"].rearrange("(c p) n -> p c n", p=128)), 16)

    @blk.tensor
    def _(te):
        def emit_qnorms(nb):
            te.wait_ge(Q_QQ.h, nb * 2 + 2)
            if nb >= 1:
                te.wait_ge(Q_NE.h, nb)
            mm = None
            for pp in range(2):
                for half in range(2):
                    oc = pp + 2 * half
                    mm = te.matmul(Qpn[:, :], hselQ_sb[:, oc * 8:(oc + 1) * 8],
                                   Qsq[:, pp * 1024 + half * 512:
                                       pp * 1024 + (half + 1) * 512],
                                   start=(pp == 0 and half == 0),
                                   stop=(pp == 1 and half == 1))
            Q_NM.inc(mm, 1)

        te.wait_ge(Q_W.h, 16)
        te.wait_ge(LN_MR.h, mr3)
        for tb in range(8):
            te.wait_ge(Q_XL.h, (tb + 1) * 16)
            if tb >= 1:
                te.wait_ge(Q_AP.h, tb * 8)
            te.matmul(QpR[:, :], onesrow_sb[0:1, :],
                      rows16_sb[0:1, tb * 512:(tb + 1) * 512], start=True, stop=True)
            Q_BC.inc(te.matmul(QpM[:, :], onesrow_sb[0:1, :],
                               rows16_sb[0:1, 4096 + tb * 512:4096 + (tb + 1) * 512],
                               start=True, stop=True), 1)
            if tb >= 1:
                emit_qnorms(tb - 1)
            for pp in range(2):
                idx = tb * 2 + pp
                if idx >= 1:
                    te.wait_ge(Q_QE.h, idx)
                for ic in range(8):
                    if pp == 0:
                        te.wait_ge(Q_AP.h, tb * 8 + ic + 1)
                    te.matmul(QpA[:, :],
                              Qwq[:, ic * 512 + pp * 128:ic * 512 + (pp + 1) * 128],
                              Qxn[:, ic * 512:(ic + 1) * 512],
                              start=(ic == 0), stop=(ic == 7))
                for ic in range(8):
                    mm = te.matmul(QpB[:, :],
                                   Qwq[:, ic * 512 + (pp + 2) * 128:ic * 512 + (pp + 3) * 128],
                                   Qxn[:, ic * 512:(ic + 1) * 512],
                                   start=(ic == 0), stop=(ic == 7))
                Q_QP.inc(mm, 1)
        emit_qnorms(7)
        te.wait_ge(Q_NE.h, 8)
        te.wait_ge(Q_MS.h, 1)
        Q_SEL.inc(te.matmul(Qpn[0:8, 0:1], selhh_sb[:, :], Qm2b[:, 0:1],
                            start=True, stop=True), 1)

    @blk.vector
    def _(ve):
        for tb in range(8):
            ve.wait_ge(Q_BC.h, tb + 1)
            if tb >= 1:
                ve.wait_ge(Q_QP.h, tb * 2)
            for ic in range(8):
                ve.tensor_mul(Qxsc[:, :],
                              Qxt[tb % 2][:, ic * 512:(ic + 1) * 512], QpR[:, :])
                Q_AP.inc(ve.tensor_sub(Qxn[:, ic * 512:(ic + 1) * 512],
                                       Qxsc[:, :], QpM[:, :]), 1)
            for pp in range(2):
                idx = tb * 2 + pp
                ve.wait_ge(Q_QE.h, idx + 1)
                ve.wait_ge(Q_CT.h, (tb + 1) * 32)
                if idx >= 1:
                    ve.wait_ge(Q_QW.h, idx * 64)
                    ve.wait_ge(Q_QQ.h, idx)
                cs = Qcs[:, (tb % 2) * 1024 + pp * 512:(tb % 2) * 1024 + (pp + 1) * 512]
                sn = Qsn[:, (tb % 2) * 1024 + pp * 512:(tb % 2) * 1024 + (pp + 1) * 512]
                ve.tensor_mul(Qu[:, :], Qraw[:, 512:1024], sn)
                ve.tensor_mul(Qrot[:, 0:512], Qraw[:, 0:512], cs)
                ve.tensor_sub(Qrot[:, 0:512], Qrot[:, 0:512], Qu[:, :])
                ve.tensor_mul(Qu[:, :], Qraw[:, 0:512], sn)
                ve.tensor_mul(Qrot[:, 512:1024], Qraw[:, 512:1024], cs)
                Q_RO.inc(ve.tensor_add(Qrot[:, 512:1024], Qrot[:, 512:1024],
                                       Qu[:, :]), 1)
        ve.wait_ge(Q_MKL.h, 16)
        Q_RM.inc(ve.reduce_max(Qm2f[:, 0:1], Qmkb[:, :],
                               axis=mybir.AxisListType.X), 1)
        ve.wait_ge(Q_MS.h, 2)
        ve.wait_ge(Q_NE.h, 8)
        ve.wait_ge(Q_MS.h, 3)
        ve.tensor_scalar(Qcqh[:, :] if False else Qtmp[:, :], Qtmp[:, :],
                         Qmk[:, 0:1], -1.0, ALU.mult, ALU.mult)
        Q_MS.inc(ve.tensor_copy(Qcqh[:, :], Qtmp[:, :]), 1)
        if DBG.get("qdbg"):
            Q_MS.inc(ve.tensor_copy(Qpndbg[:, 0:1], Qpn[0:8, 0:1]), 1)
        Q_CQ.inc(ve.tensor_sub(Qcql[:, :], Qtmp[:, :], Qcqh[:, :]), 1)

    @blk.scalar
    def _(s):
        for tb in range(8):
            for pp in range(2):
                idx = tb * 2 + pp
                s.wait_ge(Q_QP.h, idx + 1)
                if idx >= 1:
                    s.wait_ge(Q_RO.h, idx)
                s.activation(out=Qraw[:, 0:512], in_=QpA[:, :], func=AF.Copy)
                Q_QE.inc(s.activation(out=Qraw[:, 512:1024], in_=QpB[:, :],
                                      func=AF.Copy), 1)
                s.wait_ge(Q_RO.h, idx + 1)
                if idx >= 2:
                    s.wait_ge(Q_NM.h, tb)
                Q_QQ.inc(s.activation(out=Qsq[:, pp * 1024:(pp + 1) * 1024],
                                      in_=Qrot[:, :], func=AF.Square), 1)
            if tb >= 1:
                s.wait_ge(Q_NM.h, tb)
                Q_NE.inc(s.activation(out=Qn2[:, (tb - 1) * 512:tb * 512],
                                      in_=Qpn[:, :], func=AF.Copy), 1)
        s.wait_ge(Q_NM.h, 8)
        Q_NE.inc(s.activation(out=Qn2[:, 7 * 512:8 * 512],
                              in_=Qpn[:, :], func=AF.Copy), 1)
        s.wait_ge(Q_RM.h, 1)
        Q_MS.inc(s.activation(out=Qm2b[:, 0:1], in_=Qm2f[:, 0:1],
                              func=AF.Copy), 1)
        s.wait_ge(Q_SEL.h, 1)
        s.activation(out=Qmk[:, 0:1], in_=Qpn[0:8, 0:1], func=AF.Ln)
        Q_MS.inc(s.activation(out=Qmk[:, 0:1], in_=Qmk[:, 0:1],
                              func=AF.Exp, scale=0.5), 1)
        s.wait_ge(Q_NE.h, 8)
        s.activation(out=Qtmp[:, :], in_=Qn2[:, :], func=AF.Ln)
        Q_MS.inc(s.activation(out=Qtmp[:, :], in_=Qtmp[:, :],
                              func=AF.Exp, scale=0.5), 1)
    es_.close()
    return mr3


def emit_attn(nc, t, st, VLr, i128_sb):
    CST = st("CST")
    BDN = st("BDN")
    Q_QL = st("Q_QL"); Q_QW = st("Q_QW")
    AT_MP = st("AT_MP")
    AT_QS = st("AT_QS")
    AT_KB = st("AT_KB")
    AT_VB = st("AT_VB")
    AT_SC = st("AT_SC")
    AT_EX = st("AT_EX")
    AT_AV = st("AT_AV")
    AT_OE = st("AT_OE")
    AT_SU = st("AT_SU")
    AT_OW = st("AT_OW")

    chains = []
    gidx = 0
    for j in range(8):
        for sl in range(8):
            lo, hi = chain_quads(j, sl)
            kts = list(range(lo * 4, hi * 4))
            groups = []
            pos = 0
            while pos < len(kts):
                gsz = 4 if gidx % 2 == 0 else 3
                gsz = min(gsz, len(kts) - pos)
                groups.append((gidx, kts[pos:pos + gsz]))
                gidx += 1
                pos += gsz
            chunks = list(range(lo, hi, 2))
            chains.append(dict(j=j, sl=sl, lo=lo, hi=hi, groups=groups,
                               chunks=chunks))
    NG = gidx
    chunk_ord = {}
    ordn = 0
    for c, ch in enumerate(chains):
        for q0 in ch["chunks"]:
            chunk_ord[(c, q0)] = ordn
            ordn += 1
    NCHUNK = ordn

    def chunk_of_kt(ch, kt):
        q0 = ch["lo"] + ((kt // 4 - ch["lo"]) // 2) * 2
        return q0

    last_group_using_chunk = [0] * NCHUNK
    last_av_group_of_chunk = [0] * NCHUNK
    for c, ch in enumerate(chains):
        for gi, kts in ch["groups"]:
            for kt in kts:
                o = chunk_ord[(c, chunk_of_kt(ch, kt))]
                last_group_using_chunk[o] = max(last_group_using_chunk[o], gi)
                last_av_group_of_chunk[o] = max(last_av_group_of_chunk[o], gi)
    last_group_of_chain = [ch["groups"][-1][0] for ch in chains]

    es_ = contextlib.ExitStack()
    ATmp = es_.enter_context(nc.sbuf_tensor("ATmp", [128, 8 * 512], BF16))
    ATq0 = es_.enter_context(nc.sbuf_tensor("ATq0", [72, 512], BF16))
    ATq1 = es_.enter_context(nc.sbuf_tensor("ATq1", [72, 512], BF16))
    ATk0 = es_.enter_context(nc.sbuf_tensor("ATk0", [72, 1024], BF16))
    ATk1 = es_.enter_context(nc.sbuf_tensor("ATk1", [72, 1024], BF16))
    ATk2 = es_.enter_context(nc.sbuf_tensor("ATk2", [72, 1024], BF16))
    ATv0 = es_.enter_context(nc.sbuf_tensor("ATv0", [128, 8 * 65], BF16))
    ATv1 = es_.enter_context(nc.sbuf_tensor("ATv1", [128, 8 * 65], BF16))
    ATv2 = es_.enter_context(nc.sbuf_tensor("ATv2", [128, 8 * 65], BF16))
    ATp0 = es_.enter_context(nc.sbuf_tensor("ATp0", [128, 4 * 512], BF16))
    ATp1 = es_.enter_context(nc.sbuf_tensor("ATp1", [128, 3 * 512], BF16))
    ATo0 = es_.enter_context(nc.sbuf_tensor("ATo0", [64, 512], BF16))
    ATo1 = es_.enter_context(nc.sbuf_tensor("ATo1", [64, 512], BF16))
    ATs0 = es_.enter_context(nc.sbuf_tensor("ATs0", [1, 1024], BF16))
    ATs1 = es_.enter_context(nc.sbuf_tensor("ATs1", [1, 1024], BF16))
    ATsqA = es_.enter_context(nc.psum_tensor("ATsqA", [128, 4 * 512], F32))
    ATsqB = es_.enter_context(nc.psum_tensor("ATsqB", [128, 3 * 512], F32))
    ATav = es_.enter_context(nc.psum_tensor("ATav", [65, 512], F32))
    blk = es_.enter_context(nc.Block())
    ATq = [ATq0, ATq1]
    ATk = [ATk0, ATk1, ATk2]
    ATv = [ATv0, ATv1, ATv2]
    ATp = [ATp0, ATp1]
    ATsq = [ATsqA, ATsqB]
    ATo = [ATo0, ATo1]
    ATs = [ATs0, ATs1]

    @blk.sync
    def _(sy):
        AT_MP.inc(sy.dma_start(out=ATmp[:, 0:8 * 512], in_=t["maskP"][:]), 16)
        for c, ch in enumerate(chains):
            if c >= 2:
                sy.wait_ge(AT_SC.h, last_group_of_chain[c - 2] + 1)
            sy.wait_ge(Q_QL.h, 128)
            sy.wait_ge(Q_QW.h, 1024)
            AT_QS.inc(sy.dma_start(
                out=ATq[c % 2][:, :],
                in_=t["QL"][ch["sl"] * HR:(ch["sl"] + 1) * HR,
                            ch["j"] * 512:(ch["j"] + 1) * 512]), 16)

    @blk.gpsimd
    def _(g):
        g.wait_ge(BDN.h, 16)
        pid = g.partition_id()
        nkb, nvb, now = AT_KB.n, AT_VB.n, AT_OW.n
        for cid in nc.Switch(engines=[g], index=[pid], n=8):
            AT_KB.n, AT_VB.n, AT_OW.n = nkb, nvb, now
            hh = cid % 2
            for c, ch in enumerate(chains):
                sg = 8 * hh + ch["sl"]
                for q0 in ch["chunks"]:
                    o = chunk_ord[(c, q0)]
                    if o >= 3:
                        g.wait_ge(AT_SC.h, last_group_using_chunk[o - 3] + 1)
                        g.wait_ge(AT_AV.h, last_av_group_of_chunk[o - 3] + 1)
                    kw = min(1024, S - q0 * 512)
                    nblk = kw // 128
                    AT_KB.inc(g.dma_start(
                        out=ATk[o % 3][:, 0:kw],
                        in_=t["KL"][sg * HR:(sg + 1) * HR,
                                    q0 * 512:q0 * 512 + kw]), 16)
                    AT_VB.inc(g.dma_start(
                        out=ATv[o % 3][:, 0:nblk * 65
                                       ].rearrange("p (a c) -> p a c", a=nblk),
                        in_=VLr[q0 * 4:q0 * 4 + nblk, sg, :, :
                                ].rearrange("a k c -> k a c")), 16)
                if c >= 1:
                    emit_ol_writes(g, t, chains[c - 1], hh, c - 1,
                                   AT_OE, AT_SU, AT_OW, ATo, ATs)
            emit_ol_writes(g, t, chains[-1], hh, len(chains) - 1,
                           AT_OE, AT_SU, AT_OW, ATo, ATs)

    @blk.tensor
    def _(te):
        te.wait_ge(CST.h, 112)
        te.wait_ge(AT_MP.h, 16)
        te.wait_ge(AT_QS.h, 16)
        te.wait_ge(AT_KB.h, 16)
        te.wait_ge(AT_VB.h, 16)
        plan_g = []
        for c, ch in enumerate(chains):
            for k2, (gi, kts) in enumerate(ch["groups"]):
                plan_g.append((c, ch, gi, kts, k2 == 0,
                               gi == ch["groups"][-1][0]))

        def emit_av(item):
            c, ch, gi, kts, is_first, is_last = item
            te.wait_ge(AT_EX.h, gi + 1)
            if is_first:
                te.wait_ge(AT_OE.h, c)
                te.wait_ge(AT_SU.h, c)
            gslot = gi % 2
            for u, kt in enumerate(kts):
                o = chunk_ord[(c, chunk_of_kt(ch, kt))]
                te.wait_ge(AT_VB.h, (o + 1) * 16)
                klo = (kt - chunk_of_kt(ch, kt) * 4)
                mm = te.matmul(ATav[:, :],
                               ATv[o % 3][:, klo * 65:(klo + 1) * 65],
                               ATp[gslot][:, u * 512:(u + 1) * 512],
                               start=(is_first and u == 0),
                               stop=(is_last and u == len(kts) - 1))
            AT_AV.inc(mm, 1)

        pend = None
        for item in plan_g:
            c, ch, gi, kts, is_first, is_last = item
            if is_first:
                te.wait_ge(AT_QS.h, (c + 1) * 16)
            gslot = gi % 2
            if gi >= 2:
                te.wait_ge(AT_EX.h, gi - 1)
            for u, kt in enumerate(kts):
                o = chunk_ord[(c, chunk_of_kt(ch, kt))]
                te.wait_ge(AT_KB.h, (o + 1) * 16)
                ktg = kt
                masked = 8 * ch["j"] <= ktg < 8 * ch["j"] + 8
                mm = te.matmul(ATsq[gslot][:, u * 512:(u + 1) * 512],
                               ATk[o % 3][:, (kt - chunk_of_kt(ch, kt) * 4)
                                          * 128:(kt - chunk_of_kt(ch, kt) * 4 + 1) * 128],
                               ATq[c % 2][:, :],
                               start=True, stop=(not masked))
                if masked:
                    mm = te.matmul(ATsq[gslot][:, u * 512:(u + 1) * 512],
                                   i128_sb[:, :],
                                   ATmp[:, (ktg - 8 * ch["j"]) * 512:
                                        (ktg - 8 * ch["j"] + 1) * 512],
                                   start=False, stop=True)
            AT_SC.inc(mm, 1)
            if pend is not None:
                emit_av(pend)
            pend = item
        emit_av(pend)

    @blk.scalar
    def _(s):
        for c, ch in enumerate(chains):
            for gi, kts in ch["groups"]:
                gslot = gi % 2
                s.wait_ge(AT_SC.h, gi + 1)
                if gi >= 2:
                    s.wait_ge(AT_AV.h, gi - 1)
                AT_EX.inc(s.activation(
                    out=ATp[gslot][:, 0:len(kts) * 512],
                    in_=ATsq[gslot][:, 0:len(kts) * 512],
                    func=AF.Exp, scale=0.125), 1)

    @blk.vector
    def _(ve):
        for c, ch in enumerate(chains):
            ve.wait_ge(AT_AV.h, last_group_of_chain[c] + 1)
            if c >= 2:
                ve.wait_ge(AT_OW.h, (c - 1) * 48)
            AT_OE.inc(ve.tensor_copy(ATo[c % 2][:, :], ATav[0:64, :]), 1)
            ve.tensor_copy(ATs[c % 2][0:1, 0:512], ATav[64:65, :])
            AT_SU.inc(ve.tensor_sub(ATs[c % 2][0:1, 512:1024], ATav[64:65, :],
                                    ATs[c % 2][0:1, 0:512]), 1)


    es_.close()
def emit_ol_writes(g, t, ch, hh, c, AT_OE, AT_SU, AT_OW, ATo, ATs):
    sg = 8 * hh + ch["sl"]
    g.wait_ge(AT_OE.h, c + 1)
    g.wait_ge(AT_SU.h, c + 1)
    col = ch["j"] * 512
    AT_OW.inc(g.dma_start(out=t["OL"][sg * 64:(sg + 1) * 64, col:col + 512],
                          in_=ATo[c % 2][:, :]), 16)
    AT_OW.inc(g.dma_start(out=t["OL"][1024 + sg:1024 + sg + 1, col:col + 512],
                          in_=ATs[c % 2][0:1, 0:512]), 16)
    AT_OW.inc(g.dma_start(out=t["OL"][1040 + sg:1040 + sg + 1, col:col + 512],
                          in_=ATs[c % 2][0:1, 512:1024]), 16)


def emit_tail(nc, t, st, ln_block, ffn_block,
              onescol_sb, onescolb_sb, sel16_sb, BDN, WGU_CACHED):
    WGU = st("WGU")
    TL_OL = st("TL_OL")
    TL_WO = st("TL_WO")
    TL_RC = st("TL_RC")
    TL_XM = st("TL_XM")
    TL_RB = st("TL_RB")
    TL_ON = st("TL_ON")
    TL_PO = st("TL_PO")
    TL_XE = st("TL_XE")
    TL_X3 = st("TL_X3")
    TL_YS = st("TL_YS")

    es_ = contextlib.ExitStack()
    TLos = es_.enter_context(nc.sbuf_tensor("TLos", [128, 8 * 2048], BF16))
    TLsh = es_.enter_context(nc.sbuf_tensor("TLsh", [16, 2048], BF16))
    TLsl = es_.enter_context(nc.sbuf_tensor("TLsl", [16, 2048], BF16))
    TLrr = es_.enter_context(nc.sbuf_tensor("TLrr", [16, 2048], F32))
    TLrh = es_.enter_context(nc.sbuf_tensor("TLrh", [16, 2048], BF16))
    TLrl = es_.enter_context(nc.sbuf_tensor("TLrl", [16, 2048], BF16))
    TLwo = es_.enter_context(nc.sbuf_tensor("TLwo", [128, 8 * 1024], BF16))
    TLob = es_.enter_context(nc.sbuf_tensor("TLob", [128, 8], F32))
    TLxm0 = es_.enter_context(nc.sbuf_tensor("TLxm0", [128, 8 * 512], F32))
    TLxm1 = es_.enter_context(nc.sbuf_tensor("TLxm1", [128, 8 * 512], F32))
    TLon = es_.enter_context(nc.sbuf_tensor("TLon", [128, 8 * 512], BF16))
    TLpb0 = es_.enter_context(nc.psum_tensor("TLpb0", [128, 512], F32))
    TLpb1 = es_.enter_context(nc.psum_tensor("TLpb1", [128, 512], F32))
    TLpo0 = es_.enter_context(nc.psum_tensor("TLpo0", [128, 512], F32))
    TLpo1 = es_.enter_context(nc.psum_tensor("TLpo1", [128, 512], F32))
    blk = es_.enter_context(nc.Block())
    TLxm = [TLxm0, TLxm1]
    TLpb = [TLpb0, TLpb1]
    TLpo = [TLpo0, TLpo1]

    @blk.gpsimd
    def _(g):
        g.wait_ge(BDN.h, 32)
        TL_WO.inc(g.dma_start(
            out=TLwo[:].rearrange("p (c n) -> p c n", c=8),
            in_=t["woTs"].rearrange("(c p) n -> p c n", p=128)), 16)
        TL_WO.inc(g.dma_start(out=TLob[:, :], in_=t["outb"][:, :]), 16)
        pid = g.partition_id()
        nol, nxm = TL_OL.n, TL_XM.n
        for cid in nc.Switch(engines=[g], index=[pid], n=8):
            TL_OL.n, TL_XM.n = nol, nxm
            s_isl = (cid % 4) // 2
            hh = cid % 2
            c4 = s_isl + 2 * hh
            for m in range(4):
                icol = 512 * (2 * m + hh)
                TL_OL.inc(g.dma_start(
                    out=TLos[:].rearrange("p (c n) -> p c n", c=8)
                    [:, :, m * 512:(m + 1) * 512],
                    in_=t["OL"][0:1024, icol:icol + 512
                                ].rearrange("(c p) n -> p c n", p=128)), 16)
                TL_OL.inc(g.dma_start(
                    out=TLsh[:, m * 512:(m + 1) * 512],
                    in_=t["OL"][1024:1040, icol:icol + 512]), 16)
                TL_OL.inc(g.dma_start(
                    out=TLsl[:, m * 512:(m + 1) * 512],
                    in_=t["OL"][1040:1056, icol:icol + 512]), 16)
            for m in range(4):
                gcol = 512 * (4 * m + c4)
                if m >= 2:
                    g.wait_ge(TL_X3.h, (m - 1) * 16)
                TL_XM.inc(g.dma_start(
                    out=TLxm[m % 2][:, :],
                    in_=t["xms"][:, (4 * m + c4) * 4096:(4 * m + c4 + 1) * 4096]), 16)

    @blk.vector
    def _(ve):
        ve.wait_ge(TL_OL.h, 192)
        ve.tensor_add(TLrr[:, :], TLsh[:, :], TLsl[:, :])
        ve.reciprocal(TLrr[:, :], TLrr[:, :])
        ve.tensor_copy(TLrh[:, :], TLrr[:, :])
        TL_RC.inc(ve.tensor_sub(TLrl[:, :], TLrr[:, :], TLrh[:, :]), 1)
        for m in range(4):
            if m >= 1:
                ve.wait_ge(TL_PO.h, m * 8)
            for ic in range(8):
                i2 = m * 8 + ic
                ve.wait_ge(TL_RB.h, i2 + 1)
                TL_ON.inc(ve.tensor_mul(
                    TLon[:, ic * 512:(ic + 1) * 512],
                    TLos[:].rearrange("p (c n) -> p c n", c=8)
                    [:, ic, m * 512:(m + 1) * 512],
                    TLpb[i2 % 2][:, :]), 1)
            for oc in range(8):
                k = m * 8 + oc
                ve.wait_ge(TL_PO.h, k + 1)
                xv = TLxm[m % 2][:].rearrange("p (c n) -> p c n", c=8)[:, oc, :]
                TL_XE.inc(ve.scalar_tensor_tensor(
                    xv, TLpo[k % 2][:, :], TLob[:, oc:oc + 1], xv,
                    ALU.add, ALU.add), 1)

    @blk.tensor
    def _(te):
        te.wait_ge(TL_WO.h, 32)
        te.wait_ge(TL_RC.h, 1)
        for m in range(4):
            for ic in range(8):
                i2 = m * 8 + ic
                if i2 >= 2:
                    te.wait_ge(TL_ON.h, i2 - 1)
                te.matmul(TLpb[i2 % 2][:, :], sel16_sb[:, ic * 128:(ic + 1) * 128],
                          TLrh[:, m * 512:(m + 1) * 512], start=True, stop=False)
                TL_RB.inc(te.matmul(TLpb[i2 % 2][:, :],
                                    sel16_sb[:, ic * 128:(ic + 1) * 128],
                                    TLrl[:, m * 512:(m + 1) * 512],
                                    start=False, stop=True), 1)
            te.wait_ge(TL_XM.h, (m + 1) * 16)
            for oc in range(8):
                k = m * 8 + oc
                if k >= 2:
                    te.wait_ge(TL_XE.h, k - 1)
                for ic in range(8):
                    te.wait_ge(TL_ON.h, m * 8 + ic + 1)
                    mm = te.matmul(TLpo[k % 2][:, :],
                                   TLwo[:, ic * 1024 + oc * 128:ic * 1024 + (oc + 1) * 128],
                                   TLon[:, ic * 512:(ic + 1) * 512],
                                   start=(ic == 0), stop=(ic == 7))
                TL_PO.inc(mm, 1)

    @blk.sync
    def _(sy):
        for m in range(4):
            sy.wait_ge(TL_XE.h, (m + 1) * 8)
            TL_X3.inc(sy.dma_start(
                out=t["x3d"][:, m * 512:(m + 1) * 512
                             ].rearrange("(c p) n -> p c n", p=128),
                in_=TLxm[m % 2][:].rearrange("p (c n) -> p c n", c=8)), 16)

    es_.close()
    def x3_cols(tb):
        return t["x3d"][:, tb * 512:(tb + 1) * 512].rearrange("(c p) n -> p c n", p=128)

    mr4 = ln_block("L3_", 4, xsrc=x3_cols, pre_wait=[(st("TL_X3"), 64)])

    W2 = st("W2")
    es_ = contextlib.ExitStack()
    wg2_sb = es_.enter_context(nc.sbuf_tensor("wg2_sb", [128, 8 * HID], BF16))
    wu2_sb = es_.enter_context(nc.sbuf_tensor("wu2_sb", [128, 8 * HID], BF16))
    with nc.Block() as blk:
        @blk.sync
        def _(sy):
            sy.wait_ge(WGU.h, WGU_CACHED)
            W2.inc(sy.dma_start(out=wg2_sb[:, :], in_=t["wg_bf"][:, :]), 16)
            W2.inc(sy.dma_start(out=wu2_sb[:, :], in_=t["wu_bf"][:, :]), 16)

    F_XS2 = st("F_XS")

    def y_sink(g, cid, tb, buf):
        F_XS2.inc(g.dma_start(
            out=t["y"][:, tb * 512:(tb + 1) * 512
                       ].rearrange("(c p) n -> p c n", p=128),
            in_=buf[:].rearrange("p (c n) -> p c n", c=8)), 16)

    ffn_block("F2_", 4, x3_cols, wg2_sb, wu2_sb, y_sink, False, mr4,
              extra_pe_wait=[(W2, 32)],
              sync_pre_wait=[(st("TL_X3"), 64)])


    es_.close()
def _triple_split(v):
    a = v.astype(BF).astype(np.float64)
    b = (v - a).astype(BF).astype(np.float64)
    c = (v - a - b).astype(BF)
    return a.astype(BF), b.astype(BF), c

def _rope_tables(pos, nfreq=512):
    freqs = 1.0 / 10000.0 ** (np.arange(nfreq, dtype=np.float64) / 512.0)
    ang = pos[None, :] * freqs[:, None]
    return np.cos(ang).astype(BF), np.sin(ang).astype(BF)


def _host_inputs(L0_attn_x, wq, wk, wv, wo, ob, wg, wu, wd, nonce):
    xs = np.asarray(L0_attn_x)
    wgT = np.ascontiguousarray(np.asarray(wg).T)
    wuT = np.ascontiguousarray(np.asarray(wu).T)
    wdT = np.ascontiguousarray(np.asarray(wd).T)
    wkT = np.ascontiguousarray(np.asarray(wk).T)
    wvT = np.ascontiguousarray(np.asarray(wv).T)
    wqT = np.asarray(wq).T
    woT = np.asarray(wo).T
    ob = np.ascontiguousarray(
        np.asarray(ob).reshape(8, 128).T).astype(np.float32)

    perm = np.zeros(D, dtype=np.int64)
    for sg in range(16):
        h = head_of_slot(sg)
        perm[sg * 64:(sg + 1) * 64] = np.arange(h * 64, (h + 1) * 64)
    woTs = np.ascontiguousarray(woT[perm, :]).astype(np.float32)

    ones16 = np.ones((16, S), BF)
    i128 = np.eye(128, dtype=BF)
    onescol = np.ones((1, 128), np.float32)
    onescolf = np.ones((128, 1), np.float32)
    onescolb = np.ones((128, 1), BF)

    hselK = np.zeros((128, 8 * 16), BF)
    for oc in range(8):
        for r in range(128):
            h = (128 * oc + r) // 64
            hselK[r, oc * 16 + slot_of_head(h)] = 1.0
    sel16 = np.zeros((16, 8 * 128), BF)
    for ic in range(8):
        for p in range(128):
            sg = 2 * ic + p // 64
            sel16[sg, ic * 128 + p] = 1.0

    fconst = np.zeros((1, 16), np.float32)
    fconst[0, 0] = EPS
    nonce_arr = np.zeros((1, 16), np.uint32)
    nonce_arr[0, 0] = np.uint32(nonce + 1)
    nonce_arr[0, 1] = np.uint32(nonce + 2)

    ins = []
    for core in range(8):
        b = core // 4
        s_isl = (core % 4) // 2
        hh = core % 2
        c4 = s_isl + 2 * hh
        x_half = np.ascontiguousarray(xs[b, hh * NTOK:(hh + 1) * NTOK, :].T
                                      ).astype(np.float32)
        kpos = np.arange(hh * NTOK, (hh + 1) * NTOK, dtype=np.float64)
        ipos = (512.0 * (2 * (np.arange(NISL) // 512) + s_isl)
                + (np.arange(NISL) % 512)).astype(np.float64)

        csK, snK = _rope_tables(kpos)
        myh1 = [hh, hh + 2, hh + 4, hh + 6]
        cols = []
        for h in myh1:
            cols.extend(range(64 * h, 64 * h + 64))
        for h in myh1:
            cols.extend(range(64 * (h + 8), 64 * (h + 8) + 64))
        wqTm = np.ascontiguousarray(wqT[:, cols]).astype(np.float32)
        freqs_idx = np.array(cols[:256])
        fr = 1.0 / 10000.0 ** (freqs_idx.astype(np.float64) / 512.0)
        angQ = ipos[None, :] * fr[:, None]
        csQ = np.cos(angQ).astype(BF)
        snQ = np.sin(angQ).astype(BF)

        a3K = np.zeros((48, NTOK), BF)
        for sg in range(16):
            h = head_of_slot(sg)
            v = 8.0 * SLOPES[h] * kpos
            aa, bb, cc = _triple_split(v)
            a3K[sg] = aa
            a3K[16 + sg] = bb
            a3K[32 + sg] = cc
        qa3 = np.zeros((24, NISL), BF)
        for sl in range(8):
            h = 2 * sl + hh
            v = -8.0 * SLOPES[h] * ipos
            aa, bb, cc = _triple_split(v)
            qa3[sl] = aa
            qa3[8 + sl] = bb
            qa3[16 + sl] = cc

        hselQ = np.zeros((128, 32), BF)
        for oc in range(4):
            for r in range(128):
                if oc < 2:
                    sl = 2 * oc + r // 64
                else:
                    sl = 4 + 2 * (oc - 2) + r // 64
                hselQ[r, oc * 8 + sl] = 1.0

        selhh = np.zeros((16, 8), BF)
        for c in range(8):
            selhh[8 * hh + c, c] = 1.0

        maskP = np.zeros((128, 8 * 512), BF)
        pq = np.arange(512)
        for slot in range(8):
            pp = np.arange(128)[:, None]
            m = (128 * slot + pp) > (512 * s_isl + pq[None, :])
            maskP[:, slot * 512:(slot + 1) * 512] = np.where(m, NEGBIG, 0.0).astype(BF)

        ins.append(dict(
            x=x_half, wgT=wgT, wuT=wuT, wdT=wdT, wkT=wkT, wvT=wvT,
            wqTm=wqTm, woTs=woTs, outb=ob,
            csK=np.ascontiguousarray(csK), snK=np.ascontiguousarray(snK),
            csQ=np.ascontiguousarray(csQ), snQ=np.ascontiguousarray(snQ),
            a3K=a3K, qa3=qa3, ones16=ones16, maskP=maskP,
            hselK=hselK, hselQ=hselQ, sel16=sel16, selhh=selhh, i128=i128,
            onescol=onescol, onescolf=onescolf, onescolb=onescolb,
            nonce=nonce_arr, fconst=fconst))
    return ins


_NC_CACHE = {}


def kernel(L0_attn_x, L0_attn_wq, L0_attn_wk, L0_attn_wv, L0_attn_out_w,
           L0_attn_out_b, L0_ffn_w_gate, L0_ffn_w_up, L0_ffn_w_down):
    import time
    nonce = (int(time.time_ns() // 100) % (1 << 28)) * 4
    ins = _host_inputs(L0_attn_x, L0_attn_wq, L0_attn_wk, L0_attn_wv,
                       L0_attn_out_w, L0_attn_out_b,
                       L0_ffn_w_gate, L0_ffn_w_up, L0_ffn_w_down, nonce)
    if "nc" not in _NC_CACHE:
        _NC_CACHE["nc"] = build_program()
    nc, _ = _NC_CACHE["nc"]
    res = run_bass_kernel_spmd(nc, ins, core_ids=list(range(8)))
    out = np.zeros((B, S, D), np.float32)
    for core in range(8):
        b = core // 4
        s_isl = (core % 4) // 2
        hh = core % 2
        c4 = s_isl + 2 * hh
        y = res.results[core]["y"]
        for m in range(4):
            g0 = 512 * (4 * m + c4)
            out[b, g0:g0 + 512, :] = y[:, m * 512:(m + 1) * 512].T
    return out
